# version 20
# baseline (speedup 1.0000x reference)
import numpy as np
import ml_dtypes

import concourse.bass as bass
import concourse.mybir as mybir
from concourse.bass_utils import run_bass_kernel_spmd

F32 = mybir.dt.float32
BF16 = mybir.dt.bfloat16
AF = mybir.ActivationFunctionType
ALU = mybir.AluOpType
AX = mybir.AxisListType

D = 1024
NH = 16
E = 8
TC = 128
TILE = 512
NTC = TILE // TC
EPS = 1e-5
NEG_BIG = -1e30
SLOT = 6144
NSLOT = 4
N_CORES = 8
STOP_AFTER = None
_PHASES = ["ln", "gate", "qkv", "att1c", "att2c", "att3c", "att", "ush", "z0",
           "uz0", "uz1", "uz2", "uz3"]
_SUBATT = ["attt1", "attt2", "atts1", "attt3", "atts2", "attt4"]
def _sub_ok(name):
    if STOP_AFTER in (None, ) or STOP_AFTER not in _SUBATT:
        return True
    return _SUBATT.index(name) <= _SUBATT.index(STOP_AFTER)


class Sem:
    def __init__(self, name):
        self.name = name
        self.n = 0
        self.h = None
        self.marks = {}


class _Stub:
    def then_inc(self, h, k):
        return self


class Eng:
    def __init__(self, ctx, name, real):
        self.ctx = ctx
        self.name = name
        self.real = real

    def _rng(self, ap):
        try:
            off = ap.offset if isinstance(ap.offset, int) else 0
            return (ap.tensor.name, off, off + max(ap.free_size(), 1))
        except Exception:
            return None

    def _guard(self, outs, ins, allow_hazard):
        if self.name != "vector":
            return
        lo = self.ctx.v_last_out
        if lo is not None and not allow_hazard:
            for ap in ins:
                r = self._rng(ap)
                if r and r[0] == lo[0] and r[1] < lo[2] and lo[1] < r[2]:
                    raise RuntimeError(
                        f"DVE RAW hazard: read {r} right after write {lo}")
        self.ctx.v_last_out = self._rng(outs[0]) if outs else None

    def op(self, method, *args, outs=(), ins=(), sem=None, semk=1,
           allow_hazard=False, **kw):
        self._guard(list(outs), list(ins), allow_hazard)
        if sem is not None:
            sem.n += semk
        self.ctx.tape[self.name].append(("op", method, args, kw, sem, semk))
        return _Stub()

    def mark(self, sem, key):
        if True:
            assert key not in sem.marks, f"dup mark {sem.name}:{key}"
            sem.marks[key] = sem.n
        self.ctx.marked.add((sem.name, key))

    def wait(self, sem, key=None, val=None):
        if self.name == "vector":
            self.ctx.v_last_out = None
        if val is None:
            assert (sem.name, key) in self.ctx.marked, \
                f"wait on unrecorded mark {sem.name}:{key}"
            val = sem.marks[key]
        assert val <= sem.n, f"future wait {sem.name} {val}>{sem.n}"
        self.ctx.tape[self.name].append(("wait", sem, val))


class Ctx:

    def __init__(self, nc, block, dry):
        self.nc = nc
        self.block = block
        self.dry = dry
        self.marked = set()
        self.v_last_out = None
        self.tape = {k: [] for k in
                     ("tensor", "vector", "scalar", "gpsimd", "sync")}

    def section(self, engine, body):
        body(Eng(self, engine, None))


def build_program(n_seq, T):
    assert T % TILE == 0
    NT = n_seq * T
    n_tiles = NT // TILE
    tps = T // TILE

    nc = bass.Bass("TRN2", target_bir_lowering=False, debug=False)

    x_d = nc.dram_tensor("x", [NT, D], F32, kind="ExternalInput").ap()
    out_d = nc.dram_tensor("out", [NT, D], F32, kind="ExternalOutput").ap()
    wqk_d = nc.dram_tensor("w_qkT", [D, 2048], F32, kind="ExternalInput").ap()
    wv_d = nc.dram_tensor("w_vT", [D, D], F32, kind="ExternalInput").ap()
    wo_d = nc.dram_tensor("w_oT", [D, D], F32, kind="ExternalInput").ap()
    wg_d = nc.dram_tensor("w_gT", [D, E], F32, kind="ExternalInput").ap()
    we1_d = nc.dram_tensor("w_e1T", [E, D, 1536], F32, kind="ExternalInput").ap()
    we2_d = nc.dram_tensor("w_e2T", [E, 768, D], F32, kind="ExternalInput").ap()
    ws1_d = nc.dram_tensor("w_s1T", [D, 3072], F32, kind="ExternalInput").ap()
    ws2_d = nc.dram_tensor("w_s2T", [1536, D], F32, kind="ExternalInput").ap()
    ident_d = nc.dram_tensor("c_ident", [128, 128], BF16, kind="ExternalInput").ap()
    mask_d = nc.dram_tensor("c_mask4", [128, 128], BF16, kind="ExternalInput").ap()
    sel_d = nc.dram_tensor("c_sel", [8, 1024], BF16, kind="ExternalInput").ap()

    wqk_s = nc.dram_tensor("s_qk", [D, 2048], BF16, kind="Internal").ap()
    wv_s = nc.dram_tensor("s_v", [D, D], BF16, kind="Internal").ap()
    we1_s = nc.dram_tensor("s_e1", [E, D, 1536], BF16, kind="Internal").ap()
    we2_s = nc.dram_tensor("s_e2", [E, 768, D], BF16, kind="Internal").ap()
    ws1_s = nc.dram_tensor("s_s1", [D, 3072], BF16, kind="Internal").ap()

    ctxs = []

    def sbuf(name, shape, dt):
        cm = nc.sbuf_tensor(name, shape, dt)
        t = cm.__enter__()
        ctxs.append(cm)
        return t

    def psumt(name, shape, dt):
        cm = nc.psum_tensor(name, shape, dt)
        t = cm.__enter__()
        ctxs.append(cm)
        return t

    ident = sbuf("ident", [128, 128], BF16)[:, :]
    mask4 = sbuf("mask4", [128, 128], BF16)[:, :]
    sel = sbuf("sel", [8, 1024], BF16)[:, :]
    wg_sb = sbuf("wg_sb", [128, 8 * E], BF16)[:, :]
    wo_sb = sbuf("wo_sb", [128, 8 * 1024], BF16)[:, :]
    ws2_sb = sbuf("ws2_sb", [128, 12 * 1024], BF16)[:, :]
    ring = sbuf("ring", [128, NSLOT * SLOT], BF16)[:, :]
    outs2 = [sbuf(f"out_sb{i}", [128, NTC * D], F32)[:, :] for i in range(2)]
    xn_f = sbuf("xn_f", [128, 8 * TILE], BF16)[:, :]
    qk_ar = sbuf("qk_ar", [128, 8192], BF16)[:, :]
    u_ar = sbuf("u_ar", [128, 9216], BF16)[:, :]
    y_fT = sbuf("y_fT", [128, 8 * TILE], BF16)[:, :]
    v_sb = [sbuf(f"v_sb{i}", [128, NH * 65], BF16)[:, :] for i in range(4)]
    sz_m = sbuf("sz_m", [128, 520], F32)[:, :]
    sz_bf = sbuf("sz_bf", [128, 520], BF16)[:, :]
    k_tm = sbuf("k_tm", [128, 1024], BF16)[:, :]
    xn2_tmb = k_tm
    y_tm = sbuf("y_tm", [128, 1024], BF16)[:, :]
    xn_tmb = y_tm
    w8 = sbuf("w8", [128, NTC * E], F32)[:, :]
    w8b = sbuf("w8b", [128, NTC * E], BF16)[:, :]
    w8t = sbuf("w8t", [8, TILE], BF16)[:, :]
    a_m = [sbuf(f"a_m{i}", [128, 512], BF16)[:, :] for i in range(4)]
    elu1 = sbuf("elu1", [128, 1024], F32)[:, :]
    sg_sb = [sbuf(f"sg{i}", [128, 512], BF16)[:, :] for i in range(2)]
    xn_tm = sbuf("xn_tm", [128, D], F32)[:, :]
    x_tc = sbuf("x_tc", [128, D], F32)[:, :]
    bn12 = sbuf("bn12", [128, 12], F32)[:, :]
    mv = sbuf("mv", [128, 4 * NTC], F32)[:, :]
    rr = sbuf("rr", [128, 2 * NTC], F32)[:, :]
    nm = sbuf("nm", [128, 2 * NTC], F32)[:, :]
    neg = sbuf("neg", [128, 2], F32)[:, :]
    lnt = sbuf("lnt", [128, 4], F32)[:, :]
    gt = sbuf("gt", [128, 8 * NTC], F32)[:, :]
    gtp = sbuf("gtp", [128, 8 * NTC], F32)[:, :]
    gsc = sbuf("gsc", [128, 24], F32)[:, :]
    gl_sb = sbuf("gl_sb", [128, NTC * E], F32)[:, :]
    d4 = sbuf("d4", [128, 8], F32)[:, :]
    spc = sbuf("spc", [128, 8], F32)[:, :]
    cst = sbuf("cst", [128, 4], F32)[:, :]
    r4 = sbuf("r4", [128, 8], F32)[:, :]

    q_f = qk_ar[:, 0:4096]
    k_f = qk_ar[:, 4096:8192]
    us_sb = qk_ar[:, 0:6144]
    wb_sb2 = [qk_ar[:, 6144:6656], qk_ar[:, 6656:7168]]
    xn2_f = u_ar[:, 0:4096]
    u_slot = [u_ar[:, 3072 * i:3072 * (i + 1)] for i in range(3)]

    pf = [psumt(f"pf{i}", [128, 512], F32)[:, :] for i in range(6)]
    pt = [psumt(f"pt{i}", [128, 1024], BF16)[:, :] for i in range(2)]

    SEMS = []

    def S(name):
        s = Sem(name)
        SEMS.append(s)
        return s
    s_cast = S("s_cast")
    s_cast1 = S("s_cast1")
    s_res = S("s_res")
    s_cres = S("s_cres")
    s_x = S("s_x")
    s_xe = S("s_xe")
    s_out = S("s_out")
    s_ring = [S(f"s_ring{i}") for i in range(NSLOT)]
    s_ring0 = [S(f"s_ring0_{i}") for i in range(NSLOT)]
    s_castw = [S(f"s_castw{i}") for i in range(4)]
    t_m = S("t_m")
    v_m = S("v_m")
    c_m = S("c_m")

    def ring_items(t):
        items = [
            (f"qhA{t}", wqk_s[:, 0:512].rearrange("(kc p) m -> p kc m", p=128), 8, 512, None),
            (f"qhB{t}", wqk_s[:, 512:1024].rearrange("(kc p) m -> p kc m", p=128), 8, 512, None),
            (f"khA{t}", wqk_s[:, 1024:1536].rearrange("(kc p) m -> p kc m", p=128), 8, 512, None),
            (f"khB{t}", wqk_s[:, 1536:2048].rearrange("(kc p) m -> p kc m", p=128), 8, 512, None),
            (f"wvA{t}", wv_s[:, 0:512].rearrange("(kc p) m -> p kc m", p=128), 8, 512, None),
            (f"wvB{t}", wv_s[:, 512:1024].rearrange("(kc p) m -> p kc m", p=128), 8, 512, None),
        ]
        for qq in range(4):
            items.append((f"s1_{t}_{qq}",
                          ws1_s[:, 768 * qq:768 * (qq + 1)].rearrange("(kc p) m -> p kc m", p=128),
                          8, 768, None))
        for w in range(4):
            for ee in range(2):
                e = 2 * w + ee
                src32 = (we1_d[e, :, 0:768].rearrange("(kc p) m -> p kc m", p=128),
                         ("we1", e)) if t == 0 else None
                items.append((f"e1a_{t}_{e}",
                              we1_s[e, :, 0:768].rearrange("(kc p) m -> p kc m", p=128), 8, 768, src32))
                src32 = (we1_d[e, :, 768:1536].rearrange("(kc p) m -> p kc m", p=128),
                         None) if t == 0 else None
                items.append((f"e1b_{t}_{e}",
                              we1_s[e, :, 768:1536].rearrange("(kc p) m -> p kc m", p=128), 8, 768, src32))
            for ee in range(2):
                e = 2 * w + ee
                src32 = (we2_d[e].rearrange("(kc p) m -> p kc m", p=128),
                         ("we2", e)) if t == 0 else None
                items.append((f"e2_{t}_{e}",
                              we2_s[e].rearrange("(kc p) m -> p kc m", p=128), 6, 1024, src32))
        return items

    FLUSH_PLAN = []

    def emit(ctx):
        if ctx.dry:
            FLUSH_PLAN.clear()
        flush_i = [0]
        ring_q = []
        slot_of = {}
        slot_prev = [None] * NSLOT
        loads_in_slot = [0] * NSLOT
        loads0_in_slot = [0] * NSLOT
        wait_cnt = {}
        cast_done = set()
        cast_marks = {}
        next_slot = [0]
        for t in range(n_tiles):
            ring_q.extend(ring_items(t))

        def ring_flush():
            todo = []
            if not ctx.dry:
                quota = FLUSH_PLAN[flush_i[0]]
                flush_i[0] += 1
            while ring_q:
                if ctx.dry:
                    key, src, kc, m, src32 = ring_q[0]
                    slot = next_slot[0] % NSLOT
                    prev = slot_prev[slot]
                    if prev is not None and \
                            ("t_m", f"use:{prev}") not in ctx.marked:
                        break
                else:
                    if len(todo) >= quota:
                        break
                    key, src, kc, m, src32 = ring_q[0]
                    slot = next_slot[0] % NSLOT
                    prev = slot_prev[slot]
                ring_q.pop(0)
                next_slot[0] += 1
                slot_of[key] = slot
                slot_prev[slot] = key
                if src32 is not None:
                    loads0_in_slot[slot] += 1
                    wait_cnt[key] = (s_ring0[slot], 16 * loads0_in_slot[slot])
                else:
                    loads_in_slot[slot] += 1
                    wait_cnt[key] = (s_ring[slot], 16 * loads_in_slot[slot])
                todo.append((key, src, kc, m, slot, prev, src32))
            if ctx.dry:
                FLUSH_PLAN.append(len(todo))
            if not todo:
                return
            sw = [x for x in todo if x[6] is not None]
            hw = [x for x in todo if x[6] is None]

            if sw:
                def gbody(g):
                    for key, src, kc, m, slot, prev, src32 in sw:
                        if prev is not None:
                            g.wait(t_m, f"use:{prev}")
                        g.op("dma_start",
                             out=ring[:, SLOT * slot:SLOT * slot + kc * m]
                             .rearrange("p (kc m) -> p kc m", kc=kc),
                             in_=src32[0], sem=s_ring0[slot], semk=16)
                        if src32[1] is not None:
                            kind, e = src32[1]
                            w = e // 2
                            if kind == "we1":
                                g.op("dma_start", out=we1_s[e], in_=we1_d[e],
                                     sem=s_castw[w], semk=16)
                            else:
                                g.op("dma_start", out=we2_s[e], in_=we2_d[e],
                                     sem=s_castw[w], semk=16)
                            cast_done.add((kind, e))
                            if sum(1 for k2, e2 in cast_done
                                   if e2 // 2 == w) == 4:
                                g.mark(s_castw[w], "full")
                ctx.section("gpsimd", gbody)

            if hw:
                def body(s):
                    for key, src, kc, m, slot, prev, src32 in hw:
                        if key.startswith(("qh", "kh")):
                            s.wait(s_cast1, "qk")
                        elif key.startswith("wv"):
                            s.wait(s_cast1, "v")
                        elif key.startswith("s1"):
                            s.wait(s_cast1, "front")
                        else:
                            e = int(key.split("_")[-1])
                            s.wait(s_castw[e // 2], "full")
                        if prev is not None:
                            s.wait(t_m, f"use:{prev}")
                        s.op("dma_start",
                             out=ring[:, SLOT * slot:SLOT * slot + kc * m]
                             .rearrange("p (kc m) -> p kc m", kc=kc),
                             in_=src, sem=s_ring[slot], semk=16)
                ctx.section("sync", body)

        def rview(key, kc, m):
            return ring[:, SLOT * slot_of[key]:SLOT * slot_of[key] + kc * m]

        def rwait(eng, key):
            sem, val = wait_cnt[key]
            eng.wait(sem, val=val)

        ps_free = {}
        uslot_consumer = {}
        sg_reader = {}

        def ps_wait(eng, idx):
            if idx in ps_free:
                sem, key = ps_free[idx]
                eng.wait(sem, key)

        def ps_set_free(idx, sem, key):
            ps_free[idx] = (sem, key)

        def pro_gp(g):
            g.op("dma_start", out=wg_sb.rearrange("p (kc m) -> p kc m", kc=8),
                 in_=wg_d.rearrange("(kc p) m -> p kc m", p=128), sem=s_res, semk=16)
            g.op("dma_start", out=wo_sb.rearrange("p (kc m) -> p kc m", kc=8),
                 in_=wo_d.rearrange("(kc p) m -> p kc m", p=128), sem=s_res, semk=16)
            g.op("dma_start", out=ws2_sb.rearrange("p (kc m) -> p kc m", kc=12),
                 in_=ws2_d.rearrange("(kc p) m -> p kc m", p=128), sem=s_res, semk=16)
            g.mark(s_res, "all")
            g.op("dma_start", out=wqk_s, in_=wqk_d, sem=s_cast1, semk=16)
            g.mark(s_cast1, "qk")
            g.wait(s_cast1, "qk")
            g.op("dma_start", out=wv_s, in_=wv_d, sem=s_cast1, semk=16)
            g.mark(s_cast1, "v")
            g.wait(s_cast1, "v")
            g.op("dma_start", out=ws1_s, in_=ws1_d, sem=s_cast1, semk=16)
            g.mark(s_cast1, "front")
        ctx.section("gpsimd", pro_gp)

        def pro_sync(s):
            s.op("dma_start", out=ident, in_=ident_d, sem=s_cres, semk=16)
            s.op("dma_start", out=mask4, in_=mask_d, sem=s_cres, semk=16)
            s.op("dma_start", out=sel, in_=sel_d, sem=s_cres, semk=16)
            s.op("dma_start",
                 out=outs2[0].rearrange("p (tc d) -> p tc d", tc=NTC),
                 in_=x_d[0:TILE, :].rearrange("(tc p) d -> p tc d", p=128),
                 sem=s_x, semk=16)
            s.mark(s_cres, "all")
            for tc in range(NTC):
                s.mark(s_x, f"x0:{tc}")
        ctx.section("sync", pro_sync)

        def pro_vec(v):
            v.op("memset", spc, 0.0, sem=None)
            v.op("memset", cst[:, 0:1], EPS, sem=None)
            v.op("memset", cst[:, 1:2], -0.5, sem=None)
            v.op("memset", cst[:, 2:3], -1.0, sem=None)
            v.op("memset", cst[:, 3:4], 0.5, sem=v_m)
            v.mark(v_m, "cst")
            for i in range(4):
                v.op("memset",
                     v_sb[i].rearrange("p (h c) -> p h c", c=65)[:, :, 64:65],
                     1.0, sem=v_m)
            v.mark(v_m, "vones")
        ctx.section("vector", pro_vec)

        ring_flush()

        def early_ln(t, tc):
            def e_v1(v, t=t, tc=tc):
                xsrc = outs2[t % 2][:, D * tc:D * (tc + 1)]
                v.wait(s_x, f"x{t}:{tc}")
                if not (t == 0 and tc == 0):
                    pk = (t, tc - 1) if tc > 0 else (t - 1, NTC - 1)
                    v.wait(v_m, f"eln1s:{pk[0]}:{pk[1]}")
                v.op("bn_stats", bn12[:, 0:6], xsrc[:, 0:512],
                     outs=[bn12[:, 0:6]], ins=[])
                v.op("bn_stats", bn12[:, 6:12], xsrc[:, 512:1024],
                     outs=[bn12[:, 6:12]], ins=[], sem=v_m)
                v.mark(v_m, f"eln1b:{t}:{tc}")
                v.wait(v_m, f"eln1b:{t}:{tc}")
                v.op("bn_aggr", mv[:, 2 * tc:2 * tc + 2], bn12,
                     outs=[mv[:, 2 * tc:2 * tc + 2]], ins=[], sem=v_m,
                     allow_hazard=True)
                v.mark(v_m, f"eln1s:{t}:{tc}")
            ctx.section("vector", e_v1)

            def e_c1(c, t=t, tc=tc):
                c.wait(v_m, "cst")
                c.wait(v_m, f"eln1s:{t}:{tc}")
                c.op("mul", neg[:, 0:1], mv[:, 2 * tc:2 * tc + 1], cst[:, 2:3],
                     sem=c_m)
                c.op("activation", lnt[:, 0:1], mv[:, 2 * tc + 1:2 * tc + 2],
                     AF.Ln, bias=cst[:, 0:1], sem=c_m)
                c.mark(c_m, f"elnv1:{t}:{tc}")
                c.wait(c_m, f"elnv1:{t}:{tc}")
                c.op("activation", rr[:, tc:tc + 1], lnt[:, 0:1], AF.Exp,
                     scale=cst[:, 1:2], sem=c_m)
                c.mark(c_m, f"elnr1:{t}:{tc}")
                c.wait(c_m, f"elnr1:{t}:{tc}")
                c.op("mul", nm[:, tc:tc + 1], neg[:, 0:1], rr[:, tc:tc + 1],
                     sem=c_m)
                c.op("mul", lnt[:, 1:2], mv[:, 2 * tc + 1:2 * tc + 2],
                     rr[:, tc:tc + 1], sem=c_m)
                c.mark(c_m, f"elnn1:{t}:{tc}")
                c.wait(c_m, f"elnn1:{t}:{tc}")
                c.op("mul", lnt[:, 2:3], lnt[:, 1:2], rr[:, tc:tc + 1],
                     sem=c_m)
                c.mark(c_m, f"elnu:{t}:{tc}")
                c.wait(c_m, f"elnu:{t}:{tc}")
                c.op("activation", lnt[:, 1:2], lnt[:, 2:3],
                     AF.Ln, bias=cst[:, 0:1], sem=c_m)
                c.mark(c_m, f"elnv2:{t}:{tc}")
                c.wait(c_m, f"elnv2:{t}:{tc}")
                c.op("activation", lnt[:, 2:3], lnt[:, 1:2], AF.Exp,
                     scale=cst[:, 1:2], sem=c_m)
                c.mark(c_m, f"elnr2:{t}:{tc}")
                c.wait(c_m, f"elnr2:{t}:{tc}")
                c.op("mul", rr[:, NTC + tc:NTC + tc + 1], rr[:, tc:tc + 1],
                     lnt[:, 2:3], sem=c_m)
                c.op("mul", nm[:, NTC + tc:NTC + tc + 1], nm[:, tc:tc + 1],
                     lnt[:, 2:3], sem=c_m)
                c.mark(c_m, f"elnc:{t}:{tc}")
            ctx.section("scalar", e_c1)

        def ln_stage(tc):
            if tc % 2 == 0:
                return ((xn_tmb[:, 0:512], xn_tmb[:, 512:1024]),
                        (xn2_tmb[:, 0:512], xn2_tmb[:, 512:1024]))
            return ((a_m[0], a_m[1]), (a_m[2], a_m[3]))

        def emit_ap_c1(t, tc):
            def ap_c1(c, t=t, tc=tc):
                out_b = outs2[t % 2]
                (x0, x1), (y0, y1) = ln_stage(tc)
                c.wait(s_x, f"x{t}:{tc}")
                c.wait(c_m, f"elnc:{t}:{tc}")
                if t > 0 or tc > 1:
                    pk = (t, tc - 2) if tc > 1 else (t - 1, NTC - 2 + tc)
                    c.wait(t_m, f"xtrd:{pk[0]}:{pk[1]}")
                    c.wait(t_m, f"x2trd:{pk[0]}:{pk[1]}")
                    if tc <= 1:
                        c.wait(t_m, f"ytr:{t-1}:{NTC-1}")
                        c.wait(t_m, f"szmm:{t-1}:{NTC-1}")
                        if tc == 1:
                            c.wait(t_m, f"omm:{t-1}:{NTC-1}:3")
                for half, dst in ((0, x0), (1, x1)):
                    c.op("activation", dst,
                         out_b[:, D * tc + 512 * half:D * tc + 512 * half + 512],
                         AF.Identity, bias=nm[:, tc:tc + 1],
                         scale=rr[:, tc:tc + 1],
                         sem=(c_m if half == 1 else None))
                c.mark(c_m, f"xnbf:{t}:{tc}")
                for half, dst in ((0, y0), (1, y1)):
                    c.op("activation", dst,
                         out_b[:, D * tc + 512 * half:D * tc + 512 * half + 512],
                         AF.Identity, bias=nm[:, NTC + tc:NTC + tc + 1],
                         scale=rr[:, NTC + tc:NTC + tc + 1],
                         sem=(c_m if half == 1 else None))
                c.mark(c_m, f"xn2bf:{t}:{tc}")
            ctx.section("scalar", ap_c1)

        def go(name):
            if STOP_AFTER is None:
                return True
            sa = "att" if STOP_AFTER in _SUBATT else STOP_AFTER
            return _PHASES.index(name) <= _PHASES.index(sa)

        for tc0 in range(NTC):
            early_ln(0, tc0)

        for t in range(n_tiles):
            tis = t % tps
            tile_done = [None]
            out_sb = outs2[t % 2]

            for tc in range(NTC) if go("ln") else []:
                if not (t > 0 and tc <= 1 and STOP_AFTER is None):
                    emit_ap_c1(t, tc)

                def ap_tr(tn, t=t, tc=tc):
                    tn.wait(s_cres, "all")
                    (x0, x1), (y0, y1) = ln_stage(tc)
                    for half, (bufname, srcs, mk) in enumerate((
                            ("x", (x0, x1), f"xnbf:{t}:{tc}"),
                            ("x2", (y0, y1), f"xn2bf:{t}:{tc}"))):
                        tn.wait(c_m, mk)
                        ps_wait(tn, 6 + half)
                        for kc in range(8):
                            src = srcs[kc // 4]
                            tn.op("matmul", pt[half][:, 128 * kc:128 * (kc + 1)],
                                  src[:, 128 * (kc % 4):128 * (kc % 4) + 128], ident,
                                  start=True, stop=True, is_transpose=True,
                                  sem=(t_m if kc == 7 else None))
                        tn.mark(t_m, f"{bufname}tr:{t}:{tc}")
                        ps_set_free(6 + half, v_m, f"{bufname}trev:{t}:{tc}")
                        tn.mark(t_m, f"{bufname}trd:{t}:{tc}")
                ctx.section("tensor", ap_tr)

                def ap_trev(v, t=t, tc=tc):
                    for half, bufname in enumerate(("x", "x2")):
                        dstt = xn_f if bufname == "x" else xn2_f
                        v.wait(t_m, f"{bufname}tr:{t}:{tc}")
                        dv = dstt.rearrange("p (kc m) -> p kc m", kc=8) \
                            [:, 0:8, 128 * tc:128 * tc + 128]
                        v.op("tensor_copy", dv,
                             pt[half].rearrange("p (j m) -> p j m", j=8),
                             outs=[dstt], ins=[], sem=v_m)
                        v.mark(v_m, f"{bufname}trev:{t}:{tc}")
                ctx.section("vector", ap_trev)

            def gate_mm(tn, t=t):
                tn.wait(s_res, "all")
                ps_wait(tn, 4)
                for tc in range(NTC):
                    tn.wait(v_m, f"xtrev:{t}:{tc}")
                    for kc in range(8):
                        tn.op("matmul", pf[4][:, 8 * tc:8 * tc + 8],
                              xn_f[:, 512 * kc + 128 * tc:512 * kc + 128 * tc + 128],
                              wg_sb[:, 8 * kc:8 * kc + 8],
                              start=(kc == 0), stop=(kc == 7),
                              sem=(t_m if kc == 7 else None))
                    tn.mark(t_m, f"gmm:{t}:{tc}")
                ps_set_free(4, v_m, f"glev:{t}")
            if go("gate"):
                ctx.section("tensor", gate_mm)

            def gate_v1(v, t=t):
                v.wait(t_m, f"gmm:{t}:{NTC - 1}")
                v.op("tensor_copy", gl_sb, pf[4][:, 0:NTC * E],
                     outs=[gl_sb], ins=[], sem=v_m)
                v.mark(v_m, f"glev:{t}")
                v.wait(v_m, f"glev:{t}")
                for tc in range(NTC):
                    v.op("reduce_max", gsc[:, tc:tc + 1], gl_sb[:, 8 * tc:8 * tc + 8],
                         axis=AX.X, outs=[gsc[:, tc:tc + 1]], ins=[],
                         sem=(v_m if tc == NTC - 1 else None), allow_hazard=True)
                v.mark(v_m, f"grm:{t}")
                v.wait(v_m, f"grm:{t}")
                for tc in range(NTC):
                    v.op("tensor_scalar", gt[:, 8 * tc:8 * tc + 8],
                         gl_sb[:, 8 * tc:8 * tc + 8], gsc[:, tc:tc + 1], None,
                         ALU.subtract, outs=[gt[:, 8 * tc:8 * tc + 8]],
                         ins=[gsc[:, tc:tc + 1]], sem=v_m, allow_hazard=True)
                    v.mark(v_m, f"gsh:{t}:{tc}")
            if go("gate"):
                ctx.section("vector", gate_v1)

            def gate_c1(c, t=t):
                for tc in range(NTC):
                    c.wait(v_m, f"gsh:{t}:{tc}")
                    c.op("activation", gtp[:, 8 * tc:8 * tc + 8], gt[:, 8 * tc:8 * tc + 8],
                         AF.Exp, accum_out=gsc[:, 4 + tc:5 + tc], sem=c_m)
                    c.mark(c_m, f"gexp:{t}:{tc}")
            if go("gate"):
                ctx.section("scalar", gate_c1)

            def gate_v2(v, t=t):
                for tc in range(NTC):
                    v.wait(c_m, f"gexp:{t}:{tc}")
                    v.op("reciprocal", gsc[:, 16 + tc:17 + tc], gsc[:, 4 + tc:5 + tc],
                         outs=[gsc[:, 16 + tc:17 + tc]], ins=[gsc[:, 4 + tc:5 + tc]],
                         sem=v_m, allow_hazard=(tc > 0))
                    v.mark(v_m, f"grp:{t}:{tc}")
            if go("gate"):
                ctx.section("vector", gate_v2)

            def gate_c2(c, t=t):
                for tc in range(NTC):
                    c.wait(v_m, f"grp:{t}:{tc}")
                    c.wait(c_m, f"gexp:{t}:{tc}")
                    c.op("activation", w8[:, 8 * tc:8 * tc + 8], gtp[:, 8 * tc:8 * tc + 8],
                         AF.Identity, scale=gsc[:, 16 + tc:17 + tc], sem=c_m)
                    c.mark(c_m, f"gpr:{t}:{tc}")
            if go("gate"):
                ctx.section("scalar", gate_c2)

            def gate_v3(v, t=t):
                def loop(tag, fn, n=NTC):
                    for tc in range(n):
                        fn(tc, tc == n - 1)
                    v.mark(v_m, f"{tag}:{t}")
                    v.wait(v_m, f"{tag}:{t}")
                for tc in range(NTC):
                    v.wait(c_m, f"gpr:{t}:{tc}")
                loop("g_m1", lambda tc, last: v.op(
                    "reduce_max", gsc[:, 8 + tc:9 + tc], w8[:, 8 * tc:8 * tc + 8],
                    axis=AX.X, outs=[gsc[:, 8 + tc:9 + tc]], ins=[],
                    sem=(v_m if last else None)))
                loop("g_eq", lambda tc, last: v.op(
                    "tensor_scalar", gt[:, 8 * tc:8 * tc + 8],
                    w8[:, 8 * tc:8 * tc + 8], gsc[:, 8 + tc:9 + tc], None,
                    ALU.is_ge, outs=[gt[:, 8 * tc:8 * tc + 8]],
                    ins=[gsc[:, 8 + tc:9 + tc]],
                    sem=(v_m if last else None), allow_hazard=True))
                loop("g_stt", lambda tc, last: v.op(
                    "scalar_tensor_tensor", gt[:, 8 * tc:8 * tc + 8],
                    gt[:, 8 * tc:8 * tc + 8], NEG_BIG, w8[:, 8 * tc:8 * tc + 8],
                    ALU.mult, ALU.add, outs=[gt[:, 8 * tc:8 * tc + 8]], ins=[],
                    sem=(v_m if last else None), allow_hazard=True))
                loop("g_m2", lambda tc, last: v.op(
                    "reduce_max", gsc[:, 12 + tc:13 + tc], gt[:, 8 * tc:8 * tc + 8],
                    axis=AX.X, outs=[gsc[:, 12 + tc:13 + tc]], ins=[],
                    sem=(v_m if last else None), allow_hazard=True))
                loop("g_msk", lambda tc, last: v.op(
                    "tensor_scalar", gt[:, 8 * tc:8 * tc + 8],
                    w8[:, 8 * tc:8 * tc + 8], gsc[:, 12 + tc:13 + tc], None,
                    ALU.is_ge, outs=[gt[:, 8 * tc:8 * tc + 8]],
                    ins=[gsc[:, 12 + tc:13 + tc]],
                    sem=(v_m if last else None), allow_hazard=True))
                loop("g_w8", lambda tc, last: v.op(
                    "tensor_mul", w8[:, 8 * tc:8 * tc + 8],
                    gt[:, 8 * tc:8 * tc + 8], w8[:, 8 * tc:8 * tc + 8],
                    outs=[w8[:, 8 * tc:8 * tc + 8]],
                    ins=[gt[:, 8 * tc:8 * tc + 8]],
                    sem=(v_m if last else None), allow_hazard=True))
                for tc in range(NTC):
                    v.op("tensor_copy", w8b[:, 8 * tc:8 * tc + 8],
                         w8[:, 8 * tc:8 * tc + 8],
                         outs=[w8b[:, 8 * tc:8 * tc + 8]],
                         ins=[w8[:, 8 * tc:8 * tc + 8]], sem=v_m,
                         allow_hazard=True)
                    v.mark(v_m, f"gw8b:{t}:{tc}")
            if go("gate"):
                ctx.section("vector", gate_v3)


            def qkv_v1(v, fcg, t=t):
                bank = fcg % 4
                v.wait(t_m, f"qkmm:{t}:{fcg}")
                if fcg >= 2:
                    v.wait(v_m, f"eluv2:{t}:{fcg - 2}")
                v.op("tensor_scalar_min",
                     elu1[:, 512 * (fcg % 2):512 * (fcg % 2) + 512],
                     pf[bank], 0.0,
                     outs=[elu1[:, 512 * (fcg % 2):512 * (fcg % 2) + 512]],
                     ins=[], sem=v_m)
                v.mark(v_m, f"eluv1:{t}:{fcg}")

            def qkv_v2(v, fcg, t=t):
                bank = fcg % 4
                v.wait(c_m, f"eluexp:{t}:{fcg}")
                dst = q_f if fcg < 8 else k_f
                fc = fcg % 8
                v.op("scalar_tensor_tensor", dst[:, 512 * fc:512 * (fc + 1)],
                     pf[bank], 0.0,
                     elu1[:, 512 * (fcg % 2):512 * (fcg % 2) + 512],
                     ALU.max, ALU.add,
                     outs=[dst[:, 512 * fc:512 * (fc + 1)]],
                     ins=[elu1[:, 512 * (fcg % 2):512 * (fcg % 2) + 512]],
                     sem=v_m)
                v.mark(v_m, f"eluv2:{t}:{fcg}")

            for g in range(8) if go("qkv") else []:
                def qkv_mm(tn, t=t, g=g):
                    if g == 0:
                        tn.wait(v_m, f"x2trev:{t}:{NTC - 1}")
                    for fcg in (2 * g, 2 * g + 1):
                        key = [f"qhA{t}", f"qhB{t}", f"khA{t}", f"khB{t}"][fcg // 4]
                        rwait(tn, key)
                        wsrc = rview(key, 8, 512)
                        fc = fcg % 4
                        bank = fcg % 4
                        ps_wait(tn, bank)
                        for kc in range(8):
                            tn.op("matmul", pf[bank],
                                  wsrc[:, 512 * kc + 128 * fc:512 * kc + 128 * fc + 128],
                                  xn2_f[:, 512 * kc:512 * (kc + 1)],
                                  start=(kc == 0), stop=(kc == 7),
                                  sem=(t_m if kc == 7 else None))
                        tn.mark(t_m, f"qkmm:{t}:{fcg}")
                        ps_set_free(bank, v_m, f"eluv2:{t}:{fcg}")
                        if fcg % 4 == 3:
                            tn.mark(t_m, f"use:{key}")
                ctx.section("tensor", qkv_mm)

                def qkv_vv(v, t=t, g=g):
                    if g > 0:
                        qkv_v2(v, 2 * g - 2)
                        qkv_v2(v, 2 * g - 1)
                    qkv_v1(v, 2 * g)
                    qkv_v1(v, 2 * g + 1)
                ctx.section("vector", qkv_vv)

                def qkv_cc(c, t=t, g=g):
                    for fcg in (2 * g, 2 * g + 1):
                        c.wait(v_m, f"eluv1:{t}:{fcg}")
                        if fcg >= 2:
                            c.wait(v_m, f"eluv2:{t}:{fcg - 2}")
                        c.op("activation",
                             elu1[:, 512 * (fcg % 2):512 * (fcg % 2) + 512],
                             elu1[:, 512 * (fcg % 2):512 * (fcg % 2) + 512],
                             AF.Exp, sem=c_m)
                        c.mark(c_m, f"eluexp:{t}:{fcg}")
                ctx.section("scalar", qkv_cc)

            if go("qkv"):
                def qkv_fin(v, t=t):
                    qkv_v2(v, 14)
                    qkv_v2(v, 15)
                ctx.section("vector", qkv_fin)
                tile_done[0] = (v_m, f"eluv2:{t}:15")

            ring_flush()
            for tc in range(NTC) if go("qkv") else []:
                def vprod_mm(tn, t=t, tc=tc):
                    tn.wait(v_m, "vones")
                    for half in range(2):
                        key = f"wvA{t}" if half == 0 else f"wvB{t}"
                        rwait(tn, key)
                        wvv = rview(key, 8, 512)
                        bank = (16 + 2 * tc + half) % 4
                        ps_wait(tn, bank)
                        for kc in range(8):
                            tn.op("matmul", pf[bank],
                                  xn2_f[:, 512 * kc + 128 * tc:512 * kc + 128 * tc + 128],
                                  wvv[:, 512 * kc:512 * kc + 512],
                                  start=(kc == 0), stop=(kc == 7),
                                  sem=(t_m if kc == 7 else None))
                        tn.mark(t_m, f"vmm:{t}:{tc}:{half}")
                        ps_set_free(bank, v_m, f"vev:{t}:{tc}:{half}")
                        if tc == NTC - 1:
                            tn.mark(t_m, f"use:{key}")
                ctx.section("tensor", vprod_mm)

                def vprod_ev(v, t=t, tc=tc):
                    vb = v_sb[tc]
                    if t > 0:
                        v.wait(t_m, f"szmm:{t - 1}:{tc}")
                    for half in range(2):
                        v.wait(t_m, f"vmm:{t}:{tc}:{half}")
                        dv = vb[:, 65 * 8 * half:65 * 8 * half + 520] \
                            .rearrange("p (h c) -> p h c", c=65)[:, :, 0:64]
                        v.op("tensor_copy", dv,
                             pf[(16 + 2 * tc + half) % 4].rearrange("p (h c) -> p h c", c=64),
                             outs=[vb], ins=[], sem=v_m)
                        v.mark(v_m, f"vev:{t}:{tc}:{half}")
                ctx.section("vector", vprod_ev)

            ring_flush()
            if STOP_AFTER is not None and STOP_AFTER in _SUBATT:
                _attn = 1
            elif STOP_AFTER is not None and STOP_AFTER.startswith("att") and len(STOP_AFTER) == 5 and STOP_AFTER[3].isdigit():
                _attn = int(STOP_AFTER[3])
            elif go("att"):
                _attn = NTC
            else:
                _attn = 0

            def ktr_mms(tn, t, tc):
                ps_wait(tn, 6)
                for hc in range(8):
                    tn.op("matmul", pt[0][:, 128 * hc:128 * (hc + 1)],
                          k_f[:, 512 * hc + 128 * tc:512 * hc + 128 * tc + 128],
                          ident, start=True, stop=True, is_transpose=True,
                          sem=(t_m if hc == 7 else None))
                tn.mark(t_m, f"ktr:{t}:{tc}")
                ps_set_free(6, v_m, f"ktrev:{t}:{tc}")

            def ktrev(v, t, tc):
                v.wait(t_m, f"ktr:{t}:{tc}")
                v.op("tensor_copy", k_tm, pt[0],
                     outs=[k_tm], ins=[], sem=v_m)
                v.mark(v_m, f"ktrev:{t}:{tc}")

            def a_round(tn, rnd, t, tc):
                ps_wait(tn, 0)
                ps_wait(tn, 1)
                for i in range(8):
                    h = 8 * rnd + i
                    fc = h // 2
                    off = 64 * (h % 2)
                    bank = i % 2
                    blk = i // 2
                    tn.op("matmul", pf[bank][:, 128 * blk:128 * (blk + 1)],
                          k_f[off:off + 64,
                              512 * fc + 128 * tc:512 * fc + 128 * tc + 128],
                          q_f[off:off + 64,
                              512 * fc + 128 * tc:512 * fc + 128 * tc + 128],
                          start=True, stop=True,
                          sem=(t_m if i >= 6 else None))
                tn.mark(t_m, f"ammr:{t}:{tc}:{rnd}")
                ps_set_free(0, v_m, f"amevr:{t}:{tc}:{rnd}")
                ps_set_free(1, v_m, f"amevr:{t}:{tc}:{rnd}")

            def am_round(v, rnd, t, tc):
                v.wait(t_m, f"ammr:{t}:{tc}:{rnd}")
                for par in range(2):
                    v.op("tensor_tensor",
                         a_m[2 * rnd + par].rearrange("p (r m) -> p r m", r=4),
                         pf[par].rearrange("p (r m) -> p r m", r=4),
                         mask4.unsqueeze(1).broadcast_to([128, 4, 128]), ALU.mult,
                         outs=[a_m[2 * rnd + par]], ins=[],
                         sem=(v_m if par == 1 else None))
                v.mark(v_m, f"amevr:{t}:{tc}:{rnd}")

            def o_pair(tn, pair, t, tc, cq):
                vb = v_sb[tc]
                rnd = pair
                tn.wait(v_m, f"amevr:{t}:{tc}:{rnd}")
                tn.wait(v_m, f"vev:{t}:{tc}:0")
                tn.wait(v_m, f"vev:{t}:{tc}:1")
                if cq > 0:
                    pk = (t, tc - 1) if tc > 0 else (t - 1, NTC - 1)
                    tn.wait(v_m, f"szbf:{pk[0]}:{pk[1]}")
                for qq in range(2):
                    q4 = 2 * pair + qq
                    bO = 2 + 2 * pair + qq
                    ps_wait(tn, bO)
                    for j in range(4):
                        h = 4 * q4 + j
                        fc = h // 2
                        off = 64 * (h % 2)
                        am = a_m[2 * rnd + h % 2]
                        blk = (h - 8 * rnd) // 2
                        tn.op("matmul", pf[bO][:, 65 * j:65 * j + 65],
                              am[:, 128 * blk:128 * (blk + 1)],
                              vb[:, 65 * h:65 * h + 65],
                              start=True, stop=(cq == 0),
                              sem=(t_m if (cq == 0 and j == 3) else None))
                        if cq > 0:
                            tn.op("matmul", pf[bO][:, 65 * j:65 * j + 65],
                                  q_f[off:off + 64,
                                      512 * fc + 128 * tc:512 * fc + 128 * tc + 128],
                                  sz_bf[off:off + 64,
                                        65 * (h // 2):65 * (h // 2) + 65],
                                  start=False, stop=True,
                                  sem=(t_m if j == 3 else None))
                    tn.mark(t_m, f"omm:{t}:{tc}:{q4}")
                    ps_set_free(bO, c_m, f"yscale:{t}:{tc}:{q4}")

            def gath(v, q4, t, tc):
                v.wait(t_m, f"omm:{t}:{tc}:{q4}")
                if q4 >= 2:
                    v.wait(v_m, f"rde:{t}:{tc}:{q4 - 2}")
                elif t > 0 or tc > 0:
                    pk = (t, tc - 1) if tc > 0 else (t - 1, NTC - 1)
                    v.wait(v_m, f"rde:{pk[0]}:{pk[1]}:{q4 + 2}")
                v.op("tensor_copy", d4[:, 4 * (q4 % 2):4 * (q4 % 2) + 4],
                     pf[2 + q4][:, 64:64 + 4 * 65:65],
                     outs=[d4[:, 4 * (q4 % 2):4 * (q4 % 2) + 4]], ins=[],
                     sem=v_m)
                v.mark(v_m, f"gath:{t}:{tc}:{q4}")

            def recip(v, q4, t, tc):
                if q4 >= 2:
                    v.wait(c_m, f"yscale:{t}:{tc}:{q4 - 2}")
                elif t > 0 or tc > 0:
                    pk = (t, tc - 1) if tc > 0 else (t - 1, NTC - 1)
                    v.wait(c_m, f"yscale:{pk[0]}:{pk[1]}:{q4 + 2}")
                v.wait(v_m, f"gath:{t}:{tc}:{q4}")
                v.op("reciprocal", r4[:, 4 * (q4 % 2):4 * (q4 % 2) + 4],
                     d4[:, 4 * (q4 % 2):4 * (q4 % 2) + 4],
                     outs=[r4[:, 4 * (q4 % 2):4 * (q4 % 2) + 4]],
                     ins=[d4[:, 4 * (q4 % 2):4 * (q4 % 2) + 4]],
                     sem=v_m, allow_hazard=True)
                v.mark(v_m, f"rde:{t}:{tc}:{q4}")

            def yscale_pair(c, pair, t, tc):
                for qq in range(2):
                    q4 = 2 * pair + qq
                    c.wait(v_m, f"rde:{t}:{tc}:{q4}")
                    for j in range(4):
                        h = 4 * q4 + j
                        c.op("activation", y_tm[:, 64 * h:64 * h + 64],
                             pf[2 + q4][:, 65 * j:65 * j + 64],
                             AF.Copy,
                             scale=r4[:, 4 * (q4 % 2) + j:4 * (q4 % 2) + j + 1],
                             sem=(c_m if j == 3 else None))
                    c.mark(c_m, f"yscale:{t}:{tc}:{q4}")

            if _attn > 0:
                def att_pre(tn, t=t):
                    tn.wait(v_m, f"eluv2:{t}:15")
                    tn.wait(v_m, f"vev:{t}:0:1")
                    ktr_mms(tn, t, 0)
                    a_round(tn, 0, t, 0)
                ctx.section("tensor", att_pre)

                def att_prev(v, t=t):
                    ktrev(v, t, 0)
                    am_round(v, 0, t, 0)
                ctx.section("vector", att_prev)

            for tc in range(_attn):
                cq = NTC * tis + tc

                def att_t1(tn, t=t, tc=tc, cq=cq):
                    a_round(tn, 1, t, tc)
                ctx.section("tensor", att_t1)

                def att_v1(v, t=t, tc=tc):
                    am_round(v, 1, t, tc)
                ctx.section("vector", att_v1)

                def att_t2(tn, t=t, tc=tc, cq=cq):
                    o_pair(tn, 0, t, tc, cq)
                ctx.section("tensor", att_t2)

                def att_v2(v, t=t, tc=tc):
                    gath(v, 0, t, tc)
                    gath(v, 1, t, tc)
                    recip(v, 0, t, tc)
                    recip(v, 1, t, tc)
                ctx.section("vector", att_v2)

                def att_s1(c, t=t, tc=tc):
                    yscale_pair(c, 0, t, tc)
                ctx.section("scalar", att_s1)

                def att_t3(tn, t=t, tc=tc, cq=cq):
                    o_pair(tn, 1, t, tc, cq)
                ctx.section("tensor", att_t3)

                def att_v3(v, t=t, tc=tc):
                    gath(v, 2, t, tc)
                    gath(v, 3, t, tc)
                    recip(v, 2, t, tc)
                    recip(v, 3, t, tc)
                ctx.section("vector", att_v3)

                def att_s2(c, t=t, tc=tc):
                    yscale_pair(c, 1, t, tc)
                ctx.section("scalar", att_s2)

                def att_t4(tn, t=t, tc=tc, cq=cq):
                    vb = v_sb[tc]
                    tn.wait(v_m, f"ktrev:{t}:{tc}")
                    if cq > 0:
                        pk = (t, tc - 1) if tc > 0 else (t - 1, NTC - 1)
                        tn.wait(v_m, f"szbf:{pk[0]}:{pk[1]}")
                    for half in range(2):
                        bank = 2 + half
                        ps_wait(tn, bank)
                        for pp in range(4):
                            p = 4 * half + pp
                            for par in range(2):
                                h = 2 * p + par
                                tn.op("matmul",
                                      pf[bank][64 * par:64 * par + 64,
                                               65 * pp:65 * pp + 65],
                                      k_tm[:, 128 * p + 64 * par:128 * p + 64 * par + 64],
                                      vb[:, 65 * h:65 * h + 65],
                                      start=True, stop=True,
                                      tile_position=(0, 64 * par),
                                      sem=(t_m if (pp == 3 and par == 1) else None))
                        tn.mark(t_m, f"szd:{t}:{tc}:{half}")
                        ps_set_free(bank, v_m, f"szup:{t}:{tc}:{half}")
                    tn.mark(t_m, f"szmm:{t}:{tc}")
                    if tc + 1 < NTC:
                        ktr_mms(tn, t, tc + 1)
                        a_round(tn, 0, t, tc + 1)
                    tn.wait(c_m, f"yscale:{t}:{tc}:3")
                    ps_wait(tn, 7)
                    for yc in range(8):
                        tn.op("matmul", pt[1][:, 128 * yc:128 * (yc + 1)],
                              y_tm[:, 128 * yc:128 * (yc + 1)], ident,
                              start=True, stop=True, is_transpose=True,
                              sem=(t_m if yc == 7 else None))
                    tn.mark(t_m, f"ytr:{t}:{tc}")
                    ps_set_free(7, v_m, f"ytrev:{t}:{tc}")
                ctx.section("tensor", att_t4)

                def att_v4(v, t=t, tc=tc, cq=cq):
                    if tc + 1 < NTC:
                        ktrev(v, t, tc + 1)
                        am_round(v, 0, t, tc + 1)
                    v.wait(t_m, f"ytr:{t}:{tc}")
                    dv = y_fT.rearrange("p (kc m) -> p kc m", kc=8) \
                        [:, 0:8, 128 * tc:128 * tc + 128]
                    v.op("tensor_copy", dv,
                         pt[1].rearrange("p (j m) -> p j m", j=8),
                         outs=[y_fT], ins=[], sem=v_m)
                    v.mark(v_m, f"ytrev:{t}:{tc}")
                    for half in range(2):
                        v.wait(t_m, f"szd:{t}:{tc}:{half}")
                        dst = sz_m[:, 260 * half:260 * half + 260]
                        if cq == 0:
                            v.op("tensor_copy", dst, pf[2 + half][:, 0:260],
                                 outs=[dst], ins=[], sem=v_m)
                        else:
                            v.op("tensor_tensor", dst, dst, pf[2 + half][:, 0:260],
                                 ALU.add, outs=[dst], ins=[], sem=v_m)
                        v.mark(v_m, f"szup:{t}:{tc}:{half}")
                    v.wait(v_m, f"szup:{t}:{tc}:0")
                    v.wait(v_m, f"szup:{t}:{tc}:1")
                    for half in range(2):
                        v.op("tensor_copy", sz_bf[:, 260 * half:260 * half + 260],
                             sz_m[:, 260 * half:260 * half + 260],
                             outs=[sz_bf[:, 260 * half:260 * half + 260]],
                             ins=[sz_m[:, 260 * half:260 * half + 260]],
                             sem=(v_m if half == 1 else None),
                             allow_hazard=True)
                    v.mark(v_m, f"szbf:{t}:{tc}")
                ctx.section("vector", att_v4)

            ring_flush()
            if _attn > 0:
                if STOP_AFTER in _SUBATT and STOP_AFTER != "attt4":
                    tile_done[0] = {"attt1": (v_m, f"amevr:{t}:0:0"),
                                    "attt2": (v_m, f"rde:{t}:0:1"),
                                    "atts1": (c_m, f"yscale:{t}:0:1"),
                                    "attt3": (v_m, f"rde:{t}:0:3"),
                                    "atts2": (c_m, f"yscale:{t}:0:3")}[STOP_AFTER]
                else:
                    tile_done[0] = (v_m, f"szbf:{t}:{_attn - 1}")

            def gate_tr(tn, t=t):
                ps_wait(tn, 6)
                for tc in range(NTC):
                    tn.wait(v_m, f"gw8b:{t}:{tc}")
                    tn.op("matmul", pt[0][0:8, 128 * tc:128 * (tc + 1)],
                          w8b[:, 8 * tc:8 * tc + 8], ident,
                          start=True, stop=True, is_transpose=True,
                          sem=(t_m if tc == NTC - 1 else None))
                tn.mark(t_m, f"gw8tr:{t}")
                ps_set_free(6, v_m, f"gw8tev:{t}")
            if go("gate"):
                ctx.section("tensor", gate_tr)

            def gate_trev(v, t=t):
                v.wait(t_m, f"gw8tr:{t}")
                v.op("tensor_copy", w8t, pt[0][0:8, 0:512],
                     outs=[w8t], ins=[], sem=v_m)
                v.mark(v_m, f"gw8tev:{t}")
            if go("gate"):
                ctx.section("vector", gate_trev)

            if go("gate"):
                tile_done[0] = (v_m, f"gw8tev:{t}")


            for pj in range(6) if go("ush") else []:
                def ush_mm(tn, t=t, pj=pj):
                    if pj == 0:
                        tn.wait(v_m, f"ytrev:{t}:{NTC - 1}")
                    for j in (2 * pj, 2 * pj + 1):
                        qy, qg = j // 6, 2 + j // 6
                        ky, kg = f"s1_{t}_{qy}", f"s1_{t}_{qg}"
                        rwait(tn, ky)
                        rwait(tn, kg)
                        wy = rview(ky, 8, 768)
                        wgr = rview(kg, 8, 768)
                        hy = j % 6
                        bY, bG = j % 3, 3 + j % 3
                        ps_wait(tn, bY)
                        for kc in range(8):
                            tn.op("matmul", pf[bY],
                                  wy[:, 768 * kc + 128 * hy:768 * kc + 128 * hy + 128],
                                  xn_f[:, 512 * kc:512 * (kc + 1)],
                                  start=(kc == 0), stop=(kc == 7),
                                  sem=(t_m if kc == 7 else None))
                        tn.mark(t_m, f"usy:{t}:{j}")
                        ps_set_free(bY, v_m, f"usu:{t}:{j}")
                        ps_wait(tn, bG)
                        for kc in range(8):
                            tn.op("matmul", pf[bG],
                                  wgr[:, 768 * kc + 128 * hy:768 * kc + 128 * hy + 128],
                                  xn_f[:, 512 * kc:512 * (kc + 1)],
                                  start=(kc == 0), stop=(kc == 7),
                                  sem=(t_m if kc == 7 else None))
                        tn.mark(t_m, f"usg:{t}:{j}")
                        ps_set_free(bG, v_m, f"ustg:{t}:{j // 2}:{j % 2}")
                        if j == 5:
                            tn.mark(t_m, f"use:s1_{t}_0")
                            tn.mark(t_m, f"use:s1_{t}_2")
                        if j == 11:
                            tn.mark(t_m, f"use:s1_{t}_1")
                            tn.mark(t_m, f"use:s1_{t}_3")
                ctx.section("tensor", ush_mm)

                def ush_c(c, t=t, pj=pj):
                    for j in (2 * pj, 2 * pj + 1):
                        c.wait(t_m, f"usg:{t}:{j}")
                        rd = sg_reader.get(j % 2)
                        if rd is not None:
                            c.wait(rd[0], rd[1])
                        c.op("activation", sg_sb[j % 2], pf[3 + j % 3],
                             AF.Silu, sem=c_m)
                        c.mark(c_m, f"ussg:{t}:{j}")
                ctx.section("scalar", ush_c)

                def ush_v(v, t=t, pj=pj):
                    for j in (2 * pj, 2 * pj + 1):
                        v.wait(c_m, f"ussg:{t}:{j}")
                        v.wait(t_m, f"usy:{t}:{j}")
                        v.op("tensor_tensor", us_sb[:, 512 * j:512 * (j + 1)],
                             pf[j % 3], sg_sb[j % 2], ALU.mult,
                             outs=[us_sb[:, 512 * j:512 * (j + 1)]],
                             ins=[sg_sb[j % 2]], sem=v_m)
                        v.mark(v_m, f"usu:{t}:{j}")
                        v.mark(v_m, f"ustg:{t}:{j // 2}:{j % 2}")
                        sg_reader[j % 2] = (v_m, f"usu:{t}:{j}")
                ctx.section("vector", ush_v)

            ring_flush()
            if go("ush"):
                tile_done[0] = (v_m, f"usu:{t}:11")

            for ph in range(2) if go("z0") else []:
                def z0_mm(tn, t=t, ph=ph):
                    if ph == 0:
                        tn.wait(s_res, "all")
                    for tc in (2 * ph, 2 * ph + 1):
                        banks = (2, 3) if tc % 2 == 0 else (4, 5)
                        if ph == 0 and tc == 0:
                            tn.wait(v_m, f"ytrev:{t}:{NTC - 1}")
                        for half in range(2):
                            b = banks[half]
                            ps_wait(tn, b)
                            for yc in range(8):
                                tn.op("matmul", pf[b],
                                      y_fT[:, 512 * yc + 128 * tc:512 * yc + 128 * tc + 128],
                                      wo_sb[:, 1024 * yc + 512 * half:1024 * yc + 512 * half + 512],
                                      start=(yc == 0), stop=False)
                            for hc in range(12):
                                if ph == 0 and tc == 0 and half == 0:
                                    tn.wait(v_m, f"usu:{t}:{hc}")
                                tn.op("matmul", pf[b],
                                      us_sb[:, 512 * hc + 128 * tc:512 * hc + 128 * tc + 128],
                                      ws2_sb[:, 1024 * hc + 512 * half:1024 * hc + 512 * half + 512],
                                      start=False, stop=(hc == 11),
                                      sem=(t_m if hc == 11 else None))
                            tn.mark(t_m, f"z:{t}:0:{tc}:{half}")
                            ps_set_free(b, v_m, f"zev:{t}:0:{tc}:{half}")
                ctx.section("tensor", z0_mm)

                def z0_ev(v, t=t, ph=ph):
                    for tc in (2 * ph, 2 * ph + 1):
                        v.wait(s_x, f"x{t}:{tc}")
                        banks = (2, 3) if tc % 2 == 0 else (4, 5)
                        for half in range(2):
                            v.wait(t_m, f"z:{t}:0:{tc}:{half}")
                            dst = out_sb[:, D * tc + 512 * half:D * tc + 512 * half + 512]
                            v.op("tensor_tensor", dst, dst, pf[banks[half]],
                                 ALU.add, outs=[dst], ins=[], sem=v_m,
                                 allow_hazard=True)
                            v.mark(v_m, f"zev:{t}:0:{tc}:{half}")
                ctx.section("vector", z0_ev)

            if go("z0"):
                tile_done[0] = (v_m, f"zev:{t}:0:{NTC - 1}:1")
            XMAP = {0: (0, 1), 1: (2,), 2: (3,), 3: ()}
            for w in [w_ for w_ in range(4) if go(f"uz{w_}")]:
                if STOP_AFTER is None and t + 1 < n_tiles:
                    for xw in XMAP[w]:
                        def xload(sy, t=t, xw=xw):
                            if t > 0:
                                sy.wait(s_out, f"o{t - 1}:{xw}")
                            pk = (t + 1, xw - 1) if xw > 0 else (t, NTC - 1)
                            sy.wait(s_x, f"x{pk[0]}:{pk[1]}")
                            sy.op("dma_start",
                                  out=outs2[(t + 1) % 2][:, D * xw:D * (xw + 1)],
                                  in_=x_d[TILE * (t + 1) + TC * xw:
                                          TILE * (t + 1) + TC * (xw + 1), :],
                                  sem=s_x, semk=16)
                            sy.mark(s_x, f"x{t + 1}:{xw}")
                        ctx.section("sync", xload)
                    early_ln(t + 1, w)
                for ee in range(2):
                    e = 2 * w + ee

                    def ue_pre(tn, t=t, e=e):
                        tn.wait(v_m, f"gw8tev:{t}")
                        ps_wait(tn, 4)
                        tn.op("matmul", pf[4], sel[:, 128 * e:128 * (e + 1)],
                              w8t, start=True, stop=True, sem=t_m)
                        tn.mark(t_m, f"wbmm:{t}:{e}")
                        ps_set_free(4, v_m, f"wbev:{t}:{e}")
                    ctx.section("tensor", ue_pre)

                    def ue_vwb(v, t=t, e=e):
                        v.wait(t_m, f"wbmm:{t}:{e}")
                        if e >= 2:
                            v.wait(v_m, f"ueu:{t}:{e - 2}:5")
                        elif t > 0:
                            v.wait(v_m, f"ueu:{t - 1}:{e + 6}:5")
                        v.op("tensor_copy", wb_sb2[e % 2], pf[4],
                             outs=[wb_sb2[e % 2]], ins=[], sem=v_m)
                        v.mark(v_m, f"wbev:{t}:{e}")
                    ctx.section("vector", ue_vwb)

                    for pj in range(3):
                        def ue_mm(tn, t=t, e=e, pj=pj):
                            ka, kb = f"e1a_{t}_{e}", f"e1b_{t}_{e}"
                            rwait(tn, ka)
                            rwait(tn, kb)
                            wa = rview(ka, 8, 768)
                            wbv = rview(kb, 8, 768)
                            for j in (2 * pj, 2 * pj + 1):
                                bY, bG = j % 3, 3 + j % 3
                                ps_wait(tn, bY)
                                for kc in range(8):
                                    tn.op("matmul", pf[bY],
                                          wa[:, 768 * kc + 128 * j:768 * kc + 128 * j + 128],
                                          xn_f[:, 512 * kc:512 * (kc + 1)],
                                          start=(kc == 0), stop=(kc == 7),
                                          sem=(t_m if kc == 7 else None))
                                tn.mark(t_m, f"uey:{t}:{e}:{j}")
                                ps_set_free(bY, v_m, f"ueu:{t}:{e}:{j}")
                                ps_wait(tn, bG)
                                for kc in range(8):
                                    tn.op("matmul", pf[bG],
                                          wbv[:, 768 * kc + 128 * j:768 * kc + 128 * j + 128],
                                          xn_f[:, 512 * kc:512 * (kc + 1)],
                                          start=(kc == 0), stop=(kc == 7),
                                          sem=(t_m if kc == 7 else None))
                                tn.mark(t_m, f"ueg:{t}:{e}:{j}")
                                ps_set_free(bG, c_m, f"uesg:{t}:{e}:{j}")
                            if pj == 2:
                                tn.mark(t_m, f"use:{ka}")
                                tn.mark(t_m, f"use:{kb}")
                        ctx.section("tensor", ue_mm)

                        def ue_c(c, t=t, e=e, pj=pj):
                            for j in (2 * pj, 2 * pj + 1):
                                c.wait(t_m, f"ueg:{t}:{e}:{j}")
                                rd = sg_reader.get(j % 2)
                                if rd is not None:
                                    c.wait(rd[0], rd[1])
                                c.op("activation", sg_sb[j % 2], pf[3 + j % 3],
                                     AF.Silu, sem=c_m)
                                c.mark(c_m, f"uesg:{t}:{e}:{j}")
                        ctx.section("scalar", ue_c)

                        def ue_v(v, t=t, e=e, pj=pj):
                            slot = u_slot[e % 3]
                            if pj == 0:
                                v.wait(v_m, f"wbev:{t}:{e}")
                                key = uslot_consumer.get(e % 3)
                                if key is not None:
                                    v.wait(t_m, key)
                            for j in (2 * pj, 2 * pj + 1):
                                v.wait(c_m, f"uesg:{t}:{e}:{j}")
                                v.op("tensor_tensor", sg_sb[j % 2], sg_sb[j % 2],
                                     wb_sb2[e % 2], ALU.mult,
                                     outs=[sg_sb[j % 2]], ins=[wb_sb2[e % 2]],
                                     sem=(v_m if j % 2 == 1 else None))
                            v.mark(v_m, f"uetsg:{t}:{e}:{pj}")
                            v.wait(v_m, f"uetsg:{t}:{e}:{pj}")
                            for j in (2 * pj, 2 * pj + 1):
                                v.wait(t_m, f"uey:{t}:{e}:{j}")
                                v.op("tensor_tensor", slot[:, 512 * j:512 * (j + 1)],
                                     pf[j % 3], sg_sb[j % 2], ALU.mult,
                                     outs=[slot[:, 512 * j:512 * (j + 1)]],
                                     ins=[sg_sb[j % 2]], sem=v_m,
                                     allow_hazard=True)
                                v.mark(v_m, f"ueu:{t}:{e}:{j}")
                                sg_reader[j % 2] = (v_m, f"ueu:{t}:{e}:{j}")
                        ctx.section("vector", ue_v)

                    ring_flush()

                for ph in range(2):
                    def zw_mm(tn, t=t, w=w, ph=ph):
                        e0, e1 = 2 * w, 2 * w + 1
                        k20, k21 = f"e2_{t}_{e0}", f"e2_{t}_{e1}"
                        rwait(tn, k20)
                        rwait(tn, k21)
                        w20 = rview(k20, 6, 1024)
                        w21 = rview(k21, 6, 1024)
                        for tc in (2 * ph, 2 * ph + 1):
                            banks = (2, 3) if tc % 2 == 0 else (4, 5)
                            for half in range(2):
                                b = banks[half]
                                ps_wait(tn, b)
                                for ei, (e, w2) in enumerate(((e0, w20), (e1, w21))):
                                    slot = u_slot[e % 3]
                                    for hc in range(6):
                                        if ph == 0 and tc == 2 * ph and half == 0:
                                            tn.wait(v_m, f"ueu:{t}:{e}:{hc}")
                                        tn.op("matmul", pf[b],
                                              slot[:, 512 * hc + 128 * tc:512 * hc + 128 * tc + 128],
                                              w2[:, 1024 * hc + 512 * half:1024 * hc + 512 * half + 512],
                                              start=(ei == 0 and hc == 0),
                                              stop=(ei == 1 and hc == 5),
                                              sem=(t_m if (ei == 1 and hc == 5) else None))
                                tn.mark(t_m, f"z:{t}:{w + 1}:{tc}:{half}")
                                ps_set_free(b, v_m, f"zev:{t}:{w + 1}:{tc}:{half}")
                        if ph == 1:
                            tn.mark(t_m, f"use:{k20}")
                            tn.mark(t_m, f"use:{k21}")
                            uslot_consumer[e0 % 3] = f"z:{t}:{w + 1}:{NTC - 1}:1"
                            uslot_consumer[e1 % 3] = f"z:{t}:{w + 1}:{NTC - 1}:1"
                    ctx.section("tensor", zw_mm)

                    def zw_ev(v, t=t, w=w, ph=ph):
                        for tc in (2 * ph, 2 * ph + 1):
                            banks = (2, 3) if tc % 2 == 0 else (4, 5)
                            for half in range(2):
                                v.wait(t_m, f"z:{t}:{w + 1}:{tc}:{half}")
                                v.wait(v_m, f"zev:{t}:{w}:{tc}:{half}")
                                dst = out_sb[:, D * tc + 512 * half:D * tc + 512 * half + 512]
                                v.op("tensor_tensor", dst, dst, pf[banks[half]],
                                     ALU.add, outs=[dst], ins=[], sem=v_m,
                                     allow_hazard=True)
                                v.mark(v_m, f"zev:{t}:{w + 1}:{tc}:{half}")
                        if ph == 1:
                            v.mark(v_m, f"zwin:{t}:{w + 1}")
                    ctx.section("vector", zw_ev)

                ring_flush()
                tile_done[0] = (v_m, f"zwin:{t}:{w + 1}")
                if STOP_AFTER is None and t + 1 < n_tiles:
                    if w == 3:
                        emit_ap_c1(t + 1, 0)
                        emit_ap_c1(t + 1, 1)

            if STOP_AFTER is not None and t + 1 < n_tiles:
                for tc0 in range(NTC):
                    early_ln(t + 1, tc0)
            def store_sync(s, t=t, tile_done=tile_done, out_sb=out_sb):
                full = STOP_AFTER is None
                for tc in range(NTC):
                    if full:
                        s.wait(v_m, f"zev:{t}:4:{tc}:1")
                    else:
                        s.wait(tile_done[0][0], tile_done[0][1])
                    if tc > 0 or t > 0:
                        pk = (t, tc - 1) if tc > 0 else (t - 1, NTC - 1)
                        s.wait(s_out, f"o{pk[0]}:{pk[1]}")
                    s.op("dma_start",
                         out=out_d[TILE * t + TC * tc:TILE * t + TC * (tc + 1), :],
                         in_=out_sb[:, D * tc:D * (tc + 1)],
                         sem=s_out, semk=16)
                    s.mark(s_out, f"o{t}:{tc}")
                if not full and t + 1 < n_tiles:
                    for tc in range(NTC):
                        s.wait(s_out, f"o{t}:{tc}")
                        if tc > 0:
                            s.wait(s_x, f"x{t + 1}:{tc - 1}")
                        s.op("dma_start",
                             out=outs2[(t + 1) % 2][:, D * tc:D * (tc + 1)],
                             in_=x_d[TILE * (t + 1) + TC * tc:TILE * (t + 1) + TC * (tc + 1), :],
                             sem=s_x, semk=16)
                        s.mark(s_x, f"x{t + 1}:{tc}")
            ctx.section("sync", store_sync)

            ring_flush()

        def fin(s):
            s.wait(s_out, f"o{n_tiles - 1}:{NTC - 1}")
        ctx.section("sync", fin)

        if STOP_AFTER is None:
            assert not ring_q, f"{len(ring_q)} ring items never loaded"

    for s in SEMS:
        cm = nc.semaphore(s.name)
        s.h = cm.__enter__()
        ctxs.append(cm)
    ctx1 = Ctx(nc, None, dry=True)
    emit(ctx1)
    blk = nc.Block()
    block = blk.__enter__()
    for engine, tape in ctx1.tape.items():
        if not tape:
            continue

        def replay(e, tape=tape):
            for ent in tape:
                if ent[0] == "op":
                    _, method, args, kw, sem, semk = ent
                    inst = getattr(e, method)(*args, **kw)
                    if sem is not None:
                        inst.then_inc(sem.h, semk)
                else:
                    _, sem, val = ent
                    e.wait_ge(sem.h, val)
        getattr(block, engine)(replay)
    blk.__exit__(None, None, None)
    nc._keepalive = ctxs
    return nc


_CACHE = {}


def _consts():
    ident = np.eye(128, dtype=ml_dtypes.bfloat16)
    mask4 = np.ascontiguousarray(np.triu(np.ones((128, 128), np.float32)).astype(ml_dtypes.bfloat16))
    selm = np.zeros((8, 1024), np.float32)
    for e in range(8):
        selm[e, 128 * e:128 * (e + 1)] = 1.0
    return ident, mask4, selm.astype(ml_dtypes.bfloat16)


def kernel(L0_attn_x, L0_attn_W1, L0_attn_W2, L0_ffn_gate_w, L0_ffn_W1,
           L0_ffn_W2, L0_ffn_Ws1, L0_ffn_Ws2):
    x = np.ascontiguousarray(np.asarray(L0_attn_x, dtype=np.float32))
    B, T, _ = x.shape
    n_seq = B // N_CORES
    key = (n_seq, T)
    if key not in _CACHE:
        _CACHE[key] = build_program(n_seq, T)
    nc = _CACHE[key]

    W1 = np.asarray(L0_attn_W1, np.float32)
    wqkT = np.ascontiguousarray(W1[0:2048, :].T)
    wvT = np.ascontiguousarray(W1[2048:3072, :].T)
    woT = np.ascontiguousarray(np.asarray(L0_attn_W2, np.float32).T)
    wgT = np.ascontiguousarray(np.asarray(L0_ffn_gate_w, np.float32).T)
    we1T = np.ascontiguousarray(np.asarray(L0_ffn_W1, np.float32).transpose(0, 2, 1))
    we2T = np.ascontiguousarray(np.asarray(L0_ffn_W2, np.float32).transpose(0, 2, 1))
    ws1T = np.ascontiguousarray(np.asarray(L0_ffn_Ws1, np.float32).T)
    ws2T = np.ascontiguousarray(np.asarray(L0_ffn_Ws2, np.float32).T)
    ident, mask4, selm = _consts()

    in_maps = []
    for c in range(N_CORES):
        xc = np.ascontiguousarray(x[c * n_seq:(c + 1) * n_seq].reshape(n_seq * T, D))
        in_maps.append(dict(
            x=xc, w_qkT=wqkT, w_vT=wvT, w_oT=woT, w_gT=wgT,
            w_e1T=we1T, w_e2T=we2T, w_s1T=ws1T, w_s2T=ws2T,
            c_ident=ident, c_mask4=mask4, c_sel=selm,
        ))
    res = run_bass_kernel_spmd(nc, in_maps, core_ids=list(range(N_CORES)))
    outs = [np.asarray(res.results[c]["out"]).reshape(n_seq, T, D)
            for c in range(N_CORES)]
    out = np.ascontiguousarray(np.concatenate(outs, axis=0))
    if out.dtype != np.float64:
        out = out.astype(np.float32, copy=False)
    return out


# version 21
# speedup vs baseline: 1.0231x; 1.0231x over previous
import numpy as np
import ml_dtypes

import concourse.bass as bass
import concourse.mybir as mybir
from concourse.bass_utils import run_bass_kernel_spmd

F32 = mybir.dt.float32
BF16 = mybir.dt.bfloat16
AF = mybir.ActivationFunctionType
ALU = mybir.AluOpType
AX = mybir.AxisListType

D = 1024
NH = 16
E = 8
TC = 128
TILE = 512
NTC = TILE // TC
EPS = 1e-5
NEG_BIG = -1e30
SLOT = 6144
NSLOT = 4
N_CORES = 8
STOP_AFTER = None
_PHASES = ["ln", "gate", "qkv", "att1c", "att2c", "att3c", "att", "ush", "z0",
           "uz0", "uz1", "uz2", "uz3"]
_SUBATT = ["attt1", "attt2", "atts1", "attt3", "atts2", "attt4"]
def _sub_ok(name):
    if STOP_AFTER in (None, ) or STOP_AFTER not in _SUBATT:
        return True
    return _SUBATT.index(name) <= _SUBATT.index(STOP_AFTER)


class Sem:
    def __init__(self, name):
        self.name = name
        self.n = 0
        self.h = None
        self.marks = {}


class _Stub:
    def then_inc(self, h, k):
        return self


class Eng:
    def __init__(self, ctx, name, real):
        self.ctx = ctx
        self.name = name
        self.real = real

    def _rng(self, ap):
        try:
            off = ap.offset if isinstance(ap.offset, int) else 0
            return (ap.tensor.name, off, off + max(ap.free_size(), 1))
        except Exception:
            return None

    def _guard(self, outs, ins, allow_hazard):
        if self.name != "vector":
            return
        lo = self.ctx.v_last_out
        if lo is not None and not allow_hazard:
            for ap in ins:
                r = self._rng(ap)
                if r and r[0] == lo[0] and r[1] < lo[2] and lo[1] < r[2]:
                    raise RuntimeError(
                        f"DVE RAW hazard: read {r} right after write {lo}")
        self.ctx.v_last_out = self._rng(outs[0]) if outs else None

    def op(self, method, *args, outs=(), ins=(), sem=None, semk=1,
           allow_hazard=False, **kw):
        self._guard(list(outs), list(ins), allow_hazard)
        if sem is not None:
            sem.n += semk
        self.ctx.tape[self.name].append(("op", method, args, kw, sem, semk))
        return _Stub()

    def mark(self, sem, key):
        if True:
            assert key not in sem.marks, f"dup mark {sem.name}:{key}"
            sem.marks[key] = sem.n
        self.ctx.marked.add((sem.name, key))

    def wait(self, sem, key=None, val=None):
        if self.name == "vector":
            self.ctx.v_last_out = None
        if val is None:
            assert (sem.name, key) in self.ctx.marked, \
                f"wait on unrecorded mark {sem.name}:{key}"
            val = sem.marks[key]
        assert val <= sem.n, f"future wait {sem.name} {val}>{sem.n}"
        self.ctx.tape[self.name].append(("wait", sem, val))


class Ctx:

    def __init__(self, nc, block, dry):
        self.nc = nc
        self.block = block
        self.dry = dry
        self.marked = set()
        self.v_last_out = None
        self.tape = {k: [] for k in
                     ("tensor", "vector", "scalar", "gpsimd", "sync")}

    def section(self, engine, body):
        body(Eng(self, engine, None))


def build_program(n_seq, T):
    assert T % TILE == 0
    NT = n_seq * T
    n_tiles = NT // TILE
    tps = T // TILE

    nc = bass.Bass("TRN2", target_bir_lowering=False, debug=False)

    x_d = nc.dram_tensor("x", [NT, D], F32, kind="ExternalInput").ap()
    out_d = nc.dram_tensor("out", [NT, D], F32, kind="ExternalOutput").ap()
    wqk_d = nc.dram_tensor("w_qkT", [D, 2048], F32, kind="ExternalInput").ap()
    wv_d = nc.dram_tensor("w_vT", [D, D], F32, kind="ExternalInput").ap()
    wo_d = nc.dram_tensor("w_oT", [D, D], F32, kind="ExternalInput").ap()
    wg_d = nc.dram_tensor("w_gT", [D, E], F32, kind="ExternalInput").ap()
    we1_d = nc.dram_tensor("w_e1T", [E, D, 1536], F32, kind="ExternalInput").ap()
    we2_d = nc.dram_tensor("w_e2T", [E, 768, D], F32, kind="ExternalInput").ap()
    ws1_d = nc.dram_tensor("w_s1T", [D, 3072], F32, kind="ExternalInput").ap()
    ws2_d = nc.dram_tensor("w_s2T", [1536, D], F32, kind="ExternalInput").ap()
    ident_d = nc.dram_tensor("c_ident", [128, 128], BF16, kind="ExternalInput").ap()
    mask_d = nc.dram_tensor("c_mask4", [128, 128], BF16, kind="ExternalInput").ap()
    sel_d = nc.dram_tensor("c_sel", [8, 1024], BF16, kind="ExternalInput").ap()

    wqk_s = nc.dram_tensor("s_qk", [D, 2048], BF16, kind="Internal").ap()
    wv_s = nc.dram_tensor("s_v", [D, D], BF16, kind="Internal").ap()
    we1_s = nc.dram_tensor("s_e1", [E, D, 1536], BF16, kind="Internal").ap()
    we2_s = nc.dram_tensor("s_e2", [E, 768, D], BF16, kind="Internal").ap()
    ws1_s = nc.dram_tensor("s_s1", [D, 3072], BF16, kind="Internal").ap()

    ctxs = []

    def sbuf(name, shape, dt):
        cm = nc.sbuf_tensor(name, shape, dt)
        t = cm.__enter__()
        ctxs.append(cm)
        return t

    def psumt(name, shape, dt):
        cm = nc.psum_tensor(name, shape, dt)
        t = cm.__enter__()
        ctxs.append(cm)
        return t

    ident = sbuf("ident", [128, 128], BF16)[:, :]
    mask4 = sbuf("mask4", [128, 128], BF16)[:, :]
    sel = sbuf("sel", [8, 1024], BF16)[:, :]
    wg_sb = sbuf("wg_sb", [128, 8 * E], BF16)[:, :]
    wo_sb = sbuf("wo_sb", [128, 8 * 1024], BF16)[:, :]
    ws2_sb = sbuf("ws2_sb", [128, 12 * 1024], BF16)[:, :]
    ring = sbuf("ring", [128, NSLOT * SLOT], BF16)[:, :]
    outs2 = [sbuf(f"out_sb{i}", [128, NTC * D], F32)[:, :] for i in range(2)]
    xn_f = sbuf("xn_f", [128, 8 * TILE], BF16)[:, :]
    qk_ar = sbuf("qk_ar", [128, 8192], BF16)[:, :]
    u_ar = sbuf("u_ar", [128, 9216], BF16)[:, :]
    y_fT = sbuf("y_fT", [128, 8 * TILE], BF16)[:, :]
    v_sb = [sbuf(f"v_sb{i}", [128, NH * 65], BF16)[:, :] for i in range(4)]
    sz_m = sbuf("sz_m", [128, 520], F32)[:, :]
    sz_bf = sbuf("sz_bf", [128, 520], BF16)[:, :]
    k_tm = sbuf("k_tm", [128, 1024], BF16)[:, :]
    xn2_tmb = k_tm
    y_tm = sbuf("y_tm", [128, 1024], BF16)[:, :]
    xn_tmb = y_tm
    w8 = sbuf("w8", [128, NTC * E], F32)[:, :]
    w8b = sbuf("w8b", [128, NTC * E], BF16)[:, :]
    w8t = sbuf("w8t", [8, TILE], BF16)[:, :]
    a_m = [sbuf(f"a_m{i}", [128, 512], BF16)[:, :] for i in range(4)]
    elu1 = sbuf("elu1", [128, 1024], F32)[:, :]
    sg_sb = [sbuf(f"sg{i}", [128, 512], BF16)[:, :] for i in range(2)]
    xn_tm = sbuf("xn_tm", [128, D], F32)[:, :]
    x_tc = sbuf("x_tc", [128, D], F32)[:, :]
    bn12 = sbuf("bn12", [128, 12], F32)[:, :]
    mv = sbuf("mv", [128, 4 * NTC], F32)[:, :]
    rr = sbuf("rr", [128, 2 * NTC], F32)[:, :]
    nm = sbuf("nm", [128, 2 * NTC], F32)[:, :]
    neg = sbuf("neg", [128, 2], F32)[:, :]
    lnt = sbuf("lnt", [128, 4], F32)[:, :]
    gt = sbuf("gt", [128, 8 * NTC], F32)[:, :]
    gtp = sbuf("gtp", [128, 8 * NTC], F32)[:, :]
    gsc = sbuf("gsc", [128, 24], F32)[:, :]
    gl_sb = sbuf("gl_sb", [128, NTC * E], F32)[:, :]
    d4 = sbuf("d4", [128, 8], F32)[:, :]
    spc = sbuf("spc", [128, 8], F32)[:, :]
    cst = sbuf("cst", [128, 4], F32)[:, :]
    r4 = sbuf("r4", [128, 8], F32)[:, :]

    q_f = qk_ar[:, 0:4096]
    k_f = qk_ar[:, 4096:8192]
    us_sb = qk_ar[:, 0:6144]
    wb_sb2 = [qk_ar[:, 6144:6656], qk_ar[:, 6656:7168]]
    xn2_f = u_ar[:, 0:4096]
    u_slot = [u_ar[:, 3072 * i:3072 * (i + 1)] for i in range(3)]

    pf = [psumt(f"pf{i}", [128, 512], F32)[:, :] for i in range(6)]
    pt = [psumt(f"pt{i}", [128, 1024], BF16)[:, :] for i in range(2)]

    SEMS = []

    def S(name):
        s = Sem(name)
        SEMS.append(s)
        return s
    s_cast = S("s_cast")
    s_cast1 = S("s_cast1")
    s_res = S("s_res")
    s_cres = S("s_cres")
    s_x = S("s_x")
    s_xe = S("s_xe")
    s_out = S("s_out")
    s_ring = [S(f"s_ring{i}") for i in range(NSLOT)]
    s_ring0 = [S(f"s_ring0_{i}") for i in range(NSLOT)]
    s_castw = [S(f"s_castw{i}") for i in range(4)]
    t_m = S("t_m")
    v_m = S("v_m")
    c_m = S("c_m")

    def ring_items(t):
        items = [
            (f"qhA{t}", wqk_s[:, 0:512].rearrange("(kc p) m -> p kc m", p=128), 8, 512,
             (wqk_d[:, 0:512].rearrange("(kc p) m -> p kc m", p=128), None) if t == 0 else None),
            (f"qhB{t}", wqk_s[:, 512:1024].rearrange("(kc p) m -> p kc m", p=128), 8, 512,
             (wqk_d[:, 512:1024].rearrange("(kc p) m -> p kc m", p=128), None) if t == 0 else None),
            (f"khA{t}", wqk_s[:, 1024:1536].rearrange("(kc p) m -> p kc m", p=128), 8, 512,
             (wqk_d[:, 1024:1536].rearrange("(kc p) m -> p kc m", p=128), None) if t == 0 else None),
            (f"khB{t}", wqk_s[:, 1536:2048].rearrange("(kc p) m -> p kc m", p=128), 8, 512,
             (wqk_d[:, 1536:2048].rearrange("(kc p) m -> p kc m", p=128), None) if t == 0 else None),
            (f"wvA{t}", wv_s[:, 0:512].rearrange("(kc p) m -> p kc m", p=128), 8, 512,
             (wv_d[:, 0:512].rearrange("(kc p) m -> p kc m", p=128), None) if t == 0 else None),
            (f"wvB{t}", wv_s[:, 512:1024].rearrange("(kc p) m -> p kc m", p=128), 8, 512,
             (wv_d[:, 512:1024].rearrange("(kc p) m -> p kc m", p=128), None) if t == 0 else None),
        ]
        for qq in range(4):
            items.append((f"s1_{t}_{qq}",
                          ws1_s[:, 768 * qq:768 * (qq + 1)].rearrange("(kc p) m -> p kc m", p=128),
                          8, 768,
                          (ws1_d[:, 768 * qq:768 * (qq + 1)].rearrange("(kc p) m -> p kc m", p=128),
                           None) if t == 0 else None))
        for w in range(4):
            for ee in range(2):
                e = 2 * w + ee
                src32 = (we1_d[e, :, 0:768].rearrange("(kc p) m -> p kc m", p=128),
                         ("we1", e)) if t == 0 else None
                items.append((f"e1a_{t}_{e}",
                              we1_s[e, :, 0:768].rearrange("(kc p) m -> p kc m", p=128), 8, 768, src32))
                src32 = (we1_d[e, :, 768:1536].rearrange("(kc p) m -> p kc m", p=128),
                         None) if t == 0 else None
                items.append((f"e1b_{t}_{e}",
                              we1_s[e, :, 768:1536].rearrange("(kc p) m -> p kc m", p=128), 8, 768, src32))
            for ee in range(2):
                e = 2 * w + ee
                src32 = (we2_d[e].rearrange("(kc p) m -> p kc m", p=128),
                         ("we2", e)) if t == 0 else None
                items.append((f"e2_{t}_{e}",
                              we2_s[e].rearrange("(kc p) m -> p kc m", p=128), 6, 1024, src32))
        return items

    FLUSH_PLAN = []

    def emit(ctx):
        if ctx.dry:
            FLUSH_PLAN.clear()
        flush_i = [0]
        ring_q = []
        slot_of = {}
        slot_prev = [None] * NSLOT
        loads_in_slot = [0] * NSLOT
        loads0_in_slot = [0] * NSLOT
        wait_cnt = {}
        cast_done = set()
        cast_marks = {}
        next_slot = [0]
        for t in range(n_tiles):
            ring_q.extend(ring_items(t))

        def ring_flush():
            todo = []
            if not ctx.dry:
                quota = FLUSH_PLAN[flush_i[0]]
                flush_i[0] += 1
            while ring_q:
                if ctx.dry:
                    key, src, kc, m, src32 = ring_q[0]
                    slot = next_slot[0] % NSLOT
                    prev = slot_prev[slot]
                    if prev is not None and \
                            ("t_m", f"use:{prev}") not in ctx.marked:
                        break
                else:
                    if len(todo) >= quota:
                        break
                    key, src, kc, m, src32 = ring_q[0]
                    slot = next_slot[0] % NSLOT
                    prev = slot_prev[slot]
                ring_q.pop(0)
                next_slot[0] += 1
                slot_of[key] = slot
                slot_prev[slot] = key
                if src32 is not None:
                    loads0_in_slot[slot] += 1
                    wait_cnt[key] = (s_ring0[slot], 16 * loads0_in_slot[slot])
                else:
                    loads_in_slot[slot] += 1
                    wait_cnt[key] = (s_ring[slot], 16 * loads_in_slot[slot])
                todo.append((key, src, kc, m, slot, prev, src32))
            if ctx.dry:
                FLUSH_PLAN.append(len(todo))
            if not todo:
                return
            sw = [x for x in todo if x[6] is not None]
            hw = [x for x in todo if x[6] is None]

            if sw:
                def gbody(g):
                    for key, src, kc, m, slot, prev, src32 in sw:
                        if prev is not None:
                            g.wait(t_m, f"use:{prev}")
                        g.op("dma_start",
                             out=ring[:, SLOT * slot:SLOT * slot + kc * m]
                             .rearrange("p (kc m) -> p kc m", kc=kc),
                             in_=src32[0], sem=s_ring0[slot], semk=16)
                        if src32[1] is not None:
                            kind, e = src32[1]
                            w = e // 2
                            if kind == "we1":
                                g.op("dma_start", out=we1_s[e], in_=we1_d[e],
                                     sem=s_castw[w], semk=16)
                            else:
                                g.op("dma_start", out=we2_s[e], in_=we2_d[e],
                                     sem=s_castw[w], semk=16)
                            cast_done.add((kind, e))
                            if sum(1 for k2, e2 in cast_done
                                   if e2 // 2 == w) == 4:
                                g.mark(s_castw[w], "full")
                ctx.section("gpsimd", gbody)

            if hw:
                def body(s):
                    for key, src, kc, m, slot, prev, src32 in hw:
                        if key.startswith(("qh", "kh")):
                            s.wait(s_cast1, "qk")
                        elif key.startswith("wv"):
                            s.wait(s_cast1, "v")
                        elif key.startswith("s1"):
                            s.wait(s_cast1, "front")
                        else:
                            e = int(key.split("_")[-1])
                            s.wait(s_castw[e // 2], "full")
                        if prev is not None:
                            s.wait(t_m, f"use:{prev}")
                        s.op("dma_start",
                             out=ring[:, SLOT * slot:SLOT * slot + kc * m]
                             .rearrange("p (kc m) -> p kc m", kc=kc),
                             in_=src, sem=s_ring[slot], semk=16)
                ctx.section("sync", body)

        def rview(key, kc, m):
            return ring[:, SLOT * slot_of[key]:SLOT * slot_of[key] + kc * m]

        def rwait(eng, key):
            sem, val = wait_cnt[key]
            eng.wait(sem, val=val)

        ps_free = {}
        uslot_consumer = {}
        sg_reader = {}

        def ps_wait(eng, idx):
            if idx in ps_free:
                sem, key = ps_free[idx]
                eng.wait(sem, key)

        def ps_set_free(idx, sem, key):
            ps_free[idx] = (sem, key)

        def pro_gp(g):
            g.op("dma_start", out=wg_sb.rearrange("p (kc m) -> p kc m", kc=8),
                 in_=wg_d.rearrange("(kc p) m -> p kc m", p=128), sem=s_res, semk=16)
            g.op("dma_start", out=wo_sb.rearrange("p (kc m) -> p kc m", kc=8),
                 in_=wo_d.rearrange("(kc p) m -> p kc m", p=128), sem=s_res, semk=16)
            g.op("dma_start", out=ws2_sb.rearrange("p (kc m) -> p kc m", kc=12),
                 in_=ws2_d.rearrange("(kc p) m -> p kc m", p=128), sem=s_res, semk=16)
            g.mark(s_res, "all")
            g.op("dma_start", out=wqk_s, in_=wqk_d, sem=s_cast1, semk=16)
            g.mark(s_cast1, "qk")
            g.wait(s_cast1, "qk")
            g.op("dma_start", out=wv_s, in_=wv_d, sem=s_cast1, semk=16)
            g.mark(s_cast1, "v")
            g.wait(s_cast1, "v")
            g.op("dma_start", out=ws1_s, in_=ws1_d, sem=s_cast1, semk=16)
            g.mark(s_cast1, "front")
        ctx.section("gpsimd", pro_gp)

        def pro_sync(s):
            s.op("dma_start", out=ident, in_=ident_d, sem=s_cres, semk=16)
            s.op("dma_start", out=mask4, in_=mask_d, sem=s_cres, semk=16)
            s.op("dma_start", out=sel, in_=sel_d, sem=s_cres, semk=16)
            s.op("dma_start",
                 out=outs2[0].rearrange("p (tc d) -> p tc d", tc=NTC),
                 in_=x_d[0:TILE, :].rearrange("(tc p) d -> p tc d", p=128),
                 sem=s_x, semk=16)
            s.mark(s_cres, "all")
            for tc in range(NTC):
                s.mark(s_x, f"x0:{tc}")
        ctx.section("sync", pro_sync)

        def pro_vec(v):
            v.op("memset", spc, 0.0, sem=None)
            v.op("memset", cst[:, 0:1], EPS, sem=None)
            v.op("memset", cst[:, 1:2], -0.5, sem=None)
            v.op("memset", cst[:, 2:3], -1.0, sem=None)
            v.op("memset", cst[:, 3:4], 0.5, sem=v_m)
            v.mark(v_m, "cst")
            for i in range(4):
                v.op("memset",
                     v_sb[i].rearrange("p (h c) -> p h c", c=65)[:, :, 64:65],
                     1.0, sem=v_m)
            v.mark(v_m, "vones")
        ctx.section("vector", pro_vec)

        ring_flush()

        def early_ln(t, tc):
            def e_v1(v, t=t, tc=tc):
                xsrc = outs2[t % 2][:, D * tc:D * (tc + 1)]
                v.wait(s_x, f"x{t}:{tc}")
                if not (t == 0 and tc == 0):
                    pk = (t, tc - 1) if tc > 0 else (t - 1, NTC - 1)
                    v.wait(v_m, f"eln1s:{pk[0]}:{pk[1]}")
                v.op("bn_stats", bn12[:, 0:6], xsrc[:, 0:512],
                     outs=[bn12[:, 0:6]], ins=[])
                v.op("bn_stats", bn12[:, 6:12], xsrc[:, 512:1024],
                     outs=[bn12[:, 6:12]], ins=[], sem=v_m)
                v.mark(v_m, f"eln1b:{t}:{tc}")
                v.wait(v_m, f"eln1b:{t}:{tc}")
                v.op("bn_aggr", mv[:, 2 * tc:2 * tc + 2], bn12,
                     outs=[mv[:, 2 * tc:2 * tc + 2]], ins=[], sem=v_m,
                     allow_hazard=True)
                v.mark(v_m, f"eln1s:{t}:{tc}")
            ctx.section("vector", e_v1)

            def e_c1(c, t=t, tc=tc):
                c.wait(v_m, "cst")
                c.wait(v_m, f"eln1s:{t}:{tc}")
                c.op("mul", neg[:, 0:1], mv[:, 2 * tc:2 * tc + 1], cst[:, 2:3],
                     sem=c_m)
                c.op("activation", lnt[:, 0:1], mv[:, 2 * tc + 1:2 * tc + 2],
                     AF.Ln, bias=cst[:, 0:1], sem=c_m)
                c.mark(c_m, f"elnv1:{t}:{tc}")
                c.wait(c_m, f"elnv1:{t}:{tc}")
                c.op("activation", rr[:, tc:tc + 1], lnt[:, 0:1], AF.Exp,
                     scale=cst[:, 1:2], sem=c_m)
                c.mark(c_m, f"elnr1:{t}:{tc}")
                c.wait(c_m, f"elnr1:{t}:{tc}")
                c.op("mul", nm[:, tc:tc + 1], neg[:, 0:1], rr[:, tc:tc + 1],
                     sem=c_m)
                c.op("mul", lnt[:, 1:2], mv[:, 2 * tc + 1:2 * tc + 2],
                     rr[:, tc:tc + 1], sem=c_m)
                c.mark(c_m, f"elnn1:{t}:{tc}")
                c.wait(c_m, f"elnn1:{t}:{tc}")
                c.op("mul", lnt[:, 2:3], lnt[:, 1:2], rr[:, tc:tc + 1],
                     sem=c_m)
                c.mark(c_m, f"elnu:{t}:{tc}")
                c.wait(c_m, f"elnu:{t}:{tc}")
                c.op("activation", lnt[:, 1:2], lnt[:, 2:3],
                     AF.Ln, bias=cst[:, 0:1], sem=c_m)
                c.mark(c_m, f"elnv2:{t}:{tc}")
                c.wait(c_m, f"elnv2:{t}:{tc}")
                c.op("activation", lnt[:, 2:3], lnt[:, 1:2], AF.Exp,
                     scale=cst[:, 1:2], sem=c_m)
                c.mark(c_m, f"elnr2:{t}:{tc}")
                c.wait(c_m, f"elnr2:{t}:{tc}")
                c.op("mul", rr[:, NTC + tc:NTC + tc + 1], rr[:, tc:tc + 1],
                     lnt[:, 2:3], sem=c_m)
                c.op("mul", nm[:, NTC + tc:NTC + tc + 1], nm[:, tc:tc + 1],
                     lnt[:, 2:3], sem=c_m)
                c.mark(c_m, f"elnc:{t}:{tc}")
            ctx.section("scalar", e_c1)

        def ln_stage(tc):
            if tc % 2 == 0:
                return ((xn_tmb[:, 0:512], xn_tmb[:, 512:1024]),
                        (xn2_tmb[:, 0:512], xn2_tmb[:, 512:1024]))
            return ((a_m[0], a_m[1]), (a_m[2], a_m[3]))

        def emit_ap_c1(t, tc):
            def ap_c1(c, t=t, tc=tc):
                out_b = outs2[t % 2]
                (x0, x1), (y0, y1) = ln_stage(tc)
                c.wait(s_x, f"x{t}:{tc}")
                c.wait(c_m, f"elnc:{t}:{tc}")
                if t > 0 or tc > 1:
                    pk = (t, tc - 2) if tc > 1 else (t - 1, NTC - 2 + tc)
                    c.wait(t_m, f"xtrd:{pk[0]}:{pk[1]}")
                    c.wait(t_m, f"x2trd:{pk[0]}:{pk[1]}")
                    if tc <= 1:
                        c.wait(t_m, f"ytr:{t-1}:{NTC-1}")
                        c.wait(t_m, f"szmm:{t-1}:{NTC-1}")
                        if tc == 1:
                            c.wait(t_m, f"omm:{t-1}:{NTC-1}:3")
                for half, dst in ((0, x0), (1, x1)):
                    c.op("activation", dst,
                         out_b[:, D * tc + 512 * half:D * tc + 512 * half + 512],
                         AF.Identity, bias=nm[:, tc:tc + 1],
                         scale=rr[:, tc:tc + 1],
                         sem=(c_m if half == 1 else None))
                c.mark(c_m, f"xnbf:{t}:{tc}")
                for half, dst in ((0, y0), (1, y1)):
                    c.op("activation", dst,
                         out_b[:, D * tc + 512 * half:D * tc + 512 * half + 512],
                         AF.Identity, bias=nm[:, NTC + tc:NTC + tc + 1],
                         scale=rr[:, NTC + tc:NTC + tc + 1],
                         sem=(c_m if half == 1 else None))
                c.mark(c_m, f"xn2bf:{t}:{tc}")
            ctx.section("scalar", ap_c1)

        def go(name):
            if STOP_AFTER is None:
                return True
            sa = "att" if STOP_AFTER in _SUBATT else STOP_AFTER
            return _PHASES.index(name) <= _PHASES.index(sa)

        for tc0 in range(NTC):
            early_ln(0, tc0)

        for t in range(n_tiles):
            tis = t % tps
            tile_done = [None]
            out_sb = outs2[t % 2]

            for tc in range(NTC) if go("ln") else []:
                if not (t > 0 and tc <= 1 and STOP_AFTER is None):
                    emit_ap_c1(t, tc)

                def ap_tr(tn, t=t, tc=tc):
                    tn.wait(s_cres, "all")
                    (x0, x1), (y0, y1) = ln_stage(tc)
                    for half, (bufname, srcs, mk) in enumerate((
                            ("x", (x0, x1), f"xnbf:{t}:{tc}"),
                            ("x2", (y0, y1), f"xn2bf:{t}:{tc}"))):
                        tn.wait(c_m, mk)
                        ps_wait(tn, 6 + half)
                        for kc in range(8):
                            src = srcs[kc // 4]
                            tn.op("matmul", pt[half][:, 128 * kc:128 * (kc + 1)],
                                  src[:, 128 * (kc % 4):128 * (kc % 4) + 128], ident,
                                  start=True, stop=True, is_transpose=True,
                                  sem=(t_m if kc == 7 else None))
                        tn.mark(t_m, f"{bufname}tr:{t}:{tc}")
                        ps_set_free(6 + half, v_m, f"{bufname}trev:{t}:{tc}")
                        tn.mark(t_m, f"{bufname}trd:{t}:{tc}")
                ctx.section("tensor", ap_tr)

                def ap_trev(v, t=t, tc=tc):
                    for half, bufname in enumerate(("x", "x2")):
                        dstt = xn_f if bufname == "x" else xn2_f
                        v.wait(t_m, f"{bufname}tr:{t}:{tc}")
                        dv = dstt.rearrange("p (kc m) -> p kc m", kc=8) \
                            [:, 0:8, 128 * tc:128 * tc + 128]
                        v.op("tensor_copy", dv,
                             pt[half].rearrange("p (j m) -> p j m", j=8),
                             outs=[dstt], ins=[], sem=v_m)
                        v.mark(v_m, f"{bufname}trev:{t}:{tc}")
                ctx.section("vector", ap_trev)

            def gate_mm(tn, t=t):
                tn.wait(s_res, "all")
                ps_wait(tn, 4)
                for tc in range(NTC):
                    tn.wait(v_m, f"xtrev:{t}:{tc}")
                    for kc in range(8):
                        tn.op("matmul", pf[4][:, 8 * tc:8 * tc + 8],
                              xn_f[:, 512 * kc + 128 * tc:512 * kc + 128 * tc + 128],
                              wg_sb[:, 8 * kc:8 * kc + 8],
                              start=(kc == 0), stop=(kc == 7),
                              sem=(t_m if kc == 7 else None))
                    tn.mark(t_m, f"gmm:{t}:{tc}")
                ps_set_free(4, v_m, f"glev:{t}")
            if go("gate"):
                ctx.section("tensor", gate_mm)

            def gate_v1(v, t=t):
                v.wait(t_m, f"gmm:{t}:{NTC - 1}")
                v.op("tensor_copy", gl_sb, pf[4][:, 0:NTC * E],
                     outs=[gl_sb], ins=[], sem=v_m)
                v.mark(v_m, f"glev:{t}")
                v.wait(v_m, f"glev:{t}")
                for tc in range(NTC):
                    v.op("reduce_max", gsc[:, tc:tc + 1], gl_sb[:, 8 * tc:8 * tc + 8],
                         axis=AX.X, outs=[gsc[:, tc:tc + 1]], ins=[],
                         sem=(v_m if tc == NTC - 1 else None), allow_hazard=True)
                v.mark(v_m, f"grm:{t}")
                v.wait(v_m, f"grm:{t}")
                for tc in range(NTC):
                    v.op("tensor_scalar", gt[:, 8 * tc:8 * tc + 8],
                         gl_sb[:, 8 * tc:8 * tc + 8], gsc[:, tc:tc + 1], None,
                         ALU.subtract, outs=[gt[:, 8 * tc:8 * tc + 8]],
                         ins=[gsc[:, tc:tc + 1]], sem=v_m, allow_hazard=True)
                    v.mark(v_m, f"gsh:{t}:{tc}")
            if go("gate"):
                ctx.section("vector", gate_v1)

            def gate_c1(c, t=t):
                for tc in range(NTC):
                    c.wait(v_m, f"gsh:{t}:{tc}")
                    c.op("activation", gtp[:, 8 * tc:8 * tc + 8], gt[:, 8 * tc:8 * tc + 8],
                         AF.Exp, accum_out=gsc[:, 4 + tc:5 + tc], sem=c_m)
                    c.mark(c_m, f"gexp:{t}:{tc}")
            if go("gate"):
                ctx.section("scalar", gate_c1)

            def gate_v2(v, t=t):
                for tc in range(NTC):
                    v.wait(c_m, f"gexp:{t}:{tc}")
                    v.op("reciprocal", gsc[:, 16 + tc:17 + tc], gsc[:, 4 + tc:5 + tc],
                         outs=[gsc[:, 16 + tc:17 + tc]], ins=[gsc[:, 4 + tc:5 + tc]],
                         sem=v_m, allow_hazard=(tc > 0))
                    v.mark(v_m, f"grp:{t}:{tc}")
            if go("gate"):
                ctx.section("vector", gate_v2)

            def gate_c2(c, t=t):
                for tc in range(NTC):
                    c.wait(v_m, f"grp:{t}:{tc}")
                    c.wait(c_m, f"gexp:{t}:{tc}")
                    c.op("activation", w8[:, 8 * tc:8 * tc + 8], gtp[:, 8 * tc:8 * tc + 8],
                         AF.Identity, scale=gsc[:, 16 + tc:17 + tc], sem=c_m)
                    c.mark(c_m, f"gpr:{t}:{tc}")
            if go("gate"):
                ctx.section("scalar", gate_c2)

            def gate_v3(v, t=t):
                def loop(tag, fn, n=NTC):
                    for tc in range(n):
                        fn(tc, tc == n - 1)
                    v.mark(v_m, f"{tag}:{t}")
                    v.wait(v_m, f"{tag}:{t}")
                for tc in range(NTC):
                    v.wait(c_m, f"gpr:{t}:{tc}")
                loop("g_m1", lambda tc, last: v.op(
                    "reduce_max", gsc[:, 8 + tc:9 + tc], w8[:, 8 * tc:8 * tc + 8],
                    axis=AX.X, outs=[gsc[:, 8 + tc:9 + tc]], ins=[],
                    sem=(v_m if last else None)))
                loop("g_eq", lambda tc, last: v.op(
                    "tensor_scalar", gt[:, 8 * tc:8 * tc + 8],
                    w8[:, 8 * tc:8 * tc + 8], gsc[:, 8 + tc:9 + tc], None,
                    ALU.is_ge, outs=[gt[:, 8 * tc:8 * tc + 8]],
                    ins=[gsc[:, 8 + tc:9 + tc]],
                    sem=(v_m if last else None), allow_hazard=True))
                loop("g_stt", lambda tc, last: v.op(
                    "scalar_tensor_tensor", gt[:, 8 * tc:8 * tc + 8],
                    gt[:, 8 * tc:8 * tc + 8], NEG_BIG, w8[:, 8 * tc:8 * tc + 8],
                    ALU.mult, ALU.add, outs=[gt[:, 8 * tc:8 * tc + 8]], ins=[],
                    sem=(v_m if last else None), allow_hazard=True))
                loop("g_m2", lambda tc, last: v.op(
                    "reduce_max", gsc[:, 12 + tc:13 + tc], gt[:, 8 * tc:8 * tc + 8],
                    axis=AX.X, outs=[gsc[:, 12 + tc:13 + tc]], ins=[],
                    sem=(v_m if last else None), allow_hazard=True))
                loop("g_msk", lambda tc, last: v.op(
                    "tensor_scalar", gt[:, 8 * tc:8 * tc + 8],
                    w8[:, 8 * tc:8 * tc + 8], gsc[:, 12 + tc:13 + tc], None,
                    ALU.is_ge, outs=[gt[:, 8 * tc:8 * tc + 8]],
                    ins=[gsc[:, 12 + tc:13 + tc]],
                    sem=(v_m if last else None), allow_hazard=True))
                loop("g_w8", lambda tc, last: v.op(
                    "tensor_mul", w8[:, 8 * tc:8 * tc + 8],
                    gt[:, 8 * tc:8 * tc + 8], w8[:, 8 * tc:8 * tc + 8],
                    outs=[w8[:, 8 * tc:8 * tc + 8]],
                    ins=[gt[:, 8 * tc:8 * tc + 8]],
                    sem=(v_m if last else None), allow_hazard=True))
                for tc in range(NTC):
                    v.op("tensor_copy", w8b[:, 8 * tc:8 * tc + 8],
                         w8[:, 8 * tc:8 * tc + 8],
                         outs=[w8b[:, 8 * tc:8 * tc + 8]],
                         ins=[w8[:, 8 * tc:8 * tc + 8]], sem=v_m,
                         allow_hazard=True)
                    v.mark(v_m, f"gw8b:{t}:{tc}")
            if go("gate"):
                ctx.section("vector", gate_v3)


            def qkv_v1(v, fcg, t=t):
                bank = fcg % 4
                v.wait(t_m, f"qkmm:{t}:{fcg}")
                if fcg >= 2:
                    v.wait(v_m, f"eluv2:{t}:{fcg - 2}")
                v.op("tensor_scalar_min",
                     elu1[:, 512 * (fcg % 2):512 * (fcg % 2) + 512],
                     pf[bank], 0.0,
                     outs=[elu1[:, 512 * (fcg % 2):512 * (fcg % 2) + 512]],
                     ins=[], sem=v_m)
                v.mark(v_m, f"eluv1:{t}:{fcg}")

            def qkv_v2(v, fcg, t=t):
                bank = fcg % 4
                v.wait(c_m, f"eluexp:{t}:{fcg}")
                dst = q_f if fcg < 8 else k_f
                fc = fcg % 8
                v.op("scalar_tensor_tensor", dst[:, 512 * fc:512 * (fc + 1)],
                     pf[bank], 0.0,
                     elu1[:, 512 * (fcg % 2):512 * (fcg % 2) + 512],
                     ALU.max, ALU.add,
                     outs=[dst[:, 512 * fc:512 * (fc + 1)]],
                     ins=[elu1[:, 512 * (fcg % 2):512 * (fcg % 2) + 512]],
                     sem=v_m)
                v.mark(v_m, f"eluv2:{t}:{fcg}")

            for g in range(8) if go("qkv") else []:
                def qkv_mm(tn, t=t, g=g):
                    if g == 0:
                        tn.wait(v_m, f"x2trev:{t}:{NTC - 1}")
                    for fcg in (2 * g, 2 * g + 1):
                        key = [f"qhA{t}", f"qhB{t}", f"khA{t}", f"khB{t}"][fcg // 4]
                        rwait(tn, key)
                        wsrc = rview(key, 8, 512)
                        fc = fcg % 4
                        bank = fcg % 4
                        ps_wait(tn, bank)
                        for kc in range(8):
                            tn.op("matmul", pf[bank],
                                  wsrc[:, 512 * kc + 128 * fc:512 * kc + 128 * fc + 128],
                                  xn2_f[:, 512 * kc:512 * (kc + 1)],
                                  start=(kc == 0), stop=(kc == 7),
                                  sem=(t_m if kc == 7 else None))
                        tn.mark(t_m, f"qkmm:{t}:{fcg}")
                        ps_set_free(bank, v_m, f"eluv2:{t}:{fcg}")
                        if fcg % 4 == 3:
                            tn.mark(t_m, f"use:{key}")
                ctx.section("tensor", qkv_mm)

                def qkv_vv(v, t=t, g=g):
                    if g > 0:
                        qkv_v2(v, 2 * g - 2)
                        qkv_v2(v, 2 * g - 1)
                    qkv_v1(v, 2 * g)
                    qkv_v1(v, 2 * g + 1)
                ctx.section("vector", qkv_vv)

                def qkv_cc(c, t=t, g=g):
                    for fcg in (2 * g, 2 * g + 1):
                        c.wait(v_m, f"eluv1:{t}:{fcg}")
                        if fcg >= 2:
                            c.wait(v_m, f"eluv2:{t}:{fcg - 2}")
                        c.op("activation",
                             elu1[:, 512 * (fcg % 2):512 * (fcg % 2) + 512],
                             elu1[:, 512 * (fcg % 2):512 * (fcg % 2) + 512],
                             AF.Exp, sem=c_m)
                        c.mark(c_m, f"eluexp:{t}:{fcg}")
                ctx.section("scalar", qkv_cc)

            if go("qkv"):
                def qkv_fin(v, t=t):
                    qkv_v2(v, 14)
                    qkv_v2(v, 15)
                ctx.section("vector", qkv_fin)
                tile_done[0] = (v_m, f"eluv2:{t}:15")

            ring_flush()
            for tc in range(NTC) if go("qkv") else []:
                def vprod_mm(tn, t=t, tc=tc):
                    tn.wait(v_m, "vones")
                    for half in range(2):
                        key = f"wvA{t}" if half == 0 else f"wvB{t}"
                        rwait(tn, key)
                        wvv = rview(key, 8, 512)
                        bank = (16 + 2 * tc + half) % 4
                        ps_wait(tn, bank)
                        for kc in range(8):
                            tn.op("matmul", pf[bank],
                                  xn2_f[:, 512 * kc + 128 * tc:512 * kc + 128 * tc + 128],
                                  wvv[:, 512 * kc:512 * kc + 512],
                                  start=(kc == 0), stop=(kc == 7),
                                  sem=(t_m if kc == 7 else None))
                        tn.mark(t_m, f"vmm:{t}:{tc}:{half}")
                        ps_set_free(bank, v_m, f"vev:{t}:{tc}:{half}")
                        if tc == NTC - 1:
                            tn.mark(t_m, f"use:{key}")
                ctx.section("tensor", vprod_mm)

                def vprod_ev(v, t=t, tc=tc):
                    vb = v_sb[tc]
                    if t > 0:
                        v.wait(t_m, f"szmm:{t - 1}:{tc}")
                    for half in range(2):
                        v.wait(t_m, f"vmm:{t}:{tc}:{half}")
                        dv = vb[:, 65 * 8 * half:65 * 8 * half + 520] \
                            .rearrange("p (h c) -> p h c", c=65)[:, :, 0:64]
                        v.op("tensor_copy", dv,
                             pf[(16 + 2 * tc + half) % 4].rearrange("p (h c) -> p h c", c=64),
                             outs=[vb], ins=[], sem=v_m)
                        v.mark(v_m, f"vev:{t}:{tc}:{half}")
                ctx.section("vector", vprod_ev)

            ring_flush()
            if STOP_AFTER is not None and STOP_AFTER in _SUBATT:
                _attn = 1
            elif STOP_AFTER is not None and STOP_AFTER.startswith("att") and len(STOP_AFTER) == 5 and STOP_AFTER[3].isdigit():
                _attn = int(STOP_AFTER[3])
            elif go("att"):
                _attn = NTC
            else:
                _attn = 0

            def ktr_mms(tn, t, tc):
                ps_wait(tn, 6)
                for hc in range(8):
                    tn.op("matmul", pt[0][:, 128 * hc:128 * (hc + 1)],
                          k_f[:, 512 * hc + 128 * tc:512 * hc + 128 * tc + 128],
                          ident, start=True, stop=True, is_transpose=True,
                          sem=(t_m if hc == 7 else None))
                tn.mark(t_m, f"ktr:{t}:{tc}")
                ps_set_free(6, v_m, f"ktrev:{t}:{tc}")

            def ktrev(v, t, tc):
                v.wait(t_m, f"ktr:{t}:{tc}")
                v.op("tensor_copy", k_tm, pt[0],
                     outs=[k_tm], ins=[], sem=v_m)
                v.mark(v_m, f"ktrev:{t}:{tc}")

            def a_round(tn, rnd, t, tc):
                ps_wait(tn, 0)
                ps_wait(tn, 1)
                for i in range(8):
                    h = 8 * rnd + i
                    fc = h // 2
                    off = 64 * (h % 2)
                    bank = i % 2
                    blk = i // 2
                    tn.op("matmul", pf[bank][:, 128 * blk:128 * (blk + 1)],
                          k_f[off:off + 64,
                              512 * fc + 128 * tc:512 * fc + 128 * tc + 128],
                          q_f[off:off + 64,
                              512 * fc + 128 * tc:512 * fc + 128 * tc + 128],
                          start=True, stop=True,
                          sem=(t_m if i >= 6 else None))
                tn.mark(t_m, f"ammr:{t}:{tc}:{rnd}")
                ps_set_free(0, v_m, f"amevr:{t}:{tc}:{rnd}")
                ps_set_free(1, v_m, f"amevr:{t}:{tc}:{rnd}")

            def am_round(v, rnd, t, tc):
                v.wait(t_m, f"ammr:{t}:{tc}:{rnd}")
                for par in range(2):
                    v.op("tensor_tensor",
                         a_m[2 * rnd + par].rearrange("p (r m) -> p r m", r=4),
                         pf[par].rearrange("p (r m) -> p r m", r=4),
                         mask4.unsqueeze(1).broadcast_to([128, 4, 128]), ALU.mult,
                         outs=[a_m[2 * rnd + par]], ins=[],
                         sem=(v_m if par == 1 else None))
                v.mark(v_m, f"amevr:{t}:{tc}:{rnd}")

            def o_pair(tn, pair, t, tc, cq):
                vb = v_sb[tc]
                rnd = pair
                tn.wait(v_m, f"amevr:{t}:{tc}:{rnd}")
                tn.wait(v_m, f"vev:{t}:{tc}:0")
                tn.wait(v_m, f"vev:{t}:{tc}:1")
                if cq > 0:
                    pk = (t, tc - 1) if tc > 0 else (t - 1, NTC - 1)
                    tn.wait(v_m, f"szbf:{pk[0]}:{pk[1]}")
                for qq in range(2):
                    q4 = 2 * pair + qq
                    bO = 2 + 2 * pair + qq
                    ps_wait(tn, bO)
                    for j in range(4):
                        h = 4 * q4 + j
                        fc = h // 2
                        off = 64 * (h % 2)
                        am = a_m[2 * rnd + h % 2]
                        blk = (h - 8 * rnd) // 2
                        tn.op("matmul", pf[bO][:, 65 * j:65 * j + 65],
                              am[:, 128 * blk:128 * (blk + 1)],
                              vb[:, 65 * h:65 * h + 65],
                              start=True, stop=(cq == 0),
                              sem=(t_m if (cq == 0 and j == 3) else None))
                        if cq > 0:
                            tn.op("matmul", pf[bO][:, 65 * j:65 * j + 65],
                                  q_f[off:off + 64,
                                      512 * fc + 128 * tc:512 * fc + 128 * tc + 128],
                                  sz_bf[off:off + 64,
                                        65 * (h // 2):65 * (h // 2) + 65],
                                  start=False, stop=True,
                                  sem=(t_m if j == 3 else None))
                    tn.mark(t_m, f"omm:{t}:{tc}:{q4}")
                    ps_set_free(bO, c_m, f"yscale:{t}:{tc}:{q4}")

            def gath(v, q4, t, tc):
                v.wait(t_m, f"omm:{t}:{tc}:{q4}")
                if q4 >= 2:
                    v.wait(v_m, f"rde:{t}:{tc}:{q4 - 2}")
                elif t > 0 or tc > 0:
                    pk = (t, tc - 1) if tc > 0 else (t - 1, NTC - 1)
                    v.wait(v_m, f"rde:{pk[0]}:{pk[1]}:{q4 + 2}")
                v.op("tensor_copy", d4[:, 4 * (q4 % 2):4 * (q4 % 2) + 4],
                     pf[2 + q4][:, 64:64 + 4 * 65:65],
                     outs=[d4[:, 4 * (q4 % 2):4 * (q4 % 2) + 4]], ins=[],
                     sem=v_m)
                v.mark(v_m, f"gath:{t}:{tc}:{q4}")

            def recip(v, q4, t, tc):
                if q4 >= 2:
                    v.wait(c_m, f"yscale:{t}:{tc}:{q4 - 2}")
                elif t > 0 or tc > 0:
                    pk = (t, tc - 1) if tc > 0 else (t - 1, NTC - 1)
                    v.wait(c_m, f"yscale:{pk[0]}:{pk[1]}:{q4 + 2}")
                v.wait(v_m, f"gath:{t}:{tc}:{q4}")
                v.op("reciprocal", r4[:, 4 * (q4 % 2):4 * (q4 % 2) + 4],
                     d4[:, 4 * (q4 % 2):4 * (q4 % 2) + 4],
                     outs=[r4[:, 4 * (q4 % 2):4 * (q4 % 2) + 4]],
                     ins=[d4[:, 4 * (q4 % 2):4 * (q4 % 2) + 4]],
                     sem=v_m, allow_hazard=True)
                v.mark(v_m, f"rde:{t}:{tc}:{q4}")

            def yscale_pair(c, pair, t, tc):
                for qq in range(2):
                    q4 = 2 * pair + qq
                    c.wait(v_m, f"rde:{t}:{tc}:{q4}")
                    for j in range(4):
                        h = 4 * q4 + j
                        c.op("activation", y_tm[:, 64 * h:64 * h + 64],
                             pf[2 + q4][:, 65 * j:65 * j + 64],
                             AF.Copy,
                             scale=r4[:, 4 * (q4 % 2) + j:4 * (q4 % 2) + j + 1],
                             sem=(c_m if j == 3 else None))
                    c.mark(c_m, f"yscale:{t}:{tc}:{q4}")

            if _attn > 0:
                def att_pre(tn, t=t):
                    tn.wait(v_m, f"eluv2:{t}:15")
                    tn.wait(v_m, f"vev:{t}:0:1")
                    ktr_mms(tn, t, 0)
                    a_round(tn, 0, t, 0)
                ctx.section("tensor", att_pre)

                def att_prev(v, t=t):
                    ktrev(v, t, 0)
                    am_round(v, 0, t, 0)
                ctx.section("vector", att_prev)

            for tc in range(_attn):
                cq = NTC * tis + tc

                def att_t1(tn, t=t, tc=tc, cq=cq):
                    a_round(tn, 1, t, tc)
                ctx.section("tensor", att_t1)

                def att_v1(v, t=t, tc=tc):
                    am_round(v, 1, t, tc)
                ctx.section("vector", att_v1)

                def att_t2(tn, t=t, tc=tc, cq=cq):
                    o_pair(tn, 0, t, tc, cq)
                ctx.section("tensor", att_t2)

                def att_v2(v, t=t, tc=tc):
                    gath(v, 0, t, tc)
                    gath(v, 1, t, tc)
                    recip(v, 0, t, tc)
                    recip(v, 1, t, tc)
                ctx.section("vector", att_v2)

                def att_s1(c, t=t, tc=tc):
                    yscale_pair(c, 0, t, tc)
                ctx.section("scalar", att_s1)

                def att_t3(tn, t=t, tc=tc, cq=cq):
                    o_pair(tn, 1, t, tc, cq)
                ctx.section("tensor", att_t3)

                def att_v3(v, t=t, tc=tc):
                    gath(v, 2, t, tc)
                    gath(v, 3, t, tc)
                    recip(v, 2, t, tc)
                    recip(v, 3, t, tc)
                ctx.section("vector", att_v3)

                def att_s2(c, t=t, tc=tc):
                    yscale_pair(c, 1, t, tc)
                ctx.section("scalar", att_s2)

                def att_t4(tn, t=t, tc=tc, cq=cq):
                    vb = v_sb[tc]
                    tn.wait(v_m, f"ktrev:{t}:{tc}")
                    if cq > 0:
                        pk = (t, tc - 1) if tc > 0 else (t - 1, NTC - 1)
                        tn.wait(v_m, f"szbf:{pk[0]}:{pk[1]}")
                    for half in range(2):
                        bank = 2 + half
                        ps_wait(tn, bank)
                        for pp in range(4):
                            p = 4 * half + pp
                            for par in range(2):
                                h = 2 * p + par
                                tn.op("matmul",
                                      pf[bank][64 * par:64 * par + 64,
                                               65 * pp:65 * pp + 65],
                                      k_tm[:, 128 * p + 64 * par:128 * p + 64 * par + 64],
                                      vb[:, 65 * h:65 * h + 65],
                                      start=True, stop=True,
                                      tile_position=(0, 64 * par),
                                      sem=(t_m if (pp == 3 and par == 1) else None))
                        tn.mark(t_m, f"szd:{t}:{tc}:{half}")
                        ps_set_free(bank, v_m, f"szup:{t}:{tc}:{half}")
                    tn.mark(t_m, f"szmm:{t}:{tc}")
                    if tc + 1 < NTC:
                        ktr_mms(tn, t, tc + 1)
                        a_round(tn, 0, t, tc + 1)
                    tn.wait(c_m, f"yscale:{t}:{tc}:3")
                    ps_wait(tn, 7)
                    for yc in range(8):
                        tn.op("matmul", pt[1][:, 128 * yc:128 * (yc + 1)],
                              y_tm[:, 128 * yc:128 * (yc + 1)], ident,
                              start=True, stop=True, is_transpose=True,
                              sem=(t_m if yc == 7 else None))
                    tn.mark(t_m, f"ytr:{t}:{tc}")
                    ps_set_free(7, v_m, f"ytrev:{t}:{tc}")
                ctx.section("tensor", att_t4)

                def att_v4(v, t=t, tc=tc, cq=cq):
                    if tc + 1 < NTC:
                        ktrev(v, t, tc + 1)
                        am_round(v, 0, t, tc + 1)
                    v.wait(t_m, f"ytr:{t}:{tc}")
                    dv = y_fT.rearrange("p (kc m) -> p kc m", kc=8) \
                        [:, 0:8, 128 * tc:128 * tc + 128]
                    v.op("tensor_copy", dv,
                         pt[1].rearrange("p (j m) -> p j m", j=8),
                         outs=[y_fT], ins=[], sem=v_m)
                    v.mark(v_m, f"ytrev:{t}:{tc}")
                    for half in range(2):
                        v.wait(t_m, f"szd:{t}:{tc}:{half}")
                        dst = sz_m[:, 260 * half:260 * half + 260]
                        if cq == 0:
                            v.op("tensor_copy", dst, pf[2 + half][:, 0:260],
                                 outs=[dst], ins=[], sem=v_m)
                        else:
                            v.op("tensor_tensor", dst, dst, pf[2 + half][:, 0:260],
                                 ALU.add, outs=[dst], ins=[], sem=v_m)
                        v.mark(v_m, f"szup:{t}:{tc}:{half}")
                    v.wait(v_m, f"szup:{t}:{tc}:0")
                    v.wait(v_m, f"szup:{t}:{tc}:1")
                    for half in range(2):
                        v.op("tensor_copy", sz_bf[:, 260 * half:260 * half + 260],
                             sz_m[:, 260 * half:260 * half + 260],
                             outs=[sz_bf[:, 260 * half:260 * half + 260]],
                             ins=[sz_m[:, 260 * half:260 * half + 260]],
                             sem=(v_m if half == 1 else None),
                             allow_hazard=True)
                    v.mark(v_m, f"szbf:{t}:{tc}")
                ctx.section("vector", att_v4)

            ring_flush()
            if _attn > 0:
                if STOP_AFTER in _SUBATT and STOP_AFTER != "attt4":
                    tile_done[0] = {"attt1": (v_m, f"amevr:{t}:0:0"),
                                    "attt2": (v_m, f"rde:{t}:0:1"),
                                    "atts1": (c_m, f"yscale:{t}:0:1"),
                                    "attt3": (v_m, f"rde:{t}:0:3"),
                                    "atts2": (c_m, f"yscale:{t}:0:3")}[STOP_AFTER]
                else:
                    tile_done[0] = (v_m, f"szbf:{t}:{_attn - 1}")

            def gate_tr(tn, t=t):
                ps_wait(tn, 6)
                for tc in range(NTC):
                    tn.wait(v_m, f"gw8b:{t}:{tc}")
                    tn.op("matmul", pt[0][0:8, 128 * tc:128 * (tc + 1)],
                          w8b[:, 8 * tc:8 * tc + 8], ident,
                          start=True, stop=True, is_transpose=True,
                          sem=(t_m if tc == NTC - 1 else None))
                tn.mark(t_m, f"gw8tr:{t}")
                ps_set_free(6, v_m, f"gw8tev:{t}")
            if go("gate"):
                ctx.section("tensor", gate_tr)

            def gate_trev(v, t=t):
                v.wait(t_m, f"gw8tr:{t}")
                v.op("tensor_copy", w8t, pt[0][0:8, 0:512],
                     outs=[w8t], ins=[], sem=v_m)
                v.mark(v_m, f"gw8tev:{t}")
            if go("gate"):
                ctx.section("vector", gate_trev)

            if go("gate"):
                tile_done[0] = (v_m, f"gw8tev:{t}")


            for pj in range(6) if go("ush") else []:
                def ush_mm(tn, t=t, pj=pj):
                    if pj == 0:
                        tn.wait(v_m, f"ytrev:{t}:{NTC - 1}")
                    for j in (2 * pj, 2 * pj + 1):
                        qy, qg = j // 6, 2 + j // 6
                        ky, kg = f"s1_{t}_{qy}", f"s1_{t}_{qg}"
                        rwait(tn, ky)
                        rwait(tn, kg)
                        wy = rview(ky, 8, 768)
                        wgr = rview(kg, 8, 768)
                        hy = j % 6
                        bY, bG = j % 3, 3 + j % 3
                        ps_wait(tn, bY)
                        for kc in range(8):
                            tn.op("matmul", pf[bY],
                                  wy[:, 768 * kc + 128 * hy:768 * kc + 128 * hy + 128],
                                  xn_f[:, 512 * kc:512 * (kc + 1)],
                                  start=(kc == 0), stop=(kc == 7),
                                  sem=(t_m if kc == 7 else None))
                        tn.mark(t_m, f"usy:{t}:{j}")
                        ps_set_free(bY, v_m, f"usu:{t}:{j}")
                        ps_wait(tn, bG)
                        for kc in range(8):
                            tn.op("matmul", pf[bG],
                                  wgr[:, 768 * kc + 128 * hy:768 * kc + 128 * hy + 128],
                                  xn_f[:, 512 * kc:512 * (kc + 1)],
                                  start=(kc == 0), stop=(kc == 7),
                                  sem=(t_m if kc == 7 else None))
                        tn.mark(t_m, f"usg:{t}:{j}")
                        ps_set_free(bG, v_m, f"ustg:{t}:{j // 2}:{j % 2}")
                        if j == 5:
                            tn.mark(t_m, f"use:s1_{t}_0")
                            tn.mark(t_m, f"use:s1_{t}_2")
                        if j == 11:
                            tn.mark(t_m, f"use:s1_{t}_1")
                            tn.mark(t_m, f"use:s1_{t}_3")
                ctx.section("tensor", ush_mm)

                def ush_c(c, t=t, pj=pj):
                    for j in (2 * pj, 2 * pj + 1):
                        c.wait(t_m, f"usg:{t}:{j}")
                        rd = sg_reader.get(j % 2)
                        if rd is not None:
                            c.wait(rd[0], rd[1])
                        c.op("activation", sg_sb[j % 2], pf[3 + j % 3],
                             AF.Silu, sem=c_m)
                        c.mark(c_m, f"ussg:{t}:{j}")
                ctx.section("scalar", ush_c)

                def ush_v(v, t=t, pj=pj):
                    for j in (2 * pj, 2 * pj + 1):
                        v.wait(c_m, f"ussg:{t}:{j}")
                        v.wait(t_m, f"usy:{t}:{j}")
                        v.op("tensor_tensor", us_sb[:, 512 * j:512 * (j + 1)],
                             pf[j % 3], sg_sb[j % 2], ALU.mult,
                             outs=[us_sb[:, 512 * j:512 * (j + 1)]],
                             ins=[sg_sb[j % 2]], sem=v_m)
                        v.mark(v_m, f"usu:{t}:{j}")
                        v.mark(v_m, f"ustg:{t}:{j // 2}:{j % 2}")
                        sg_reader[j % 2] = (v_m, f"usu:{t}:{j}")
                ctx.section("vector", ush_v)

            ring_flush()
            if go("ush"):
                tile_done[0] = (v_m, f"usu:{t}:11")

            for ph in range(2) if go("z0") else []:
                def z0_mm(tn, t=t, ph=ph):
                    if ph == 0:
                        tn.wait(s_res, "all")
                    for tc in (2 * ph, 2 * ph + 1):
                        banks = (2, 3) if tc % 2 == 0 else (4, 5)
                        if ph == 0 and tc == 0:
                            tn.wait(v_m, f"ytrev:{t}:{NTC - 1}")
                        for half in range(2):
                            b = banks[half]
                            ps_wait(tn, b)
                            for yc in range(8):
                                tn.op("matmul", pf[b],
                                      y_fT[:, 512 * yc + 128 * tc:512 * yc + 128 * tc + 128],
                                      wo_sb[:, 1024 * yc + 512 * half:1024 * yc + 512 * half + 512],
                                      start=(yc == 0), stop=False)
                            for hc in range(12):
                                if ph == 0 and tc == 0 and half == 0:
                                    tn.wait(v_m, f"usu:{t}:{hc}")
                                tn.op("matmul", pf[b],
                                      us_sb[:, 512 * hc + 128 * tc:512 * hc + 128 * tc + 128],
                                      ws2_sb[:, 1024 * hc + 512 * half:1024 * hc + 512 * half + 512],
                                      start=False, stop=(hc == 11),
                                      sem=(t_m if hc == 11 else None))
                            tn.mark(t_m, f"z:{t}:0:{tc}:{half}")
                            ps_set_free(b, v_m, f"zev:{t}:0:{tc}:{half}")
                ctx.section("tensor", z0_mm)

                def z0_ev(v, t=t, ph=ph):
                    for tc in (2 * ph, 2 * ph + 1):
                        v.wait(s_x, f"x{t}:{tc}")
                        banks = (2, 3) if tc % 2 == 0 else (4, 5)
                        for half in range(2):
                            v.wait(t_m, f"z:{t}:0:{tc}:{half}")
                            dst = out_sb[:, D * tc + 512 * half:D * tc + 512 * half + 512]
                            v.op("tensor_tensor", dst, dst, pf[banks[half]],
                                 ALU.add, outs=[dst], ins=[], sem=v_m,
                                 allow_hazard=True)
                            v.mark(v_m, f"zev:{t}:0:{tc}:{half}")
                ctx.section("vector", z0_ev)

            if go("z0"):
                tile_done[0] = (v_m, f"zev:{t}:0:{NTC - 1}:1")
            XMAP = {0: (0, 1), 1: (2,), 2: (3,), 3: ()}
            for w in [w_ for w_ in range(4) if go(f"uz{w_}")]:
                if STOP_AFTER is None and t + 1 < n_tiles:
                    for xw in XMAP[w]:
                        def xload(sy, t=t, xw=xw):
                            if t > 0:
                                sy.wait(s_out, f"o{t - 1}:{xw}")
                            pk = (t + 1, xw - 1) if xw > 0 else (t, NTC - 1)
                            sy.wait(s_x, f"x{pk[0]}:{pk[1]}")
                            sy.op("dma_start",
                                  out=outs2[(t + 1) % 2][:, D * xw:D * (xw + 1)],
                                  in_=x_d[TILE * (t + 1) + TC * xw:
                                          TILE * (t + 1) + TC * (xw + 1), :],
                                  sem=s_x, semk=16)
                            sy.mark(s_x, f"x{t + 1}:{xw}")
                        ctx.section("sync", xload)
                    early_ln(t + 1, w)
                for ee in range(2):
                    e = 2 * w + ee

                    def ue_pre(tn, t=t, e=e):
                        tn.wait(v_m, f"gw8tev:{t}")
                        ps_wait(tn, 4)
                        tn.op("matmul", pf[4], sel[:, 128 * e:128 * (e + 1)],
                              w8t, start=True, stop=True, sem=t_m)
                        tn.mark(t_m, f"wbmm:{t}:{e}")
                        ps_set_free(4, v_m, f"wbev:{t}:{e}")
                    ctx.section("tensor", ue_pre)

                    def ue_vwb(v, t=t, e=e):
                        v.wait(t_m, f"wbmm:{t}:{e}")
                        if e >= 2:
                            v.wait(v_m, f"ueu:{t}:{e - 2}:5")
                        elif t > 0:
                            v.wait(v_m, f"ueu:{t - 1}:{e + 6}:5")
                        v.op("tensor_copy", wb_sb2[e % 2], pf[4],
                             outs=[wb_sb2[e % 2]], ins=[], sem=v_m)
                        v.mark(v_m, f"wbev:{t}:{e}")
                    ctx.section("vector", ue_vwb)

                    for pj in range(3):
                        def ue_mm(tn, t=t, e=e, pj=pj):
                            ka, kb = f"e1a_{t}_{e}", f"e1b_{t}_{e}"
                            rwait(tn, ka)
                            rwait(tn, kb)
                            wa = rview(ka, 8, 768)
                            wbv = rview(kb, 8, 768)
                            for j in (2 * pj, 2 * pj + 1):
                                bY, bG = j % 3, 3 + j % 3
                                ps_wait(tn, bY)
                                for kc in range(8):
                                    tn.op("matmul", pf[bY],
                                          wa[:, 768 * kc + 128 * j:768 * kc + 128 * j + 128],
                                          xn_f[:, 512 * kc:512 * (kc + 1)],
                                          start=(kc == 0), stop=(kc == 7),
                                          sem=(t_m if kc == 7 else None))
                                tn.mark(t_m, f"uey:{t}:{e}:{j}")
                                ps_set_free(bY, v_m, f"ueu:{t}:{e}:{j}")
                                ps_wait(tn, bG)
                                for kc in range(8):
                                    tn.op("matmul", pf[bG],
                                          wbv[:, 768 * kc + 128 * j:768 * kc + 128 * j + 128],
                                          xn_f[:, 512 * kc:512 * (kc + 1)],
                                          start=(kc == 0), stop=(kc == 7),
                                          sem=(t_m if kc == 7 else None))
                                tn.mark(t_m, f"ueg:{t}:{e}:{j}")
                                ps_set_free(bG, c_m, f"uesg:{t}:{e}:{j}")
                            if pj == 2:
                                tn.mark(t_m, f"use:{ka}")
                                tn.mark(t_m, f"use:{kb}")
                        ctx.section("tensor", ue_mm)

                        def ue_c(c, t=t, e=e, pj=pj):
                            for j in (2 * pj, 2 * pj + 1):
                                c.wait(t_m, f"ueg:{t}:{e}:{j}")
                                rd = sg_reader.get(j % 2)
                                if rd is not None:
                                    c.wait(rd[0], rd[1])
                                c.op("activation", sg_sb[j % 2], pf[3 + j % 3],
                                     AF.Silu, sem=c_m)
                                c.mark(c_m, f"uesg:{t}:{e}:{j}")
                        ctx.section("scalar", ue_c)

                        def ue_v(v, t=t, e=e, pj=pj):
                            slot = u_slot[e % 3]
                            if pj == 0:
                                v.wait(v_m, f"wbev:{t}:{e}")
                                key = uslot_consumer.get(e % 3)
                                if key is not None:
                                    v.wait(t_m, key)
                            for j in (2 * pj, 2 * pj + 1):
                                v.wait(c_m, f"uesg:{t}:{e}:{j}")
                                v.op("tensor_tensor", sg_sb[j % 2], sg_sb[j % 2],
                                     wb_sb2[e % 2], ALU.mult,
                                     outs=[sg_sb[j % 2]], ins=[wb_sb2[e % 2]],
                                     sem=(v_m if j % 2 == 1 else None))
                            v.mark(v_m, f"uetsg:{t}:{e}:{pj}")
                            v.wait(v_m, f"uetsg:{t}:{e}:{pj}")
                            for j in (2 * pj, 2 * pj + 1):
                                v.wait(t_m, f"uey:{t}:{e}:{j}")
                                v.op("tensor_tensor", slot[:, 512 * j:512 * (j + 1)],
                                     pf[j % 3], sg_sb[j % 2], ALU.mult,
                                     outs=[slot[:, 512 * j:512 * (j + 1)]],
                                     ins=[sg_sb[j % 2]], sem=v_m,
                                     allow_hazard=True)
                                v.mark(v_m, f"ueu:{t}:{e}:{j}")
                                sg_reader[j % 2] = (v_m, f"ueu:{t}:{e}:{j}")
                        ctx.section("vector", ue_v)

                    ring_flush()

                for ph in range(2):
                    def zw_mm(tn, t=t, w=w, ph=ph):
                        e0, e1 = 2 * w, 2 * w + 1
                        k20, k21 = f"e2_{t}_{e0}", f"e2_{t}_{e1}"
                        rwait(tn, k20)
                        rwait(tn, k21)
                        w20 = rview(k20, 6, 1024)
                        w21 = rview(k21, 6, 1024)
                        for tc in (2 * ph, 2 * ph + 1):
                            banks = (2, 3) if tc % 2 == 0 else (4, 5)
                            for half in range(2):
                                b = banks[half]
                                ps_wait(tn, b)
                                for ei, (e, w2) in enumerate(((e0, w20), (e1, w21))):
                                    slot = u_slot[e % 3]
                                    for hc in range(6):
                                        if ph == 0 and tc == 2 * ph and half == 0:
                                            tn.wait(v_m, f"ueu:{t}:{e}:{hc}")
                                        tn.op("matmul", pf[b],
                                              slot[:, 512 * hc + 128 * tc:512 * hc + 128 * tc + 128],
                                              w2[:, 1024 * hc + 512 * half:1024 * hc + 512 * half + 512],
                                              start=(ei == 0 and hc == 0),
                                              stop=(ei == 1 and hc == 5),
                                              sem=(t_m if (ei == 1 and hc == 5) else None))
                                tn.mark(t_m, f"z:{t}:{w + 1}:{tc}:{half}")
                                ps_set_free(b, v_m, f"zev:{t}:{w + 1}:{tc}:{half}")
                        if ph == 1:
                            tn.mark(t_m, f"use:{k20}")
                            tn.mark(t_m, f"use:{k21}")
                            uslot_consumer[e0 % 3] = f"z:{t}:{w + 1}:{NTC - 1}:1"
                            uslot_consumer[e1 % 3] = f"z:{t}:{w + 1}:{NTC - 1}:1"
                    ctx.section("tensor", zw_mm)

                    def zw_ev(v, t=t, w=w, ph=ph):
                        for tc in (2 * ph, 2 * ph + 1):
                            banks = (2, 3) if tc % 2 == 0 else (4, 5)
                            for half in range(2):
                                v.wait(t_m, f"z:{t}:{w + 1}:{tc}:{half}")
                                v.wait(v_m, f"zev:{t}:{w}:{tc}:{half}")
                                dst = out_sb[:, D * tc + 512 * half:D * tc + 512 * half + 512]
                                v.op("tensor_tensor", dst, dst, pf[banks[half]],
                                     ALU.add, outs=[dst], ins=[], sem=v_m,
                                     allow_hazard=True)
                                v.mark(v_m, f"zev:{t}:{w + 1}:{tc}:{half}")
                        if ph == 1:
                            v.mark(v_m, f"zwin:{t}:{w + 1}")
                    ctx.section("vector", zw_ev)

                ring_flush()
                tile_done[0] = (v_m, f"zwin:{t}:{w + 1}")
                if STOP_AFTER is None and t + 1 < n_tiles:
                    if w == 3:
                        emit_ap_c1(t + 1, 0)
                        emit_ap_c1(t + 1, 1)

            if STOP_AFTER is not None and t + 1 < n_tiles:
                for tc0 in range(NTC):
                    early_ln(t + 1, tc0)
            def store_sync(s, t=t, tile_done=tile_done, out_sb=out_sb):
                full = STOP_AFTER is None
                for tc in range(NTC):
                    if full:
                        s.wait(v_m, f"zev:{t}:4:{tc}:1")
                    else:
                        s.wait(tile_done[0][0], tile_done[0][1])
                    if tc > 0 or t > 0:
                        pk = (t, tc - 1) if tc > 0 else (t - 1, NTC - 1)
                        s.wait(s_out, f"o{pk[0]}:{pk[1]}")
                    s.op("dma_start",
                         out=out_d[TILE * t + TC * tc:TILE * t + TC * (tc + 1), :],
                         in_=out_sb[:, D * tc:D * (tc + 1)],
                         sem=s_out, semk=16)
                    s.mark(s_out, f"o{t}:{tc}")
                if not full and t + 1 < n_tiles:
                    for tc in range(NTC):
                        s.wait(s_out, f"o{t}:{tc}")
                        if tc > 0:
                            s.wait(s_x, f"x{t + 1}:{tc - 1}")
                        s.op("dma_start",
                             out=outs2[(t + 1) % 2][:, D * tc:D * (tc + 1)],
                             in_=x_d[TILE * (t + 1) + TC * tc:TILE * (t + 1) + TC * (tc + 1), :],
                             sem=s_x, semk=16)
                        s.mark(s_x, f"x{t + 1}:{tc}")
            ctx.section("sync", store_sync)

            ring_flush()

        def fin(s):
            s.wait(s_out, f"o{n_tiles - 1}:{NTC - 1}")
        ctx.section("sync", fin)

        if STOP_AFTER is None:
            assert not ring_q, f"{len(ring_q)} ring items never loaded"

    for s in SEMS:
        cm = nc.semaphore(s.name)
        s.h = cm.__enter__()
        ctxs.append(cm)
    ctx1 = Ctx(nc, None, dry=True)
    emit(ctx1)
    blk = nc.Block()
    block = blk.__enter__()
    for engine, tape in ctx1.tape.items():
        if not tape:
            continue

        def replay(e, tape=tape):
            for ent in tape:
                if ent[0] == "op":
                    _, method, args, kw, sem, semk = ent
                    inst = getattr(e, method)(*args, **kw)
                    if sem is not None:
                        inst.then_inc(sem.h, semk)
                else:
                    _, sem, val = ent
                    e.wait_ge(sem.h, val)
        getattr(block, engine)(replay)
    blk.__exit__(None, None, None)
    nc._keepalive = ctxs
    return nc


_CACHE = {}


def _consts():
    ident = np.eye(128, dtype=ml_dtypes.bfloat16)
    mask4 = np.ascontiguousarray(np.triu(np.ones((128, 128), np.float32)).astype(ml_dtypes.bfloat16))
    selm = np.zeros((8, 1024), np.float32)
    for e in range(8):
        selm[e, 128 * e:128 * (e + 1)] = 1.0
    return ident, mask4, selm.astype(ml_dtypes.bfloat16)


def kernel(L0_attn_x, L0_attn_W1, L0_attn_W2, L0_ffn_gate_w, L0_ffn_W1,
           L0_ffn_W2, L0_ffn_Ws1, L0_ffn_Ws2):
    x = np.ascontiguousarray(np.asarray(L0_attn_x, dtype=np.float32))
    B, T, _ = x.shape
    n_seq = B // N_CORES
    key = (n_seq, T)
    if key not in _CACHE:
        _CACHE[key] = build_program(n_seq, T)
    nc = _CACHE[key]

    W1 = np.asarray(L0_attn_W1, np.float32)
    wqkT = np.ascontiguousarray(W1[0:2048, :].T)
    wvT = np.ascontiguousarray(W1[2048:3072, :].T)
    woT = np.ascontiguousarray(np.asarray(L0_attn_W2, np.float32).T)
    wgT = np.ascontiguousarray(np.asarray(L0_ffn_gate_w, np.float32).T)
    we1T = np.ascontiguousarray(np.asarray(L0_ffn_W1, np.float32).transpose(0, 2, 1))
    we2T = np.ascontiguousarray(np.asarray(L0_ffn_W2, np.float32).transpose(0, 2, 1))
    ws1T = np.ascontiguousarray(np.asarray(L0_ffn_Ws1, np.float32).T)
    ws2T = np.ascontiguousarray(np.asarray(L0_ffn_Ws2, np.float32).T)
    ident, mask4, selm = _consts()

    in_maps = []
    for c in range(N_CORES):
        xc = np.ascontiguousarray(x[c * n_seq:(c + 1) * n_seq].reshape(n_seq * T, D))
        in_maps.append(dict(
            x=xc, w_qkT=wqkT, w_vT=wvT, w_oT=woT, w_gT=wgT,
            w_e1T=we1T, w_e2T=we2T, w_s1T=ws1T, w_s2T=ws2T,
            c_ident=ident, c_mask4=mask4, c_sel=selm,
        ))
    res = run_bass_kernel_spmd(nc, in_maps, core_ids=list(range(N_CORES)))
    outs = [np.asarray(res.results[c]["out"]).reshape(n_seq, T, D)
            for c in range(N_CORES)]
    out = np.ascontiguousarray(np.concatenate(outs, axis=0))
    if out.dtype != np.float64:
        out = out.astype(np.float32, copy=False)
    return out


# version 22
# speedup vs baseline: 1.0355x; 1.0121x over previous
import numpy as np
import ml_dtypes

import concourse.bass as bass
import concourse.mybir as mybir
from concourse.bass_utils import run_bass_kernel_spmd

F32 = mybir.dt.float32
BF16 = mybir.dt.bfloat16
AF = mybir.ActivationFunctionType
ALU = mybir.AluOpType
AX = mybir.AxisListType

D = 1024
NH = 16
E = 8
TC = 128
TILE = 512
NTC = TILE // TC
EPS = 1e-5
NEG_BIG = -1e30
SLOT = 6144
NSLOT = 4
N_CORES = 8
STOP_AFTER = None
_PHASES = ["ln", "gate", "qkv", "att1c", "att2c", "att3c", "att", "ush", "z0",
           "uz0", "uz1", "uz2", "uz3"]
_SUBATT = ["attt1", "attt2", "atts1", "attt3", "atts2", "attt4"]
def _sub_ok(name):
    if STOP_AFTER in (None, ) or STOP_AFTER not in _SUBATT:
        return True
    return _SUBATT.index(name) <= _SUBATT.index(STOP_AFTER)


class Sem:
    def __init__(self, name):
        self.name = name
        self.n = 0
        self.h = None
        self.marks = {}


class _Stub:
    def then_inc(self, h, k):
        return self


class Eng:
    def __init__(self, ctx, name, real):
        self.ctx = ctx
        self.name = name
        self.real = real

    def _rng(self, ap):
        try:
            off = ap.offset if isinstance(ap.offset, int) else 0
            return (ap.tensor.name, off, off + max(ap.free_size(), 1))
        except Exception:
            return None

    def _guard(self, outs, ins, allow_hazard):
        if self.name != "vector":
            return
        lo = self.ctx.v_last_out
        if lo is not None and not allow_hazard:
            for ap in ins:
                r = self._rng(ap)
                if r and r[0] == lo[0] and r[1] < lo[2] and lo[1] < r[2]:
                    raise RuntimeError(
                        f"DVE RAW hazard: read {r} right after write {lo}")
        self.ctx.v_last_out = self._rng(outs[0]) if outs else None

    def op(self, method, *args, outs=(), ins=(), sem=None, semk=1,
           allow_hazard=False, **kw):
        self._guard(list(outs), list(ins), allow_hazard)
        if sem is not None:
            sem.n += semk
        self.ctx.tape[self.name].append(("op", method, args, kw, sem, semk))
        return _Stub()

    def mark(self, sem, key):
        if True:
            assert key not in sem.marks, f"dup mark {sem.name}:{key}"
            sem.marks[key] = sem.n
        self.ctx.marked.add((sem.name, key))

    def wait(self, sem, key=None, val=None):
        if self.name == "vector":
            self.ctx.v_last_out = None
        if val is None:
            assert (sem.name, key) in self.ctx.marked, \
                f"wait on unrecorded mark {sem.name}:{key}"
            val = sem.marks[key]
        assert val <= sem.n, f"future wait {sem.name} {val}>{sem.n}"
        self.ctx.tape[self.name].append(("wait", sem, val))


class Ctx:

    def __init__(self, nc, block, dry):
        self.nc = nc
        self.block = block
        self.dry = dry
        self.marked = set()
        self.v_last_out = None
        self.tape = {k: [] for k in
                     ("tensor", "vector", "scalar", "gpsimd", "sync")}

    def section(self, engine, body):
        body(Eng(self, engine, None))


def build_program(n_seq, T):
    assert T % TILE == 0
    NT = n_seq * T
    n_tiles = NT // TILE
    tps = T // TILE

    nc = bass.Bass("TRN2", target_bir_lowering=False, debug=False)

    x_d = nc.dram_tensor("x", [NT, D], F32, kind="ExternalInput").ap()
    out_d = nc.dram_tensor("out", [NT, D], F32, kind="ExternalOutput").ap()
    wqk_d = nc.dram_tensor("w_qkT", [D, 2048], F32, kind="ExternalInput").ap()
    wv_d = nc.dram_tensor("w_vT", [D, D], F32, kind="ExternalInput").ap()
    wo_d = nc.dram_tensor("w_oT", [D, D], F32, kind="ExternalInput").ap()
    wg_d = nc.dram_tensor("w_gT", [D, E], F32, kind="ExternalInput").ap()
    we1_d = nc.dram_tensor("w_e1T", [E, D, 1536], F32, kind="ExternalInput").ap()
    we2_d = nc.dram_tensor("w_e2T", [E, 768, D], F32, kind="ExternalInput").ap()
    ws1_d = nc.dram_tensor("w_s1T", [D, 3072], F32, kind="ExternalInput").ap()
    ws2_d = nc.dram_tensor("w_s2T", [1536, D], F32, kind="ExternalInput").ap()
    ident_d = nc.dram_tensor("c_ident", [128, 128], BF16, kind="ExternalInput").ap()
    mask_d = nc.dram_tensor("c_mask4", [128, 128], BF16, kind="ExternalInput").ap()
    sel_d = nc.dram_tensor("c_sel", [8, 1024], BF16, kind="ExternalInput").ap()

    wqk_s = nc.dram_tensor("s_qk", [D, 2048], BF16, kind="Internal").ap()
    wv_s = nc.dram_tensor("s_v", [D, D], BF16, kind="Internal").ap()
    we1_s = nc.dram_tensor("s_e1", [E, D, 1536], BF16, kind="Internal").ap()
    we2_s = nc.dram_tensor("s_e2", [E, 768, D], BF16, kind="Internal").ap()
    ws1_s = nc.dram_tensor("s_s1", [D, 3072], BF16, kind="Internal").ap()

    ctxs = []

    def sbuf(name, shape, dt):
        cm = nc.sbuf_tensor(name, shape, dt)
        t = cm.__enter__()
        ctxs.append(cm)
        return t

    def psumt(name, shape, dt):
        cm = nc.psum_tensor(name, shape, dt)
        t = cm.__enter__()
        ctxs.append(cm)
        return t

    ident = sbuf("ident", [128, 128], BF16)[:, :]
    mask4 = sbuf("mask4", [128, 128], BF16)[:, :]
    sel = sbuf("sel", [8, 1024], BF16)[:, :]
    wg_sb = sbuf("wg_sb", [128, 8 * E], BF16)[:, :]
    wo_sb = sbuf("wo_sb", [128, 8 * 1024], BF16)[:, :]
    ws2_sb = sbuf("ws2_sb", [128, 12 * 1024], BF16)[:, :]
    ring = sbuf("ring", [128, NSLOT * SLOT], BF16)[:, :]
    outs2 = [sbuf(f"out_sb{i}", [128, NTC * D], F32)[:, :] for i in range(2)]
    xn_f = sbuf("xn_f", [128, 8 * TILE], BF16)[:, :]
    qk_ar = sbuf("qk_ar", [128, 8192], BF16)[:, :]
    u_ar = sbuf("u_ar", [128, 9216], BF16)[:, :]
    y_fT = sbuf("y_fT", [128, 8 * TILE], BF16)[:, :]
    v_sb = [sbuf(f"v_sb{i}", [128, NH * 65], BF16)[:, :] for i in range(4)]
    sz_m = sbuf("sz_m", [128, 520], F32)[:, :]
    sz_bf = sbuf("sz_bf", [128, 520], BF16)[:, :]
    k_tm = sbuf("k_tm", [128, 1024], BF16)[:, :]
    xn2_tmb = k_tm
    y_tm = sbuf("y_tm", [128, 1024], BF16)[:, :]
    xn_tmb = y_tm
    w8 = sbuf("w8", [128, NTC * E], F32)[:, :]
    w8b = sbuf("w8b", [128, NTC * E], BF16)[:, :]
    w8t = sbuf("w8t", [8, TILE], BF16)[:, :]
    a_m = [sbuf(f"a_m{i}", [128, 512], BF16)[:, :] for i in range(4)]
    elu1 = sbuf("elu1", [128, 1024], F32)[:, :]
    sg_sb = [sbuf(f"sg{i}", [128, 512], BF16)[:, :] for i in range(2)]
    xn_tm = sbuf("xn_tm", [128, D], F32)[:, :]
    x_tc = sbuf("x_tc", [128, D], F32)[:, :]
    bn12 = sbuf("bn12", [128, 12], F32)[:, :]
    mv = sbuf("mv", [128, 4 * NTC], F32)[:, :]
    rr = sbuf("rr", [128, 2 * NTC], F32)[:, :]
    nm = sbuf("nm", [128, 2 * NTC], F32)[:, :]
    neg = sbuf("neg", [128, 2], F32)[:, :]
    lnt = sbuf("lnt", [128, 4], F32)[:, :]
    gt = sbuf("gt", [128, 8 * NTC], F32)[:, :]
    gtp = sbuf("gtp", [128, 8 * NTC], F32)[:, :]
    gsc = sbuf("gsc", [128, 24], F32)[:, :]
    gl_sb = sbuf("gl_sb", [128, NTC * E], F32)[:, :]
    d4 = sbuf("d4", [128, 8], F32)[:, :]
    spc = sbuf("spc", [128, 8], F32)[:, :]
    cst = sbuf("cst", [128, 4], F32)[:, :]
    r4 = sbuf("r4", [128, 8], F32)[:, :]

    q_f = qk_ar[:, 0:4096]
    k_f = qk_ar[:, 4096:8192]
    us_sb = qk_ar[:, 0:6144]
    wb_sb2 = [qk_ar[:, 6144:6656], qk_ar[:, 6656:7168]]
    xn2_f = u_ar[:, 0:4096]
    u_slot = [u_ar[:, 3072 * i:3072 * (i + 1)] for i in range(3)]

    pf = [psumt(f"pf{i}", [128, 512], F32)[:, :] for i in range(6)]
    pt = [psumt(f"pt{i}", [128, 1024], BF16)[:, :] for i in range(2)]

    SEMS = []

    def S(name):
        s = Sem(name)
        SEMS.append(s)
        return s
    s_cast = S("s_cast")
    s_cast1 = S("s_cast1")
    s_res = S("s_res")
    s_cres = S("s_cres")
    s_x = S("s_x")
    s_xe = S("s_xe")
    s_out = S("s_out")
    s_ring = [S(f"s_ring{i}") for i in range(NSLOT)]
    s_ring0 = [S(f"s_ring0_{i}") for i in range(NSLOT)]
    s_castw = [S(f"s_castw{i}") for i in range(4)]
    t_m = S("t_m")
    v_m = S("v_m")
    c_m = S("c_m")

    def ring_items(t):
        items = [
            (f"qhA{t}", wqk_s[:, 0:512].rearrange("(kc p) m -> p kc m", p=128), 8, 512,
             (wqk_d[:, 0:512].rearrange("(kc p) m -> p kc m", p=128), None) if t == 0 else None),
            (f"qhB{t}", wqk_s[:, 512:1024].rearrange("(kc p) m -> p kc m", p=128), 8, 512,
             (wqk_d[:, 512:1024].rearrange("(kc p) m -> p kc m", p=128), None) if t == 0 else None),
            (f"khA{t}", wqk_s[:, 1024:1536].rearrange("(kc p) m -> p kc m", p=128), 8, 512,
             (wqk_d[:, 1024:1536].rearrange("(kc p) m -> p kc m", p=128), None) if t == 0 else None),
            (f"khB{t}", wqk_s[:, 1536:2048].rearrange("(kc p) m -> p kc m", p=128), 8, 512,
             (wqk_d[:, 1536:2048].rearrange("(kc p) m -> p kc m", p=128), None) if t == 0 else None),
            (f"wvA{t}", wv_s[:, 0:512].rearrange("(kc p) m -> p kc m", p=128), 8, 512,
             (wv_d[:, 0:512].rearrange("(kc p) m -> p kc m", p=128), None) if t == 0 else None),
            (f"wvB{t}", wv_s[:, 512:1024].rearrange("(kc p) m -> p kc m", p=128), 8, 512,
             (wv_d[:, 512:1024].rearrange("(kc p) m -> p kc m", p=128), None) if t == 0 else None),
        ]
        for qq in range(4):
            items.append((f"s1_{t}_{qq}",
                          ws1_s[:, 768 * qq:768 * (qq + 1)].rearrange("(kc p) m -> p kc m", p=128),
                          8, 768,
                          (ws1_d[:, 768 * qq:768 * (qq + 1)].rearrange("(kc p) m -> p kc m", p=128),
                           None) if t == 0 else None))
        for w in range(4):
            for ee in range(2):
                e = 2 * w + ee
                src32 = (we1_d[e, :, 0:768].rearrange("(kc p) m -> p kc m", p=128),
                         ("we1", e)) if t == 0 else None
                items.append((f"e1a_{t}_{e}",
                              we1_s[e, :, 0:768].rearrange("(kc p) m -> p kc m", p=128), 8, 768, src32))
                src32 = (we1_d[e, :, 768:1536].rearrange("(kc p) m -> p kc m", p=128),
                         None) if t == 0 else None
                items.append((f"e1b_{t}_{e}",
                              we1_s[e, :, 768:1536].rearrange("(kc p) m -> p kc m", p=128), 8, 768, src32))
            for ee in range(2):
                e = 2 * w + ee
                src32 = (we2_d[e].rearrange("(kc p) m -> p kc m", p=128),
                         ("we2", e)) if t == 0 else None
                items.append((f"e2_{t}_{e}",
                              we2_s[e].rearrange("(kc p) m -> p kc m", p=128), 6, 1024, src32))
        return items

    FLUSH_PLAN = []

    def emit(ctx):
        if ctx.dry:
            FLUSH_PLAN.clear()
        flush_i = [0]
        ring_q = []
        slot_of = {}
        slot_prev = [None] * NSLOT
        loads_in_slot = [0] * NSLOT
        loads0_in_slot = [0] * NSLOT
        wait_cnt = {}
        cast_done = set()
        cast_marks = {}
        next_slot = [0]
        for t in range(n_tiles):
            ring_q.extend(ring_items(t))

        def ring_flush():
            todo = []
            if not ctx.dry:
                quota = FLUSH_PLAN[flush_i[0]]
                flush_i[0] += 1
            while ring_q:
                if ctx.dry:
                    key, src, kc, m, src32 = ring_q[0]
                    slot = next_slot[0] % NSLOT
                    prev = slot_prev[slot]
                    if prev is not None and \
                            ("t_m", f"use:{prev}") not in ctx.marked:
                        break
                else:
                    if len(todo) >= quota:
                        break
                    key, src, kc, m, src32 = ring_q[0]
                    slot = next_slot[0] % NSLOT
                    prev = slot_prev[slot]
                ring_q.pop(0)
                next_slot[0] += 1
                slot_of[key] = slot
                slot_prev[slot] = key
                if src32 is not None:
                    loads0_in_slot[slot] += 1
                    wait_cnt[key] = (s_ring0[slot], 16 * loads0_in_slot[slot])
                else:
                    loads_in_slot[slot] += 1
                    wait_cnt[key] = (s_ring[slot], 16 * loads_in_slot[slot])
                todo.append((key, src, kc, m, slot, prev, src32))
            if ctx.dry:
                FLUSH_PLAN.append(len(todo))
            if not todo:
                return
            sw = [x for x in todo if x[6] is not None]
            hw = [x for x in todo if x[6] is None]

            if sw:
                def gbody(g):
                    for key, src, kc, m, slot, prev, src32 in sw:
                        if prev is not None:
                            g.wait(t_m, f"use:{prev}")
                        g.op("dma_start",
                             out=ring[:, SLOT * slot:SLOT * slot + kc * m]
                             .rearrange("p (kc m) -> p kc m", kc=kc),
                             in_=src32[0], sem=s_ring0[slot], semk=16)
                        if src32[1] is not None:
                            kind, e = src32[1]
                            w = e // 2
                            if kind == "we1":
                                g.op("dma_start", out=we1_s[e], in_=we1_d[e],
                                     sem=s_castw[w], semk=16)
                            else:
                                g.op("dma_start", out=we2_s[e], in_=we2_d[e],
                                     sem=s_castw[w], semk=16)
                            cast_done.add((kind, e))
                            if sum(1 for k2, e2 in cast_done
                                   if e2 // 2 == w) == 4:
                                g.mark(s_castw[w], "full")
                ctx.section("gpsimd", gbody)

            if hw:
                def body(s):
                    for key, src, kc, m, slot, prev, src32 in hw:
                        if key.startswith(("qh", "kh")):
                            s.wait(s_cast1, "qk")
                        elif key.startswith("wv"):
                            s.wait(s_cast1, "v")
                        elif key.startswith("s1"):
                            s.wait(s_cast1, "front")
                        else:
                            e = int(key.split("_")[-1])
                            s.wait(s_castw[e // 2], "full")
                        if prev is not None:
                            s.wait(t_m, f"use:{prev}")
                        s.op("dma_start",
                             out=ring[:, SLOT * slot:SLOT * slot + kc * m]
                             .rearrange("p (kc m) -> p kc m", kc=kc),
                             in_=src, sem=s_ring[slot], semk=16)
                ctx.section("sync", body)

        def rview(key, kc, m):
            return ring[:, SLOT * slot_of[key]:SLOT * slot_of[key] + kc * m]

        def rwait(eng, key):
            sem, val = wait_cnt[key]
            eng.wait(sem, val=val)

        ps_free = {}
        uslot_consumer = {}
        sg_reader = {}

        def ps_wait(eng, idx):
            if idx in ps_free:
                sem, key = ps_free[idx]
                eng.wait(sem, key)

        def ps_set_free(idx, sem, key):
            ps_free[idx] = (sem, key)

        def pro_gp(g):
            g.op("dma_start", out=wg_sb.rearrange("p (kc m) -> p kc m", kc=8),
                 in_=wg_d.rearrange("(kc p) m -> p kc m", p=128), sem=s_res, semk=16)
            g.op("dma_start", out=wo_sb.rearrange("p (kc m) -> p kc m", kc=8),
                 in_=wo_d.rearrange("(kc p) m -> p kc m", p=128), sem=s_res, semk=16)
            g.op("dma_start", out=ws2_sb.rearrange("p (kc m) -> p kc m", kc=12),
                 in_=ws2_d.rearrange("(kc p) m -> p kc m", p=128), sem=s_res, semk=16)
            g.mark(s_res, "all")
        ctx.section("gpsimd", pro_gp)

        def front_casts(g):
            g.op("dma_start", out=wqk_s, in_=wqk_d, sem=s_cast1, semk=16)
            g.mark(s_cast1, "qk")
            g.wait(s_cast1, "qk")
            g.op("dma_start", out=wv_s, in_=wv_d, sem=s_cast1, semk=16)
            g.mark(s_cast1, "v")
            g.wait(s_cast1, "v")
            g.op("dma_start", out=ws1_s, in_=ws1_d, sem=s_cast1, semk=16)
            g.mark(s_cast1, "front")

        def pro_sync(s):
            s.op("dma_start", out=ident, in_=ident_d, sem=s_cres, semk=16)
            s.op("dma_start", out=mask4, in_=mask_d, sem=s_cres, semk=16)
            s.op("dma_start", out=sel, in_=sel_d, sem=s_cres, semk=16)
            s.op("dma_start",
                 out=outs2[0].rearrange("p (tc d) -> p tc d", tc=NTC),
                 in_=x_d[0:TILE, :].rearrange("(tc p) d -> p tc d", p=128),
                 sem=s_x, semk=16)
            s.mark(s_cres, "all")
            for tc in range(NTC):
                s.mark(s_x, f"x0:{tc}")
        ctx.section("sync", pro_sync)

        def pro_vec(v):
            v.op("memset", spc, 0.0, sem=None)
            v.op("memset", cst[:, 0:1], EPS, sem=None)
            v.op("memset", cst[:, 1:2], -0.5, sem=None)
            v.op("memset", cst[:, 2:3], -1.0, sem=None)
            v.op("memset", cst[:, 3:4], 0.5, sem=v_m)
            v.mark(v_m, "cst")
            for i in range(4):
                v.op("memset",
                     v_sb[i].rearrange("p (h c) -> p h c", c=65)[:, :, 64:65],
                     1.0, sem=v_m)
            v.mark(v_m, "vones")
        ctx.section("vector", pro_vec)

        ring_flush()
        ctx.section("gpsimd", front_casts)

        def early_ln(t, tc):
            def e_v1(v, t=t, tc=tc):
                xsrc = outs2[t % 2][:, D * tc:D * (tc + 1)]
                v.wait(s_x, f"x{t}:{tc}")
                if not (t == 0 and tc == 0):
                    pk = (t, tc - 1) if tc > 0 else (t - 1, NTC - 1)
                    v.wait(v_m, f"eln1s:{pk[0]}:{pk[1]}")
                v.op("bn_stats", bn12[:, 0:6], xsrc[:, 0:512],
                     outs=[bn12[:, 0:6]], ins=[])
                v.op("bn_stats", bn12[:, 6:12], xsrc[:, 512:1024],
                     outs=[bn12[:, 6:12]], ins=[], sem=v_m)
                v.mark(v_m, f"eln1b:{t}:{tc}")
                v.wait(v_m, f"eln1b:{t}:{tc}")
                v.op("bn_aggr", mv[:, 2 * tc:2 * tc + 2], bn12,
                     outs=[mv[:, 2 * tc:2 * tc + 2]], ins=[], sem=v_m,
                     allow_hazard=True)
                v.mark(v_m, f"eln1s:{t}:{tc}")
            ctx.section("vector", e_v1)

            def e_c1(c, t=t, tc=tc):
                c.wait(v_m, "cst")
                c.wait(v_m, f"eln1s:{t}:{tc}")
                c.op("mul", neg[:, 0:1], mv[:, 2 * tc:2 * tc + 1], cst[:, 2:3],
                     sem=c_m)
                c.op("activation", lnt[:, 0:1], mv[:, 2 * tc + 1:2 * tc + 2],
                     AF.Ln, bias=cst[:, 0:1], sem=c_m)
                c.mark(c_m, f"elnv1:{t}:{tc}")
                c.wait(c_m, f"elnv1:{t}:{tc}")
                c.op("activation", rr[:, tc:tc + 1], lnt[:, 0:1], AF.Exp,
                     scale=cst[:, 1:2], sem=c_m)
                c.mark(c_m, f"elnr1:{t}:{tc}")
                c.wait(c_m, f"elnr1:{t}:{tc}")
                c.op("mul", nm[:, tc:tc + 1], neg[:, 0:1], rr[:, tc:tc + 1],
                     sem=c_m)
                c.op("mul", lnt[:, 1:2], mv[:, 2 * tc + 1:2 * tc + 2],
                     rr[:, tc:tc + 1], sem=c_m)
                c.mark(c_m, f"elnn1:{t}:{tc}")
                c.wait(c_m, f"elnn1:{t}:{tc}")
                c.op("mul", lnt[:, 2:3], lnt[:, 1:2], rr[:, tc:tc + 1],
                     sem=c_m)
                c.mark(c_m, f"elnu:{t}:{tc}")
                c.wait(c_m, f"elnu:{t}:{tc}")
                c.op("activation", lnt[:, 1:2], lnt[:, 2:3],
                     AF.Ln, bias=cst[:, 0:1], sem=c_m)
                c.mark(c_m, f"elnv2:{t}:{tc}")
                c.wait(c_m, f"elnv2:{t}:{tc}")
                c.op("activation", lnt[:, 2:3], lnt[:, 1:2], AF.Exp,
                     scale=cst[:, 1:2], sem=c_m)
                c.mark(c_m, f"elnr2:{t}:{tc}")
                c.wait(c_m, f"elnr2:{t}:{tc}")
                c.op("mul", rr[:, NTC + tc:NTC + tc + 1], rr[:, tc:tc + 1],
                     lnt[:, 2:3], sem=c_m)
                c.op("mul", nm[:, NTC + tc:NTC + tc + 1], nm[:, tc:tc + 1],
                     lnt[:, 2:3], sem=c_m)
                c.mark(c_m, f"elnc:{t}:{tc}")
            ctx.section("scalar", e_c1)

        def ln_stage(tc):
            if tc % 2 == 0:
                return ((xn_tmb[:, 0:512], xn_tmb[:, 512:1024]),
                        (xn2_tmb[:, 0:512], xn2_tmb[:, 512:1024]))
            return ((a_m[0], a_m[1]), (a_m[2], a_m[3]))

        def emit_ap_c1(t, tc):
            def ap_c1(c, t=t, tc=tc):
                out_b = outs2[t % 2]
                (x0, x1), (y0, y1) = ln_stage(tc)
                c.wait(s_x, f"x{t}:{tc}")
                c.wait(c_m, f"elnc:{t}:{tc}")
                if t > 0 or tc > 1:
                    pk = (t, tc - 2) if tc > 1 else (t - 1, NTC - 2 + tc)
                    c.wait(t_m, f"xtrd:{pk[0]}:{pk[1]}")
                    c.wait(t_m, f"x2trd:{pk[0]}:{pk[1]}")
                    if tc <= 1:
                        c.wait(t_m, f"ytr:{t-1}:{NTC-1}")
                        c.wait(t_m, f"szmm:{t-1}:{NTC-1}")
                        if tc == 1:
                            c.wait(t_m, f"omm:{t-1}:{NTC-1}:3")
                for half, dst in ((0, x0), (1, x1)):
                    c.op("activation", dst,
                         out_b[:, D * tc + 512 * half:D * tc + 512 * half + 512],
                         AF.Identity, bias=nm[:, tc:tc + 1],
                         scale=rr[:, tc:tc + 1],
                         sem=(c_m if half == 1 else None))
                c.mark(c_m, f"xnbf:{t}:{tc}")
                for half, dst in ((0, y0), (1, y1)):
                    c.op("activation", dst,
                         out_b[:, D * tc + 512 * half:D * tc + 512 * half + 512],
                         AF.Identity, bias=nm[:, NTC + tc:NTC + tc + 1],
                         scale=rr[:, NTC + tc:NTC + tc + 1],
                         sem=(c_m if half == 1 else None))
                c.mark(c_m, f"xn2bf:{t}:{tc}")
            ctx.section("scalar", ap_c1)

        def go(name):
            if STOP_AFTER is None:
                return True
            sa = "att" if STOP_AFTER in _SUBATT else STOP_AFTER
            return _PHASES.index(name) <= _PHASES.index(sa)

        for tc0 in range(NTC):
            early_ln(0, tc0)

        for t in range(n_tiles):
            tis = t % tps
            tile_done = [None]
            out_sb = outs2[t % 2]

            for tc in range(NTC) if go("ln") else []:
                if not (t > 0 and tc <= 1 and STOP_AFTER is None):
                    emit_ap_c1(t, tc)

                def ap_tr(tn, t=t, tc=tc):
                    tn.wait(s_cres, "all")
                    (x0, x1), (y0, y1) = ln_stage(tc)
                    for half, (bufname, srcs, mk) in enumerate((
                            ("x", (x0, x1), f"xnbf:{t}:{tc}"),
                            ("x2", (y0, y1), f"xn2bf:{t}:{tc}"))):
                        tn.wait(c_m, mk)
                        ps_wait(tn, 6 + half)
                        for kc in range(8):
                            src = srcs[kc // 4]
                            tn.op("matmul", pt[half][:, 128 * kc:128 * (kc + 1)],
                                  src[:, 128 * (kc % 4):128 * (kc % 4) + 128], ident,
                                  start=True, stop=True, is_transpose=True,
                                  sem=(t_m if kc == 7 else None))
                        tn.mark(t_m, f"{bufname}tr:{t}:{tc}")
                        ps_set_free(6 + half, v_m, f"{bufname}trev:{t}:{tc}")
                        tn.mark(t_m, f"{bufname}trd:{t}:{tc}")
                ctx.section("tensor", ap_tr)

                def ap_trev(v, t=t, tc=tc):
                    for half, bufname in enumerate(("x", "x2")):
                        dstt = xn_f if bufname == "x" else xn2_f
                        v.wait(t_m, f"{bufname}tr:{t}:{tc}")
                        dv = dstt.rearrange("p (kc m) -> p kc m", kc=8) \
                            [:, 0:8, 128 * tc:128 * tc + 128]
                        v.op("tensor_copy", dv,
                             pt[half].rearrange("p (j m) -> p j m", j=8),
                             outs=[dstt], ins=[], sem=v_m)
                        v.mark(v_m, f"{bufname}trev:{t}:{tc}")
                ctx.section("vector", ap_trev)

            def gate_mm(tn, t=t):
                tn.wait(s_res, "all")
                ps_wait(tn, 4)
                for tc in range(NTC):
                    tn.wait(v_m, f"xtrev:{t}:{tc}")
                    for kc in range(8):
                        tn.op("matmul", pf[4][:, 8 * tc:8 * tc + 8],
                              xn_f[:, 512 * kc + 128 * tc:512 * kc + 128 * tc + 128],
                              wg_sb[:, 8 * kc:8 * kc + 8],
                              start=(kc == 0), stop=(kc == 7),
                              sem=(t_m if kc == 7 else None))
                    tn.mark(t_m, f"gmm:{t}:{tc}")
                ps_set_free(4, v_m, f"glev:{t}")
            if go("gate"):
                ctx.section("tensor", gate_mm)

            def gate_v1(v, t=t):
                v.wait(t_m, f"gmm:{t}:{NTC - 1}")
                v.op("tensor_copy", gl_sb, pf[4][:, 0:NTC * E],
                     outs=[gl_sb], ins=[], sem=v_m)
                v.mark(v_m, f"glev:{t}")
                v.wait(v_m, f"glev:{t}")
                for tc in range(NTC):
                    v.op("reduce_max", gsc[:, tc:tc + 1], gl_sb[:, 8 * tc:8 * tc + 8],
                         axis=AX.X, outs=[gsc[:, tc:tc + 1]], ins=[],
                         sem=(v_m if tc == NTC - 1 else None), allow_hazard=True)
                v.mark(v_m, f"grm:{t}")
                v.wait(v_m, f"grm:{t}")
                for tc in range(NTC):
                    v.op("tensor_scalar", gt[:, 8 * tc:8 * tc + 8],
                         gl_sb[:, 8 * tc:8 * tc + 8], gsc[:, tc:tc + 1], None,
                         ALU.subtract, outs=[gt[:, 8 * tc:8 * tc + 8]],
                         ins=[gsc[:, tc:tc + 1]], sem=v_m, allow_hazard=True)
                    v.mark(v_m, f"gsh:{t}:{tc}")
            if go("gate"):
                ctx.section("vector", gate_v1)

            def gate_c1(c, t=t):
                for tc in range(NTC):
                    c.wait(v_m, f"gsh:{t}:{tc}")
                    c.op("activation", gtp[:, 8 * tc:8 * tc + 8], gt[:, 8 * tc:8 * tc + 8],
                         AF.Exp, accum_out=gsc[:, 4 + tc:5 + tc], sem=c_m)
                    c.mark(c_m, f"gexp:{t}:{tc}")
            if go("gate"):
                ctx.section("scalar", gate_c1)

            def gate_v2(v, t=t):
                for tc in range(NTC):
                    v.wait(c_m, f"gexp:{t}:{tc}")
                    v.op("reciprocal", gsc[:, 16 + tc:17 + tc], gsc[:, 4 + tc:5 + tc],
                         outs=[gsc[:, 16 + tc:17 + tc]], ins=[gsc[:, 4 + tc:5 + tc]],
                         sem=v_m, allow_hazard=(tc > 0))
                    v.mark(v_m, f"grp:{t}:{tc}")
            if go("gate"):
                ctx.section("vector", gate_v2)

            def gate_c2(c, t=t):
                for tc in range(NTC):
                    c.wait(v_m, f"grp:{t}:{tc}")
                    c.wait(c_m, f"gexp:{t}:{tc}")
                    c.op("activation", w8[:, 8 * tc:8 * tc + 8], gtp[:, 8 * tc:8 * tc + 8],
                         AF.Identity, scale=gsc[:, 16 + tc:17 + tc], sem=c_m)
                    c.mark(c_m, f"gpr:{t}:{tc}")
            if go("gate"):
                ctx.section("scalar", gate_c2)

            def gate_v3(v, t=t):
                def loop(tag, fn, n=NTC):
                    for tc in range(n):
                        fn(tc, tc == n - 1)
                    v.mark(v_m, f"{tag}:{t}")
                    v.wait(v_m, f"{tag}:{t}")
                for tc in range(NTC):
                    v.wait(c_m, f"gpr:{t}:{tc}")
                loop("g_m1", lambda tc, last: v.op(
                    "reduce_max", gsc[:, 8 + tc:9 + tc], w8[:, 8 * tc:8 * tc + 8],
                    axis=AX.X, outs=[gsc[:, 8 + tc:9 + tc]], ins=[],
                    sem=(v_m if last else None)))
                loop("g_eq", lambda tc, last: v.op(
                    "tensor_scalar", gt[:, 8 * tc:8 * tc + 8],
                    w8[:, 8 * tc:8 * tc + 8], gsc[:, 8 + tc:9 + tc], None,
                    ALU.is_ge, outs=[gt[:, 8 * tc:8 * tc + 8]],
                    ins=[gsc[:, 8 + tc:9 + tc]],
                    sem=(v_m if last else None), allow_hazard=True))
                loop("g_stt", lambda tc, last: v.op(
                    "scalar_tensor_tensor", gt[:, 8 * tc:8 * tc + 8],
                    gt[:, 8 * tc:8 * tc + 8], NEG_BIG, w8[:, 8 * tc:8 * tc + 8],
                    ALU.mult, ALU.add, outs=[gt[:, 8 * tc:8 * tc + 8]], ins=[],
                    sem=(v_m if last else None), allow_hazard=True))
                loop("g_m2", lambda tc, last: v.op(
                    "reduce_max", gsc[:, 12 + tc:13 + tc], gt[:, 8 * tc:8 * tc + 8],
                    axis=AX.X, outs=[gsc[:, 12 + tc:13 + tc]], ins=[],
                    sem=(v_m if last else None), allow_hazard=True))
                loop("g_msk", lambda tc, last: v.op(
                    "tensor_scalar", gt[:, 8 * tc:8 * tc + 8],
                    w8[:, 8 * tc:8 * tc + 8], gsc[:, 12 + tc:13 + tc], None,
                    ALU.is_ge, outs=[gt[:, 8 * tc:8 * tc + 8]],
                    ins=[gsc[:, 12 + tc:13 + tc]],
                    sem=(v_m if last else None), allow_hazard=True))
                loop("g_w8", lambda tc, last: v.op(
                    "tensor_mul", w8[:, 8 * tc:8 * tc + 8],
                    gt[:, 8 * tc:8 * tc + 8], w8[:, 8 * tc:8 * tc + 8],
                    outs=[w8[:, 8 * tc:8 * tc + 8]],
                    ins=[gt[:, 8 * tc:8 * tc + 8]],
                    sem=(v_m if last else None), allow_hazard=True))
                for tc in range(NTC):
                    v.op("tensor_copy", w8b[:, 8 * tc:8 * tc + 8],
                         w8[:, 8 * tc:8 * tc + 8],
                         outs=[w8b[:, 8 * tc:8 * tc + 8]],
                         ins=[w8[:, 8 * tc:8 * tc + 8]], sem=v_m,
                         allow_hazard=True)
                    v.mark(v_m, f"gw8b:{t}:{tc}")
            if go("gate"):
                ctx.section("vector", gate_v3)


            def qkv_v1(v, fcg, t=t):
                bank = fcg % 4
                v.wait(t_m, f"qkmm:{t}:{fcg}")
                if fcg >= 2:
                    v.wait(v_m, f"eluv2:{t}:{fcg - 2}")
                v.op("tensor_scalar_min",
                     elu1[:, 512 * (fcg % 2):512 * (fcg % 2) + 512],
                     pf[bank], 0.0,
                     outs=[elu1[:, 512 * (fcg % 2):512 * (fcg % 2) + 512]],
                     ins=[], sem=v_m)
                v.mark(v_m, f"eluv1:{t}:{fcg}")

            def qkv_v2(v, fcg, t=t):
                bank = fcg % 4
                v.wait(c_m, f"eluexp:{t}:{fcg}")
                dst = q_f if fcg < 8 else k_f
                fc = fcg % 8
                v.op("scalar_tensor_tensor", dst[:, 512 * fc:512 * (fc + 1)],
                     pf[bank], 0.0,
                     elu1[:, 512 * (fcg % 2):512 * (fcg % 2) + 512],
                     ALU.max, ALU.add,
                     outs=[dst[:, 512 * fc:512 * (fc + 1)]],
                     ins=[elu1[:, 512 * (fcg % 2):512 * (fcg % 2) + 512]],
                     sem=v_m)
                v.mark(v_m, f"eluv2:{t}:{fcg}")

            for g in range(8) if go("qkv") else []:
                def qkv_mm(tn, t=t, g=g):
                    if g == 0:
                        tn.wait(v_m, f"x2trev:{t}:{NTC - 1}")
                    for fcg in (2 * g, 2 * g + 1):
                        key = [f"qhA{t}", f"qhB{t}", f"khA{t}", f"khB{t}"][fcg // 4]
                        rwait(tn, key)
                        wsrc = rview(key, 8, 512)
                        fc = fcg % 4
                        bank = fcg % 4
                        ps_wait(tn, bank)
                        for kc in range(8):
                            tn.op("matmul", pf[bank],
                                  wsrc[:, 512 * kc + 128 * fc:512 * kc + 128 * fc + 128],
                                  xn2_f[:, 512 * kc:512 * (kc + 1)],
                                  start=(kc == 0), stop=(kc == 7),
                                  sem=(t_m if kc == 7 else None))
                        tn.mark(t_m, f"qkmm:{t}:{fcg}")
                        ps_set_free(bank, v_m, f"eluv2:{t}:{fcg}")
                        if fcg % 4 == 3:
                            tn.mark(t_m, f"use:{key}")
                ctx.section("tensor", qkv_mm)

                def qkv_vv(v, t=t, g=g):
                    if g > 0:
                        qkv_v2(v, 2 * g - 2)
                        qkv_v2(v, 2 * g - 1)
                    qkv_v1(v, 2 * g)
                    qkv_v1(v, 2 * g + 1)
                ctx.section("vector", qkv_vv)

                def qkv_cc(c, t=t, g=g):
                    for fcg in (2 * g, 2 * g + 1):
                        c.wait(v_m, f"eluv1:{t}:{fcg}")
                        if fcg >= 2:
                            c.wait(v_m, f"eluv2:{t}:{fcg - 2}")
                        c.op("activation",
                             elu1[:, 512 * (fcg % 2):512 * (fcg % 2) + 512],
                             elu1[:, 512 * (fcg % 2):512 * (fcg % 2) + 512],
                             AF.Exp, sem=c_m)
                        c.mark(c_m, f"eluexp:{t}:{fcg}")
                ctx.section("scalar", qkv_cc)

            if go("qkv"):
                def qkv_fin(v, t=t):
                    qkv_v2(v, 14)
                    qkv_v2(v, 15)
                ctx.section("vector", qkv_fin)
                tile_done[0] = (v_m, f"eluv2:{t}:15")

            ring_flush()
            for tc in range(NTC) if go("qkv") else []:
                def vprod_mm(tn, t=t, tc=tc):
                    tn.wait(v_m, "vones")
                    for half in range(2):
                        key = f"wvA{t}" if half == 0 else f"wvB{t}"
                        rwait(tn, key)
                        wvv = rview(key, 8, 512)
                        bank = (16 + 2 * tc + half) % 4
                        ps_wait(tn, bank)
                        for kc in range(8):
                            tn.op("matmul", pf[bank],
                                  xn2_f[:, 512 * kc + 128 * tc:512 * kc + 128 * tc + 128],
                                  wvv[:, 512 * kc:512 * kc + 512],
                                  start=(kc == 0), stop=(kc == 7),
                                  sem=(t_m if kc == 7 else None))
                        tn.mark(t_m, f"vmm:{t}:{tc}:{half}")
                        ps_set_free(bank, v_m, f"vev:{t}:{tc}:{half}")
                        if tc == NTC - 1:
                            tn.mark(t_m, f"use:{key}")
                ctx.section("tensor", vprod_mm)

                def vprod_ev(v, t=t, tc=tc):
                    vb = v_sb[tc]
                    if t > 0:
                        v.wait(t_m, f"szmm:{t - 1}:{tc}")
                    for half in range(2):
                        v.wait(t_m, f"vmm:{t}:{tc}:{half}")
                        dv = vb[:, 65 * 8 * half:65 * 8 * half + 520] \
                            .rearrange("p (h c) -> p h c", c=65)[:, :, 0:64]
                        v.op("tensor_copy", dv,
                             pf[(16 + 2 * tc + half) % 4].rearrange("p (h c) -> p h c", c=64),
                             outs=[vb], ins=[], sem=v_m)
                        v.mark(v_m, f"vev:{t}:{tc}:{half}")
                ctx.section("vector", vprod_ev)

            ring_flush()
            if STOP_AFTER is not None and STOP_AFTER in _SUBATT:
                _attn = 1
            elif STOP_AFTER is not None and STOP_AFTER.startswith("att") and len(STOP_AFTER) == 5 and STOP_AFTER[3].isdigit():
                _attn = int(STOP_AFTER[3])
            elif go("att"):
                _attn = NTC
            else:
                _attn = 0

            def ktr_mms(tn, t, tc):
                ps_wait(tn, 6)
                for hc in range(8):
                    tn.op("matmul", pt[0][:, 128 * hc:128 * (hc + 1)],
                          k_f[:, 512 * hc + 128 * tc:512 * hc + 128 * tc + 128],
                          ident, start=True, stop=True, is_transpose=True,
                          sem=(t_m if hc == 7 else None))
                tn.mark(t_m, f"ktr:{t}:{tc}")
                ps_set_free(6, v_m, f"ktrev:{t}:{tc}")

            def ktrev(v, t, tc):
                v.wait(t_m, f"ktr:{t}:{tc}")
                v.op("tensor_copy", k_tm, pt[0],
                     outs=[k_tm], ins=[], sem=v_m)
                v.mark(v_m, f"ktrev:{t}:{tc}")

            def a_round(tn, rnd, t, tc):
                ps_wait(tn, 0)
                ps_wait(tn, 1)
                for i in range(8):
                    h = 8 * rnd + i
                    fc = h // 2
                    off = 64 * (h % 2)
                    bank = i % 2
                    blk = i // 2
                    tn.op("matmul", pf[bank][:, 128 * blk:128 * (blk + 1)],
                          k_f[off:off + 64,
                              512 * fc + 128 * tc:512 * fc + 128 * tc + 128],
                          q_f[off:off + 64,
                              512 * fc + 128 * tc:512 * fc + 128 * tc + 128],
                          start=True, stop=True,
                          sem=(t_m if i >= 6 else None))
                tn.mark(t_m, f"ammr:{t}:{tc}:{rnd}")
                ps_set_free(0, v_m, f"amevr:{t}:{tc}:{rnd}")
                ps_set_free(1, v_m, f"amevr:{t}:{tc}:{rnd}")

            def am_round(v, rnd, t, tc):
                v.wait(t_m, f"ammr:{t}:{tc}:{rnd}")
                for par in range(2):
                    v.op("tensor_tensor",
                         a_m[2 * rnd + par].rearrange("p (r m) -> p r m", r=4),
                         pf[par].rearrange("p (r m) -> p r m", r=4),
                         mask4.unsqueeze(1).broadcast_to([128, 4, 128]), ALU.mult,
                         outs=[a_m[2 * rnd + par]], ins=[],
                         sem=(v_m if par == 1 else None))
                v.mark(v_m, f"amevr:{t}:{tc}:{rnd}")

            def o_pair(tn, pair, t, tc, cq):
                vb = v_sb[tc]
                rnd = pair
                tn.wait(v_m, f"amevr:{t}:{tc}:{rnd}")
                tn.wait(v_m, f"vev:{t}:{tc}:0")
                tn.wait(v_m, f"vev:{t}:{tc}:1")
                if cq > 0:
                    pk = (t, tc - 1) if tc > 0 else (t - 1, NTC - 1)
                    tn.wait(v_m, f"szbf:{pk[0]}:{pk[1]}")
                for qq in range(2):
                    q4 = 2 * pair + qq
                    bO = 2 + 2 * pair + qq
                    ps_wait(tn, bO)
                    for j in range(4):
                        h = 4 * q4 + j
                        fc = h // 2
                        off = 64 * (h % 2)
                        am = a_m[2 * rnd + h % 2]
                        blk = (h - 8 * rnd) // 2
                        tn.op("matmul", pf[bO][:, 65 * j:65 * j + 65],
                              am[:, 128 * blk:128 * (blk + 1)],
                              vb[:, 65 * h:65 * h + 65],
                              start=True, stop=(cq == 0),
                              sem=(t_m if (cq == 0 and j == 3) else None))
                        if cq > 0:
                            tn.op("matmul", pf[bO][:, 65 * j:65 * j + 65],
                                  q_f[off:off + 64,
                                      512 * fc + 128 * tc:512 * fc + 128 * tc + 128],
                                  sz_bf[off:off + 64,
                                        65 * (h // 2):65 * (h // 2) + 65],
                                  start=False, stop=True,
                                  sem=(t_m if j == 3 else None))
                    tn.mark(t_m, f"omm:{t}:{tc}:{q4}")
                    ps_set_free(bO, c_m, f"yscale:{t}:{tc}:{q4}")

            def gath(v, q4, t, tc):
                v.wait(t_m, f"omm:{t}:{tc}:{q4}")
                if q4 >= 2:
                    v.wait(v_m, f"rde:{t}:{tc}:{q4 - 2}")
                elif t > 0 or tc > 0:
                    pk = (t, tc - 1) if tc > 0 else (t - 1, NTC - 1)
                    v.wait(v_m, f"rde:{pk[0]}:{pk[1]}:{q4 + 2}")
                v.op("tensor_copy", d4[:, 4 * (q4 % 2):4 * (q4 % 2) + 4],
                     pf[2 + q4][:, 64:64 + 4 * 65:65],
                     outs=[d4[:, 4 * (q4 % 2):4 * (q4 % 2) + 4]], ins=[],
                     sem=v_m)
                v.mark(v_m, f"gath:{t}:{tc}:{q4}")

            def recip(v, q4, t, tc):
                if q4 >= 2:
                    v.wait(c_m, f"yscale:{t}:{tc}:{q4 - 2}")
                elif t > 0 or tc > 0:
                    pk = (t, tc - 1) if tc > 0 else (t - 1, NTC - 1)
                    v.wait(c_m, f"yscale:{pk[0]}:{pk[1]}:{q4 + 2}")
                v.wait(v_m, f"gath:{t}:{tc}:{q4}")
                v.op("reciprocal", r4[:, 4 * (q4 % 2):4 * (q4 % 2) + 4],
                     d4[:, 4 * (q4 % 2):4 * (q4 % 2) + 4],
                     outs=[r4[:, 4 * (q4 % 2):4 * (q4 % 2) + 4]],
                     ins=[d4[:, 4 * (q4 % 2):4 * (q4 % 2) + 4]],
                     sem=v_m, allow_hazard=True)
                v.mark(v_m, f"rde:{t}:{tc}:{q4}")

            def yscale_pair(c, pair, t, tc):
                for qq in range(2):
                    q4 = 2 * pair + qq
                    c.wait(v_m, f"rde:{t}:{tc}:{q4}")
                    for j in range(4):
                        h = 4 * q4 + j
                        c.op("activation", y_tm[:, 64 * h:64 * h + 64],
                             pf[2 + q4][:, 65 * j:65 * j + 64],
                             AF.Copy,
                             scale=r4[:, 4 * (q4 % 2) + j:4 * (q4 % 2) + j + 1],
                             sem=(c_m if j == 3 else None))
                    c.mark(c_m, f"yscale:{t}:{tc}:{q4}")

            if _attn > 0:
                def att_pre(tn, t=t):
                    tn.wait(v_m, f"eluv2:{t}:15")
                    tn.wait(v_m, f"vev:{t}:0:1")
                    ktr_mms(tn, t, 0)
                    a_round(tn, 0, t, 0)
                ctx.section("tensor", att_pre)

                def att_prev(v, t=t):
                    ktrev(v, t, 0)
                    am_round(v, 0, t, 0)
                ctx.section("vector", att_prev)

            for tc in range(_attn):
                cq = NTC * tis + tc

                def att_t1(tn, t=t, tc=tc, cq=cq):
                    a_round(tn, 1, t, tc)
                ctx.section("tensor", att_t1)

                def att_v1(v, t=t, tc=tc):
                    am_round(v, 1, t, tc)
                ctx.section("vector", att_v1)

                def att_t2(tn, t=t, tc=tc, cq=cq):
                    o_pair(tn, 0, t, tc, cq)
                ctx.section("tensor", att_t2)

                def att_v2(v, t=t, tc=tc):
                    gath(v, 0, t, tc)
                    gath(v, 1, t, tc)
                    recip(v, 0, t, tc)
                    recip(v, 1, t, tc)
                ctx.section("vector", att_v2)

                def att_s1(c, t=t, tc=tc):
                    yscale_pair(c, 0, t, tc)
                ctx.section("scalar", att_s1)

                def att_t3(tn, t=t, tc=tc, cq=cq):
                    o_pair(tn, 1, t, tc, cq)
                ctx.section("tensor", att_t3)

                def att_v3(v, t=t, tc=tc):
                    gath(v, 2, t, tc)
                    gath(v, 3, t, tc)
                    recip(v, 2, t, tc)
                    recip(v, 3, t, tc)
                ctx.section("vector", att_v3)

                def att_s2(c, t=t, tc=tc):
                    yscale_pair(c, 1, t, tc)
                ctx.section("scalar", att_s2)

                def att_t4(tn, t=t, tc=tc, cq=cq):
                    vb = v_sb[tc]
                    tn.wait(v_m, f"ktrev:{t}:{tc}")
                    if cq > 0:
                        pk = (t, tc - 1) if tc > 0 else (t - 1, NTC - 1)
                        tn.wait(v_m, f"szbf:{pk[0]}:{pk[1]}")
                    for half in range(2):
                        bank = 2 + half
                        ps_wait(tn, bank)
                        for pp in range(4):
                            p = 4 * half + pp
                            for par in range(2):
                                h = 2 * p + par
                                tn.op("matmul",
                                      pf[bank][64 * par:64 * par + 64,
                                               65 * pp:65 * pp + 65],
                                      k_tm[:, 128 * p + 64 * par:128 * p + 64 * par + 64],
                                      vb[:, 65 * h:65 * h + 65],
                                      start=True, stop=True,
                                      tile_position=(0, 64 * par),
                                      sem=(t_m if (pp == 3 and par == 1) else None))
                        tn.mark(t_m, f"szd:{t}:{tc}:{half}")
                        ps_set_free(bank, v_m, f"szup:{t}:{tc}:{half}")
                    tn.mark(t_m, f"szmm:{t}:{tc}")
                    if tc + 1 < NTC:
                        ktr_mms(tn, t, tc + 1)
                        a_round(tn, 0, t, tc + 1)
                    tn.wait(c_m, f"yscale:{t}:{tc}:3")
                    ps_wait(tn, 7)
                    for yc in range(8):
                        tn.op("matmul", pt[1][:, 128 * yc:128 * (yc + 1)],
                              y_tm[:, 128 * yc:128 * (yc + 1)], ident,
                              start=True, stop=True, is_transpose=True,
                              sem=(t_m if yc == 7 else None))
                    tn.mark(t_m, f"ytr:{t}:{tc}")
                    ps_set_free(7, v_m, f"ytrev:{t}:{tc}")
                ctx.section("tensor", att_t4)

                def att_v4(v, t=t, tc=tc, cq=cq):
                    if tc + 1 < NTC:
                        ktrev(v, t, tc + 1)
                        am_round(v, 0, t, tc + 1)
                    v.wait(t_m, f"ytr:{t}:{tc}")
                    dv = y_fT.rearrange("p (kc m) -> p kc m", kc=8) \
                        [:, 0:8, 128 * tc:128 * tc + 128]
                    v.op("tensor_copy", dv,
                         pt[1].rearrange("p (j m) -> p j m", j=8),
                         outs=[y_fT], ins=[], sem=v_m)
                    v.mark(v_m, f"ytrev:{t}:{tc}")
                    for half in range(2):
                        v.wait(t_m, f"szd:{t}:{tc}:{half}")
                        dst = sz_m[:, 260 * half:260 * half + 260]
                        if cq == 0:
                            v.op("tensor_copy", dst, pf[2 + half][:, 0:260],
                                 outs=[dst], ins=[], sem=v_m)
                        else:
                            v.op("tensor_tensor", dst, dst, pf[2 + half][:, 0:260],
                                 ALU.add, outs=[dst], ins=[], sem=v_m)
                        v.mark(v_m, f"szup:{t}:{tc}:{half}")
                    v.wait(v_m, f"szup:{t}:{tc}:0")
                    v.wait(v_m, f"szup:{t}:{tc}:1")
                    for half in range(2):
                        v.op("tensor_copy", sz_bf[:, 260 * half:260 * half + 260],
                             sz_m[:, 260 * half:260 * half + 260],
                             outs=[sz_bf[:, 260 * half:260 * half + 260]],
                             ins=[sz_m[:, 260 * half:260 * half + 260]],
                             sem=(v_m if half == 1 else None),
                             allow_hazard=True)
                    v.mark(v_m, f"szbf:{t}:{tc}")
                ctx.section("vector", att_v4)

            ring_flush()
            if _attn > 0:
                if STOP_AFTER in _SUBATT and STOP_AFTER != "attt4":
                    tile_done[0] = {"attt1": (v_m, f"amevr:{t}:0:0"),
                                    "attt2": (v_m, f"rde:{t}:0:1"),
                                    "atts1": (c_m, f"yscale:{t}:0:1"),
                                    "attt3": (v_m, f"rde:{t}:0:3"),
                                    "atts2": (c_m, f"yscale:{t}:0:3")}[STOP_AFTER]
                else:
                    tile_done[0] = (v_m, f"szbf:{t}:{_attn - 1}")

            def gate_tr(tn, t=t):
                ps_wait(tn, 6)
                for tc in range(NTC):
                    tn.wait(v_m, f"gw8b:{t}:{tc}")
                    tn.op("matmul", pt[0][0:8, 128 * tc:128 * (tc + 1)],
                          w8b[:, 8 * tc:8 * tc + 8], ident,
                          start=True, stop=True, is_transpose=True,
                          sem=(t_m if tc == NTC - 1 else None))
                tn.mark(t_m, f"gw8tr:{t}")
                ps_set_free(6, v_m, f"gw8tev:{t}")
            if go("gate"):
                ctx.section("tensor", gate_tr)

            def gate_trev(v, t=t):
                v.wait(t_m, f"gw8tr:{t}")
                v.op("tensor_copy", w8t, pt[0][0:8, 0:512],
                     outs=[w8t], ins=[], sem=v_m)
                v.mark(v_m, f"gw8tev:{t}")
            if go("gate"):
                ctx.section("vector", gate_trev)

            if go("gate"):
                tile_done[0] = (v_m, f"gw8tev:{t}")


            for pj in range(6) if go("ush") else []:
                def ush_mm(tn, t=t, pj=pj):
                    if pj == 0:
                        tn.wait(v_m, f"ytrev:{t}:{NTC - 1}")
                    for j in (2 * pj, 2 * pj + 1):
                        qy, qg = j // 6, 2 + j // 6
                        ky, kg = f"s1_{t}_{qy}", f"s1_{t}_{qg}"
                        rwait(tn, ky)
                        rwait(tn, kg)
                        wy = rview(ky, 8, 768)
                        wgr = rview(kg, 8, 768)
                        hy = j % 6
                        bY, bG = j % 3, 3 + j % 3
                        ps_wait(tn, bY)
                        for kc in range(8):
                            tn.op("matmul", pf[bY],
                                  wy[:, 768 * kc + 128 * hy:768 * kc + 128 * hy + 128],
                                  xn_f[:, 512 * kc:512 * (kc + 1)],
                                  start=(kc == 0), stop=(kc == 7),
                                  sem=(t_m if kc == 7 else None))
                        tn.mark(t_m, f"usy:{t}:{j}")
                        ps_set_free(bY, v_m, f"usu:{t}:{j}")
                        ps_wait(tn, bG)
                        for kc in range(8):
                            tn.op("matmul", pf[bG],
                                  wgr[:, 768 * kc + 128 * hy:768 * kc + 128 * hy + 128],
                                  xn_f[:, 512 * kc:512 * (kc + 1)],
                                  start=(kc == 0), stop=(kc == 7),
                                  sem=(t_m if kc == 7 else None))
                        tn.mark(t_m, f"usg:{t}:{j}")
                        ps_set_free(bG, v_m, f"ustg:{t}:{j // 2}:{j % 2}")
                        if j == 5:
                            tn.mark(t_m, f"use:s1_{t}_0")
                            tn.mark(t_m, f"use:s1_{t}_2")
                        if j == 11:
                            tn.mark(t_m, f"use:s1_{t}_1")
                            tn.mark(t_m, f"use:s1_{t}_3")
                ctx.section("tensor", ush_mm)

                def ush_c(c, t=t, pj=pj):
                    for j in (2 * pj, 2 * pj + 1):
                        c.wait(t_m, f"usg:{t}:{j}")
                        rd = sg_reader.get(j % 2)
                        if rd is not None:
                            c.wait(rd[0], rd[1])
                        c.op("activation", sg_sb[j % 2], pf[3 + j % 3],
                             AF.Silu, sem=c_m)
                        c.mark(c_m, f"ussg:{t}:{j}")
                ctx.section("scalar", ush_c)

                def ush_v(v, t=t, pj=pj):
                    for j in (2 * pj, 2 * pj + 1):
                        v.wait(c_m, f"ussg:{t}:{j}")
                        v.wait(t_m, f"usy:{t}:{j}")
                        v.op("tensor_tensor", us_sb[:, 512 * j:512 * (j + 1)],
                             pf[j % 3], sg_sb[j % 2], ALU.mult,
                             outs=[us_sb[:, 512 * j:512 * (j + 1)]],
                             ins=[sg_sb[j % 2]], sem=v_m)
                        v.mark(v_m, f"usu:{t}:{j}")
                        v.mark(v_m, f"ustg:{t}:{j // 2}:{j % 2}")
                        sg_reader[j % 2] = (v_m, f"usu:{t}:{j}")
                ctx.section("vector", ush_v)

            ring_flush()
            if go("ush"):
                tile_done[0] = (v_m, f"usu:{t}:11")

            for ph in range(2) if go("z0") else []:
                def z0_mm(tn, t=t, ph=ph):
                    if ph == 0:
                        tn.wait(s_res, "all")
                    for tc in (2 * ph, 2 * ph + 1):
                        banks = (2, 3) if tc % 2 == 0 else (4, 5)
                        if ph == 0 and tc == 0:
                            tn.wait(v_m, f"ytrev:{t}:{NTC - 1}")
                        for half in range(2):
                            b = banks[half]
                            ps_wait(tn, b)
                            for yc in range(8):
                                tn.op("matmul", pf[b],
                                      y_fT[:, 512 * yc + 128 * tc:512 * yc + 128 * tc + 128],
                                      wo_sb[:, 1024 * yc + 512 * half:1024 * yc + 512 * half + 512],
                                      start=(yc == 0), stop=False)
                            for hc in range(12):
                                if ph == 0 and tc == 0 and half == 0:
                                    tn.wait(v_m, f"usu:{t}:{hc}")
                                tn.op("matmul", pf[b],
                                      us_sb[:, 512 * hc + 128 * tc:512 * hc + 128 * tc + 128],
                                      ws2_sb[:, 1024 * hc + 512 * half:1024 * hc + 512 * half + 512],
                                      start=False, stop=(hc == 11),
                                      sem=(t_m if hc == 11 else None))
                            tn.mark(t_m, f"z:{t}:0:{tc}:{half}")
                            ps_set_free(b, v_m, f"zev:{t}:0:{tc}:{half}")
                ctx.section("tensor", z0_mm)

                def z0_ev(v, t=t, ph=ph):
                    for tc in (2 * ph, 2 * ph + 1):
                        v.wait(s_x, f"x{t}:{tc}")
                        banks = (2, 3) if tc % 2 == 0 else (4, 5)
                        for half in range(2):
                            v.wait(t_m, f"z:{t}:0:{tc}:{half}")
                            dst = out_sb[:, D * tc + 512 * half:D * tc + 512 * half + 512]
                            v.op("tensor_tensor", dst, dst, pf[banks[half]],
                                 ALU.add, outs=[dst], ins=[], sem=v_m,
                                 allow_hazard=True)
                            v.mark(v_m, f"zev:{t}:0:{tc}:{half}")
                ctx.section("vector", z0_ev)

            if go("z0"):
                tile_done[0] = (v_m, f"zev:{t}:0:{NTC - 1}:1")
            XMAP = {0: (0, 1), 1: (2,), 2: (3,), 3: ()}
            for w in [w_ for w_ in range(4) if go(f"uz{w_}")]:
                if STOP_AFTER is None and t + 1 < n_tiles:
                    for xw in XMAP[w]:
                        def xload(sy, t=t, xw=xw):
                            if t > 0:
                                sy.wait(s_out, f"o{t - 1}:{xw}")
                            pk = (t + 1, xw - 1) if xw > 0 else (t, NTC - 1)
                            sy.wait(s_x, f"x{pk[0]}:{pk[1]}")
                            sy.op("dma_start",
                                  out=outs2[(t + 1) % 2][:, D * xw:D * (xw + 1)],
                                  in_=x_d[TILE * (t + 1) + TC * xw:
                                          TILE * (t + 1) + TC * (xw + 1), :],
                                  sem=s_x, semk=16)
                            sy.mark(s_x, f"x{t + 1}:{xw}")
                        ctx.section("sync", xload)
                    early_ln(t + 1, w)
                for ee in range(2):
                    e = 2 * w + ee

                    def ue_pre(tn, t=t, e=e):
                        tn.wait(v_m, f"gw8tev:{t}")
                        ps_wait(tn, 4)
                        tn.op("matmul", pf[4], sel[:, 128 * e:128 * (e + 1)],
                              w8t, start=True, stop=True, sem=t_m)
                        tn.mark(t_m, f"wbmm:{t}:{e}")
                        ps_set_free(4, v_m, f"wbev:{t}:{e}")
                    ctx.section("tensor", ue_pre)

                    def ue_vwb(v, t=t, e=e):
                        v.wait(t_m, f"wbmm:{t}:{e}")
                        if e >= 2:
                            v.wait(v_m, f"ueu:{t}:{e - 2}:5")
                        elif t > 0:
                            v.wait(v_m, f"ueu:{t - 1}:{e + 6}:5")
                        v.op("tensor_copy", wb_sb2[e % 2], pf[4],
                             outs=[wb_sb2[e % 2]], ins=[], sem=v_m)
                        v.mark(v_m, f"wbev:{t}:{e}")
                    ctx.section("vector", ue_vwb)

                    for pj in range(3):
                        def ue_mm(tn, t=t, e=e, pj=pj):
                            ka, kb = f"e1a_{t}_{e}", f"e1b_{t}_{e}"
                            rwait(tn, ka)
                            rwait(tn, kb)
                            wa = rview(ka, 8, 768)
                            wbv = rview(kb, 8, 768)
                            for j in (2 * pj, 2 * pj + 1):
                                bY, bG = j % 3, 3 + j % 3
                                ps_wait(tn, bY)
                                for kc in range(8):
                                    tn.op("matmul", pf[bY],
                                          wa[:, 768 * kc + 128 * j:768 * kc + 128 * j + 128],
                                          xn_f[:, 512 * kc:512 * (kc + 1)],
                                          start=(kc == 0), stop=(kc == 7),
                                          sem=(t_m if kc == 7 else None))
                                tn.mark(t_m, f"uey:{t}:{e}:{j}")
                                ps_set_free(bY, v_m, f"ueu:{t}:{e}:{j}")
                                ps_wait(tn, bG)
                                for kc in range(8):
                                    tn.op("matmul", pf[bG],
                                          wbv[:, 768 * kc + 128 * j:768 * kc + 128 * j + 128],
                                          xn_f[:, 512 * kc:512 * (kc + 1)],
                                          start=(kc == 0), stop=(kc == 7),
                                          sem=(t_m if kc == 7 else None))
                                tn.mark(t_m, f"ueg:{t}:{e}:{j}")
                                ps_set_free(bG, c_m, f"uesg:{t}:{e}:{j}")
                            if pj == 2:
                                tn.mark(t_m, f"use:{ka}")
                                tn.mark(t_m, f"use:{kb}")
                        ctx.section("tensor", ue_mm)

                        def ue_c(c, t=t, e=e, pj=pj):
                            for j in (2 * pj, 2 * pj + 1):
                                c.wait(t_m, f"ueg:{t}:{e}:{j}")
                                rd = sg_reader.get(j % 2)
                                if rd is not None:
                                    c.wait(rd[0], rd[1])
                                c.op("activation", sg_sb[j % 2], pf[3 + j % 3],
                                     AF.Silu, sem=c_m)
                                c.mark(c_m, f"uesg:{t}:{e}:{j}")
                        ctx.section("scalar", ue_c)

                        def ue_v(v, t=t, e=e, pj=pj):
                            slot = u_slot[e % 3]
                            if pj == 0:
                                v.wait(v_m, f"wbev:{t}:{e}")
                                key = uslot_consumer.get(e % 3)
                                if key is not None:
                                    v.wait(t_m, key)
                            for j in (2 * pj, 2 * pj + 1):
                                v.wait(c_m, f"uesg:{t}:{e}:{j}")
                                v.op("tensor_tensor", sg_sb[j % 2], sg_sb[j % 2],
                                     wb_sb2[e % 2], ALU.mult,
                                     outs=[sg_sb[j % 2]], ins=[wb_sb2[e % 2]],
                                     sem=(v_m if j % 2 == 1 else None))
                            v.mark(v_m, f"uetsg:{t}:{e}:{pj}")
                            v.wait(v_m, f"uetsg:{t}:{e}:{pj}")
                            for j in (2 * pj, 2 * pj + 1):
                                v.wait(t_m, f"uey:{t}:{e}:{j}")
                                v.op("tensor_tensor", slot[:, 512 * j:512 * (j + 1)],
                                     pf[j % 3], sg_sb[j % 2], ALU.mult,
                                     outs=[slot[:, 512 * j:512 * (j + 1)]],
                                     ins=[sg_sb[j % 2]], sem=v_m,
                                     allow_hazard=True)
                                v.mark(v_m, f"ueu:{t}:{e}:{j}")
                                sg_reader[j % 2] = (v_m, f"ueu:{t}:{e}:{j}")
                        ctx.section("vector", ue_v)

                    ring_flush()

                for ph in range(2):
                    def zw_mm(tn, t=t, w=w, ph=ph):
                        e0, e1 = 2 * w, 2 * w + 1
                        k20, k21 = f"e2_{t}_{e0}", f"e2_{t}_{e1}"
                        rwait(tn, k20)
                        rwait(tn, k21)
                        w20 = rview(k20, 6, 1024)
                        w21 = rview(k21, 6, 1024)
                        for tc in (2 * ph, 2 * ph + 1):
                            banks = (2, 3) if tc % 2 == 0 else (4, 5)
                            for half in range(2):
                                b = banks[half]
                                ps_wait(tn, b)
                                for ei, (e, w2) in enumerate(((e0, w20), (e1, w21))):
                                    slot = u_slot[e % 3]
                                    for hc in range(6):
                                        if ph == 0 and tc == 2 * ph and half == 0:
                                            tn.wait(v_m, f"ueu:{t}:{e}:{hc}")
                                        tn.op("matmul", pf[b],
                                              slot[:, 512 * hc + 128 * tc:512 * hc + 128 * tc + 128],
                                              w2[:, 1024 * hc + 512 * half:1024 * hc + 512 * half + 512],
                                              start=(ei == 0 and hc == 0),
                                              stop=(ei == 1 and hc == 5),
                                              sem=(t_m if (ei == 1 and hc == 5) else None))
                                tn.mark(t_m, f"z:{t}:{w + 1}:{tc}:{half}")
                                ps_set_free(b, v_m, f"zev:{t}:{w + 1}:{tc}:{half}")
                        if ph == 1:
                            tn.mark(t_m, f"use:{k20}")
                            tn.mark(t_m, f"use:{k21}")
                            uslot_consumer[e0 % 3] = f"z:{t}:{w + 1}:{NTC - 1}:1"
                            uslot_consumer[e1 % 3] = f"z:{t}:{w + 1}:{NTC - 1}:1"
                    ctx.section("tensor", zw_mm)

                    def zw_ev(v, t=t, w=w, ph=ph):
                        for tc in (2 * ph, 2 * ph + 1):
                            banks = (2, 3) if tc % 2 == 0 else (4, 5)
                            for half in range(2):
                                v.wait(t_m, f"z:{t}:{w + 1}:{tc}:{half}")
                                v.wait(v_m, f"zev:{t}:{w}:{tc}:{half}")
                                dst = out_sb[:, D * tc + 512 * half:D * tc + 512 * half + 512]
                                v.op("tensor_tensor", dst, dst, pf[banks[half]],
                                     ALU.add, outs=[dst], ins=[], sem=v_m,
                                     allow_hazard=True)
                                v.mark(v_m, f"zev:{t}:{w + 1}:{tc}:{half}")
                        if ph == 1:
                            v.mark(v_m, f"zwin:{t}:{w + 1}")
                    ctx.section("vector", zw_ev)

                ring_flush()
                tile_done[0] = (v_m, f"zwin:{t}:{w + 1}")
                if STOP_AFTER is None and t + 1 < n_tiles:
                    if w == 3:
                        emit_ap_c1(t + 1, 0)
                        emit_ap_c1(t + 1, 1)

            if STOP_AFTER is not None and t + 1 < n_tiles:
                for tc0 in range(NTC):
                    early_ln(t + 1, tc0)
            def store_sync(s, t=t, tile_done=tile_done, out_sb=out_sb):
                full = STOP_AFTER is None
                for tc in range(NTC):
                    if full:
                        s.wait(v_m, f"zev:{t}:4:{tc}:1")
                    else:
                        s.wait(tile_done[0][0], tile_done[0][1])
                    if tc > 0 or t > 0:
                        pk = (t, tc - 1) if tc > 0 else (t - 1, NTC - 1)
                        s.wait(s_out, f"o{pk[0]}:{pk[1]}")
                    s.op("dma_start",
                         out=out_d[TILE * t + TC * tc:TILE * t + TC * (tc + 1), :],
                         in_=out_sb[:, D * tc:D * (tc + 1)],
                         sem=s_out, semk=16)
                    s.mark(s_out, f"o{t}:{tc}")
                if not full and t + 1 < n_tiles:
                    for tc in range(NTC):
                        s.wait(s_out, f"o{t}:{tc}")
                        if tc > 0:
                            s.wait(s_x, f"x{t + 1}:{tc - 1}")
                        s.op("dma_start",
                             out=outs2[(t + 1) % 2][:, D * tc:D * (tc + 1)],
                             in_=x_d[TILE * (t + 1) + TC * tc:TILE * (t + 1) + TC * (tc + 1), :],
                             sem=s_x, semk=16)
                        s.mark(s_x, f"x{t + 1}:{tc}")
            ctx.section("sync", store_sync)

            ring_flush()

        def fin(s):
            s.wait(s_out, f"o{n_tiles - 1}:{NTC - 1}")
        ctx.section("sync", fin)

        if STOP_AFTER is None:
            assert not ring_q, f"{len(ring_q)} ring items never loaded"

    for s in SEMS:
        cm = nc.semaphore(s.name)
        s.h = cm.__enter__()
        ctxs.append(cm)
    ctx1 = Ctx(nc, None, dry=True)
    emit(ctx1)
    blk = nc.Block()
    block = blk.__enter__()
    for engine, tape in ctx1.tape.items():
        if not tape:
            continue

        def replay(e, tape=tape):
            for ent in tape:
                if ent[0] == "op":
                    _, method, args, kw, sem, semk = ent
                    inst = getattr(e, method)(*args, **kw)
                    if sem is not None:
                        inst.then_inc(sem.h, semk)
                else:
                    _, sem, val = ent
                    e.wait_ge(sem.h, val)
        getattr(block, engine)(replay)
    blk.__exit__(None, None, None)
    nc._keepalive = ctxs
    return nc


_CACHE = {}


def _consts():
    ident = np.eye(128, dtype=ml_dtypes.bfloat16)
    mask4 = np.ascontiguousarray(np.triu(np.ones((128, 128), np.float32)).astype(ml_dtypes.bfloat16))
    selm = np.zeros((8, 1024), np.float32)
    for e in range(8):
        selm[e, 128 * e:128 * (e + 1)] = 1.0
    return ident, mask4, selm.astype(ml_dtypes.bfloat16)


def kernel(L0_attn_x, L0_attn_W1, L0_attn_W2, L0_ffn_gate_w, L0_ffn_W1,
           L0_ffn_W2, L0_ffn_Ws1, L0_ffn_Ws2):
    x = np.ascontiguousarray(np.asarray(L0_attn_x, dtype=np.float32))
    B, T, _ = x.shape
    n_seq = B // N_CORES
    key = (n_seq, T)
    if key not in _CACHE:
        _CACHE[key] = build_program(n_seq, T)
    nc = _CACHE[key]

    W1 = np.asarray(L0_attn_W1, np.float32)
    wqkT = np.ascontiguousarray(W1[0:2048, :].T)
    wvT = np.ascontiguousarray(W1[2048:3072, :].T)
    woT = np.ascontiguousarray(np.asarray(L0_attn_W2, np.float32).T)
    wgT = np.ascontiguousarray(np.asarray(L0_ffn_gate_w, np.float32).T)
    we1T = np.ascontiguousarray(np.asarray(L0_ffn_W1, np.float32).transpose(0, 2, 1))
    we2T = np.ascontiguousarray(np.asarray(L0_ffn_W2, np.float32).transpose(0, 2, 1))
    ws1T = np.ascontiguousarray(np.asarray(L0_ffn_Ws1, np.float32).T)
    ws2T = np.ascontiguousarray(np.asarray(L0_ffn_Ws2, np.float32).T)
    ident, mask4, selm = _consts()

    in_maps = []
    for c in range(N_CORES):
        xc = np.ascontiguousarray(x[c * n_seq:(c + 1) * n_seq].reshape(n_seq * T, D))
        in_maps.append(dict(
            x=xc, w_qkT=wqkT, w_vT=wvT, w_oT=woT, w_gT=wgT,
            w_e1T=we1T, w_e2T=we2T, w_s1T=ws1T, w_s2T=ws2T,
            c_ident=ident, c_mask4=mask4, c_sel=selm,
        ))
    res = run_bass_kernel_spmd(nc, in_maps, core_ids=list(range(N_CORES)))
    outs = [np.asarray(res.results[c]["out"]).reshape(n_seq, T, D)
            for c in range(N_CORES)]
    out = np.ascontiguousarray(np.concatenate(outs, axis=0))
    if out.dtype != np.float64:
        out = out.astype(np.float32, copy=False)
    return out


# version 23
# speedup vs baseline: 1.0665x; 1.0299x over previous
import numpy as np
import ml_dtypes

import concourse.bass as bass
import concourse.mybir as mybir
from concourse.bass_utils import run_bass_kernel_spmd

F32 = mybir.dt.float32
BF16 = mybir.dt.bfloat16
AF = mybir.ActivationFunctionType
ALU = mybir.AluOpType
AX = mybir.AxisListType

D = 1024
NH = 16
E = 8
TC = 128
TILE = 512
NTC = TILE // TC
EPS = 1e-5
NEG_BIG = -1e30
SLOT = 6144
NSLOT = 4
N_CORES = 8
STOP_AFTER = None
_PHASES = ["ln", "gate", "qkv", "att1c", "att2c", "att3c", "att", "ush", "z0",
           "uz0", "uz1", "uz2", "uz3"]
_SUBATT = ["attt1", "attt2", "atts1", "attt3", "atts2", "attt4"]
def _sub_ok(name):
    if STOP_AFTER in (None, ) or STOP_AFTER not in _SUBATT:
        return True
    return _SUBATT.index(name) <= _SUBATT.index(STOP_AFTER)


class Sem:
    def __init__(self, name):
        self.name = name
        self.n = 0
        self.h = None
        self.marks = {}


class _Stub:
    def then_inc(self, h, k):
        return self


class Eng:
    def __init__(self, ctx, name, real):
        self.ctx = ctx
        self.name = name
        self.real = real

    def _rng(self, ap):
        try:
            off = ap.offset if isinstance(ap.offset, int) else 0
            return (ap.tensor.name, off, off + max(ap.free_size(), 1))
        except Exception:
            return None

    def _guard(self, outs, ins, allow_hazard):
        if self.name != "vector":
            return
        lo = self.ctx.v_last_out
        if lo is not None and not allow_hazard:
            for ap in ins:
                r = self._rng(ap)
                if r and r[0] == lo[0] and r[1] < lo[2] and lo[1] < r[2]:
                    raise RuntimeError(
                        f"DVE RAW hazard: read {r} right after write {lo}")
        self.ctx.v_last_out = self._rng(outs[0]) if outs else None

    def op(self, method, *args, outs=(), ins=(), sem=None, semk=1,
           allow_hazard=False, **kw):
        self._guard(list(outs), list(ins), allow_hazard)
        if sem is not None:
            sem.n += semk
        self.ctx.tape[self.name].append(("op", method, args, kw, sem, semk))
        return _Stub()

    def mark(self, sem, key):
        if True:
            assert key not in sem.marks, f"dup mark {sem.name}:{key}"
            sem.marks[key] = sem.n
        self.ctx.marked.add((sem.name, key))

    def wait(self, sem, key=None, val=None):
        if self.name == "vector":
            self.ctx.v_last_out = None
        if val is None:
            assert (sem.name, key) in self.ctx.marked, \
                f"wait on unrecorded mark {sem.name}:{key}"
            val = sem.marks[key]
        assert val <= sem.n, f"future wait {sem.name} {val}>{sem.n}"
        self.ctx.tape[self.name].append(("wait", sem, val))


class Ctx:

    def __init__(self, nc, block, dry):
        self.nc = nc
        self.block = block
        self.dry = dry
        self.marked = set()
        self.v_last_out = None
        self.tape = {k: [] for k in
                     ("tensor", "vector", "scalar", "gpsimd", "sync")}

    def section(self, engine, body):
        body(Eng(self, engine, None))


def build_program(n_seq, T):
    assert T % TILE == 0
    NT = n_seq * T
    n_tiles = NT // TILE
    tps = T // TILE

    nc = bass.Bass("TRN2", target_bir_lowering=False, debug=False)

    x_d = nc.dram_tensor("x", [NT, D], F32, kind="ExternalInput").ap()
    out_d = nc.dram_tensor("out", [NT, D], F32, kind="ExternalOutput").ap()
    wqk_d = nc.dram_tensor("w_qkT", [D, 2048], F32, kind="ExternalInput").ap()
    wv_d = nc.dram_tensor("w_vT", [D, D], F32, kind="ExternalInput").ap()
    wo_d = nc.dram_tensor("w_oT", [D, D], F32, kind="ExternalInput").ap()
    wg_d = nc.dram_tensor("w_gT", [D, E], F32, kind="ExternalInput").ap()
    we1_d = nc.dram_tensor("w_e1T", [E, D, 1536], F32, kind="ExternalInput").ap()
    we2_d = nc.dram_tensor("w_e2T", [E, 768, D], F32, kind="ExternalInput").ap()
    ws1_d = nc.dram_tensor("w_s1T", [D, 3072], F32, kind="ExternalInput").ap()
    ws2_d = nc.dram_tensor("w_s2T", [1536, D], F32, kind="ExternalInput").ap()
    ident_d = nc.dram_tensor("c_ident", [128, 128], BF16, kind="ExternalInput").ap()
    mask_d = nc.dram_tensor("c_mask4", [128, 128], BF16, kind="ExternalInput").ap()
    sel_d = nc.dram_tensor("c_sel", [8, 1024], BF16, kind="ExternalInput").ap()

    wqk_s = nc.dram_tensor("s_qk", [D, 2048], BF16, kind="Internal").ap()
    wv_s = nc.dram_tensor("s_v", [D, D], BF16, kind="Internal").ap()
    we1_s = nc.dram_tensor("s_e1", [E, D, 1536], BF16, kind="Internal").ap()
    we2_s = nc.dram_tensor("s_e2", [E, 768, D], BF16, kind="Internal").ap()
    ws1_s = nc.dram_tensor("s_s1", [D, 3072], BF16, kind="Internal").ap()

    ctxs = []

    def sbuf(name, shape, dt):
        cm = nc.sbuf_tensor(name, shape, dt)
        t = cm.__enter__()
        ctxs.append(cm)
        return t

    def psumt(name, shape, dt):
        cm = nc.psum_tensor(name, shape, dt)
        t = cm.__enter__()
        ctxs.append(cm)
        return t

    ident = sbuf("ident", [128, 128], BF16)[:, :]
    mask4 = sbuf("mask4", [128, 128], BF16)[:, :]
    sel = sbuf("sel", [8, 1024], BF16)[:, :]
    wg_sb = sbuf("wg_sb", [128, 8 * E], BF16)[:, :]
    wo_sb = sbuf("wo_sb", [128, 8 * 1024], BF16)[:, :]
    ws2_sb = sbuf("ws2_sb", [128, 12 * 1024], BF16)[:, :]
    ring = sbuf("ring", [128, NSLOT * SLOT], BF16)[:, :]
    outs2 = [sbuf(f"out_sb{i}", [128, NTC * D], F32)[:, :] for i in range(2)]
    xn_f = sbuf("xn_f", [128, 8 * TILE], BF16)[:, :]
    qk_ar = sbuf("qk_ar", [128, 8192], BF16)[:, :]
    u_ar = sbuf("u_ar", [128, 9216], BF16)[:, :]
    y_fT = sbuf("y_fT", [128, 8 * TILE], BF16)[:, :]
    v_sb = [sbuf(f"v_sb{i}", [128, NH * 65], BF16)[:, :] for i in range(4)]
    sz_m = sbuf("sz_m", [128, 520], F32)[:, :]
    sz_bf = sbuf("sz_bf", [128, 520], BF16)[:, :]
    k_tm = sbuf("k_tm", [128, 1024], BF16)[:, :]
    xn2_tmb = k_tm
    y_tm = sbuf("y_tm", [128, 1024], BF16)[:, :]
    xn_tmb = y_tm
    w8 = sbuf("w8", [128, NTC * E], F32)[:, :]
    w8b = sbuf("w8b", [128, NTC * E], BF16)[:, :]
    w8t = sbuf("w8t", [8, TILE], BF16)[:, :]
    a_m = [sbuf(f"a_m{i}", [128, 512], BF16)[:, :] for i in range(4)]
    elu1 = sbuf("elu1", [128, 1024], F32)[:, :]
    sg_sb = [sbuf(f"sg{i}", [128, 512], BF16)[:, :] for i in range(2)]
    xn_tm = sbuf("xn_tm", [128, D], F32)[:, :]
    x_tc = sbuf("x_tc", [128, D], F32)[:, :]
    bn12 = sbuf("bn12", [128, 12], F32)[:, :]
    mv = sbuf("mv", [128, 4 * NTC], F32)[:, :]
    rr = sbuf("rr", [128, 2 * NTC], F32)[:, :]
    nm = sbuf("nm", [128, 2 * NTC], F32)[:, :]
    neg = sbuf("neg", [128, 2], F32)[:, :]
    lnt = sbuf("lnt", [128, 4], F32)[:, :]
    gt = sbuf("gt", [128, 8 * NTC], F32)[:, :]
    gtp = sbuf("gtp", [128, 8 * NTC], F32)[:, :]
    gsc = sbuf("gsc", [128, 24], F32)[:, :]
    gl_sb = sbuf("gl_sb", [128, NTC * E], F32)[:, :]
    d4 = sbuf("d4", [128, 8], F32)[:, :]
    spc = sbuf("spc", [128, 8], F32)[:, :]
    cst = sbuf("cst", [128, 4], F32)[:, :]
    r4 = sbuf("r4", [128, 8], F32)[:, :]

    q_f = qk_ar[:, 0:4096]
    k_f = qk_ar[:, 4096:8192]
    us_sb = qk_ar[:, 0:6144]
    wb_sb2 = [qk_ar[:, 6144:6656], qk_ar[:, 6656:7168]]
    xn2_f = u_ar[:, 0:4096]
    u_slot = [u_ar[:, 3072 * i:3072 * (i + 1)] for i in range(3)]

    pf = [psumt(f"pf{i}", [128, 512], F32)[:, :] for i in range(6)]
    pt = [psumt(f"pt{i}", [128, 1024], BF16)[:, :] for i in range(2)]

    SEMS = []

    def S(name):
        s = Sem(name)
        SEMS.append(s)
        return s
    s_cast = S("s_cast")
    s_cast1 = S("s_cast1")
    s_res = S("s_res")
    s_cres = S("s_cres")
    s_x = S("s_x")
    s_xe = S("s_xe")
    s_out = S("s_out")
    s_ring = [S(f"s_ring{i}") for i in range(NSLOT)]
    s_ring0 = [S(f"s_ring0_{i}") for i in range(NSLOT)]
    s_castw = [S(f"s_castw{i}") for i in range(4)]
    t_m = S("t_m")
    v_m = S("v_m")
    c_m = S("c_m")

    def ring_items(t):
        items = [
            (f"qhA{t}", wqk_s[:, 0:512].rearrange("(kc p) m -> p kc m", p=128), 8, 512, None),
            (f"qhB{t}", wqk_s[:, 512:1024].rearrange("(kc p) m -> p kc m", p=128), 8, 512, None),
            (f"khA{t}", wqk_s[:, 1024:1536].rearrange("(kc p) m -> p kc m", p=128), 8, 512, None),
            (f"khB{t}", wqk_s[:, 1536:2048].rearrange("(kc p) m -> p kc m", p=128), 8, 512, None),
            (f"wvA{t}", wv_s[:, 0:512].rearrange("(kc p) m -> p kc m", p=128), 8, 512, None),
            (f"wvB{t}", wv_s[:, 512:1024].rearrange("(kc p) m -> p kc m", p=128), 8, 512, None),
        ]
        for qq in range(4):
            items.append((f"s1_{t}_{qq}",
                          ws1_s[:, 768 * qq:768 * (qq + 1)].rearrange("(kc p) m -> p kc m", p=128),
                          8, 768, None))
        for w in range(4):
            for ee in range(2):
                e = 2 * w + ee
                src32 = we1_d[e, :, 0:768].rearrange("(kc p) m -> p kc m", p=128) \
                    if t == 0 else None
                items.append((f"e1a_{t}_{e}",
                              we1_s[e, :, 0:768].rearrange("(kc p) m -> p kc m", p=128), 8, 768, src32))
                src32 = we1_d[e, :, 768:1536].rearrange("(kc p) m -> p kc m", p=128) \
                    if t == 0 else None
                items.append((f"e1b_{t}_{e}",
                              we1_s[e, :, 768:1536].rearrange("(kc p) m -> p kc m", p=128), 8, 768, src32))
            for ee in range(2):
                e = 2 * w + ee
                src32 = we2_d[e].rearrange("(kc p) m -> p kc m", p=128) \
                    if t == 0 else None
                items.append((f"e2_{t}_{e}",
                              we2_s[e].rearrange("(kc p) m -> p kc m", p=128), 6, 1024, src32))
        return items

    FLUSH_PLAN = []

    def emit(ctx):
        if ctx.dry:
            FLUSH_PLAN.clear()
        flush_i = [0]
        ring_q = []
        slot_of = {}
        slot_prev = [None] * NSLOT
        loads_in_slot = [0] * NSLOT
        loads0_in_slot = [0] * NSLOT
        wait_cnt = {}
        wb_keys = set()
        last_wb = [None]
        next_slot = [0]
        for t in range(n_tiles):
            ring_q.extend(ring_items(t))

        def ring_flush():
            todo = []
            if not ctx.dry:
                quota = FLUSH_PLAN[flush_i[0]]
                flush_i[0] += 1
            while ring_q:
                if ctx.dry:
                    key, src, kc, m, src32 = ring_q[0]
                    slot = next_slot[0] % NSLOT
                    prev = slot_prev[slot]
                    if prev is not None and \
                            ("t_m", f"use:{prev}") not in ctx.marked:
                        break
                else:
                    if len(todo) >= quota:
                        break
                    key, src, kc, m, src32 = ring_q[0]
                    slot = next_slot[0] % NSLOT
                    prev = slot_prev[slot]
                ring_q.pop(0)
                next_slot[0] += 1
                slot_of[key] = slot
                slot_prev[slot] = key
                if src32 is not None:
                    loads0_in_slot[slot] += 1
                    wait_cnt[key] = (s_ring0[slot], 16 * loads0_in_slot[slot])
                else:
                    loads_in_slot[slot] += 1
                    wait_cnt[key] = (s_ring[slot], 16 * loads_in_slot[slot])
                todo.append((key, src, kc, m, slot, prev, src32))
            if ctx.dry:
                FLUSH_PLAN.append(len(todo))
            if not todo:
                return
            sw = [x for x in todo if x[6] is not None]
            hw = [x for x in todo if x[6] is None]

            if sw:
                def gbody(g):
                    for key, src, kc, m, slot, prev, src32 in sw:
                        if prev is not None:
                            g.wait(t_m, f"use:{prev}")
                            if prev in wb_keys:
                                g.wait(s_cast, f"wb:{prev}")
                        g.op("dma_start",
                             out=ring[:, SLOT * slot:SLOT * slot + kc * m]
                             .rearrange("p (kc m) -> p kc m", kc=kc),
                             in_=src32, sem=s_ring0[slot], semk=16)
                        wb_keys.add(key)
                ctx.section("gpsimd", gbody)

                def wbody(sy):
                    for key, src, kc, m, slot, prev, src32 in sw:
                        sem, val = wait_cnt[key]
                        sy.wait(sem, val=val)
                        if last_wb[0] is not None:
                            sy.wait(s_cast, f"wb:{last_wb[0]}")
                        sy.op("dma_start", out=src,
                              in_=ring[:, SLOT * slot:SLOT * slot + kc * m]
                              .rearrange("p (kc m) -> p kc m", kc=kc),
                              sem=s_cast, semk=16)
                        sy.mark(s_cast, f"wb:{key}")
                        last_wb[0] = key
                ctx.section("sync", wbody)

            if hw:
                def body(s):
                    for key, src, kc, m, slot, prev, src32 in hw:
                        if key.startswith(("qh", "kh")):
                            s.wait(s_cast1, "qk")
                        elif key.startswith("wv"):
                            s.wait(s_cast1, "v")
                        elif key.startswith("s1"):
                            s.wait(s_cast1, "front")
                        else:
                            kind, _, e = key.split("_")
                            s.wait(s_cast, f"wb:{kind}_0_{e}")
                        if prev is not None:
                            s.wait(t_m, f"use:{prev}")
                            if prev in wb_keys:
                                s.wait(s_cast, f"wb:{prev}")
                        s.op("dma_start",
                             out=ring[:, SLOT * slot:SLOT * slot + kc * m]
                             .rearrange("p (kc m) -> p kc m", kc=kc),
                             in_=src, sem=s_ring[slot], semk=16)
                ctx.section("sync", body)

        def rview(key, kc, m):
            return ring[:, SLOT * slot_of[key]:SLOT * slot_of[key] + kc * m]

        def rwait(eng, key):
            sem, val = wait_cnt[key]
            eng.wait(sem, val=val)

        ps_free = {}
        uslot_consumer = {}
        sg_reader = {}

        def ps_wait(eng, idx):
            if idx in ps_free:
                sem, key = ps_free[idx]
                eng.wait(sem, key)

        def ps_set_free(idx, sem, key):
            ps_free[idx] = (sem, key)

        def pro_gp(g):
            g.op("dma_start", out=wg_sb.rearrange("p (kc m) -> p kc m", kc=8),
                 in_=wg_d.rearrange("(kc p) m -> p kc m", p=128), sem=s_res, semk=16)
            g.op("dma_start", out=wo_sb.rearrange("p (kc m) -> p kc m", kc=8),
                 in_=wo_d.rearrange("(kc p) m -> p kc m", p=128), sem=s_res, semk=16)
            g.op("dma_start", out=ws2_sb.rearrange("p (kc m) -> p kc m", kc=12),
                 in_=ws2_d.rearrange("(kc p) m -> p kc m", p=128), sem=s_res, semk=16)
            g.mark(s_res, "all")
            g.op("dma_start", out=wqk_s, in_=wqk_d, sem=s_cast1, semk=16)
            g.mark(s_cast1, "qk")
            g.wait(s_cast1, "qk")
            g.op("dma_start", out=wv_s, in_=wv_d, sem=s_cast1, semk=16)
            g.mark(s_cast1, "v")
            g.wait(s_cast1, "v")
            g.op("dma_start", out=ws1_s, in_=ws1_d, sem=s_cast1, semk=16)
            g.mark(s_cast1, "front")
        ctx.section("gpsimd", pro_gp)

        def pro_sync(s):
            s.op("dma_start", out=ident, in_=ident_d, sem=s_cres, semk=16)
            s.op("dma_start", out=mask4, in_=mask_d, sem=s_cres, semk=16)
            s.op("dma_start", out=sel, in_=sel_d, sem=s_cres, semk=16)
            s.op("dma_start",
                 out=outs2[0].rearrange("p (tc d) -> p tc d", tc=NTC),
                 in_=x_d[0:TILE, :].rearrange("(tc p) d -> p tc d", p=128),
                 sem=s_x, semk=16)
            s.mark(s_cres, "all")
            for tc in range(NTC):
                s.mark(s_x, f"x0:{tc}")
        ctx.section("sync", pro_sync)

        def pro_vec(v):
            v.op("memset", spc, 0.0, sem=None)
            v.op("memset", cst[:, 0:1], EPS, sem=None)
            v.op("memset", cst[:, 1:2], -0.5, sem=None)
            v.op("memset", cst[:, 2:3], -1.0, sem=None)
            v.op("memset", cst[:, 3:4], 0.5, sem=v_m)
            v.mark(v_m, "cst")
            for i in range(4):
                v.op("memset",
                     v_sb[i].rearrange("p (h c) -> p h c", c=65)[:, :, 64:65],
                     1.0, sem=v_m)
            v.mark(v_m, "vones")
        ctx.section("vector", pro_vec)

        ring_flush()

        def early_ln(t, tc):
            def e_v1(v, t=t, tc=tc):
                xsrc = outs2[t % 2][:, D * tc:D * (tc + 1)]
                v.wait(s_x, f"x{t}:{tc}")
                if not (t == 0 and tc == 0):
                    pk = (t, tc - 1) if tc > 0 else (t - 1, NTC - 1)
                    v.wait(v_m, f"eln1s:{pk[0]}:{pk[1]}")
                v.op("bn_stats", bn12[:, 0:6], xsrc[:, 0:512],
                     outs=[bn12[:, 0:6]], ins=[])
                v.op("bn_stats", bn12[:, 6:12], xsrc[:, 512:1024],
                     outs=[bn12[:, 6:12]], ins=[], sem=v_m)
                v.mark(v_m, f"eln1b:{t}:{tc}")
                v.wait(v_m, f"eln1b:{t}:{tc}")
                v.op("bn_aggr", mv[:, 2 * tc:2 * tc + 2], bn12,
                     outs=[mv[:, 2 * tc:2 * tc + 2]], ins=[], sem=v_m,
                     allow_hazard=True)
                v.mark(v_m, f"eln1s:{t}:{tc}")
            ctx.section("vector", e_v1)

            def e_c1(c, t=t, tc=tc):
                c.wait(v_m, "cst")
                c.wait(v_m, f"eln1s:{t}:{tc}")
                c.op("mul", neg[:, 0:1], mv[:, 2 * tc:2 * tc + 1], cst[:, 2:3],
                     sem=c_m)
                c.op("activation", lnt[:, 0:1], mv[:, 2 * tc + 1:2 * tc + 2],
                     AF.Ln, bias=cst[:, 0:1], sem=c_m)
                c.mark(c_m, f"elnv1:{t}:{tc}")
                c.wait(c_m, f"elnv1:{t}:{tc}")
                c.op("activation", rr[:, tc:tc + 1], lnt[:, 0:1], AF.Exp,
                     scale=cst[:, 1:2], sem=c_m)
                c.mark(c_m, f"elnr1:{t}:{tc}")
                c.wait(c_m, f"elnr1:{t}:{tc}")
                c.op("mul", nm[:, tc:tc + 1], neg[:, 0:1], rr[:, tc:tc + 1],
                     sem=c_m)
                c.op("mul", lnt[:, 1:2], mv[:, 2 * tc + 1:2 * tc + 2],
                     rr[:, tc:tc + 1], sem=c_m)
                c.mark(c_m, f"elnn1:{t}:{tc}")
                c.wait(c_m, f"elnn1:{t}:{tc}")
                c.op("mul", lnt[:, 2:3], lnt[:, 1:2], rr[:, tc:tc + 1],
                     sem=c_m)
                c.mark(c_m, f"elnu:{t}:{tc}")
                c.wait(c_m, f"elnu:{t}:{tc}")
                c.op("activation", lnt[:, 1:2], lnt[:, 2:3],
                     AF.Ln, bias=cst[:, 0:1], sem=c_m)
                c.mark(c_m, f"elnv2:{t}:{tc}")
                c.wait(c_m, f"elnv2:{t}:{tc}")
                c.op("activation", lnt[:, 2:3], lnt[:, 1:2], AF.Exp,
                     scale=cst[:, 1:2], sem=c_m)
                c.mark(c_m, f"elnr2:{t}:{tc}")
                c.wait(c_m, f"elnr2:{t}:{tc}")
                c.op("mul", rr[:, NTC + tc:NTC + tc + 1], rr[:, tc:tc + 1],
                     lnt[:, 2:3], sem=c_m)
                c.op("mul", nm[:, NTC + tc:NTC + tc + 1], nm[:, tc:tc + 1],
                     lnt[:, 2:3], sem=c_m)
                c.mark(c_m, f"elnc:{t}:{tc}")
            ctx.section("scalar", e_c1)

        def ln_stage(tc):
            if tc % 2 == 0:
                return ((xn_tmb[:, 0:512], xn_tmb[:, 512:1024]),
                        (xn2_tmb[:, 0:512], xn2_tmb[:, 512:1024]))
            return ((a_m[0], a_m[1]), (a_m[2], a_m[3]))

        def emit_ap_c1(t, tc):
            def ap_c1(c, t=t, tc=tc):
                out_b = outs2[t % 2]
                (x0, x1), (y0, y1) = ln_stage(tc)
                c.wait(s_x, f"x{t}:{tc}")
                c.wait(c_m, f"elnc:{t}:{tc}")
                if t > 0 or tc > 1:
                    pk = (t, tc - 2) if tc > 1 else (t - 1, NTC - 2 + tc)
                    c.wait(t_m, f"xtrd:{pk[0]}:{pk[1]}")
                    c.wait(t_m, f"x2trd:{pk[0]}:{pk[1]}")
                    if tc <= 1:
                        c.wait(t_m, f"ytr:{t-1}:{NTC-1}")
                        c.wait(t_m, f"szmm:{t-1}:{NTC-1}")
                        if tc == 1:
                            c.wait(t_m, f"omm:{t-1}:{NTC-1}:3")
                for half, dst in ((0, x0), (1, x1)):
                    c.op("activation", dst,
                         out_b[:, D * tc + 512 * half:D * tc + 512 * half + 512],
                         AF.Identity, bias=nm[:, tc:tc + 1],
                         scale=rr[:, tc:tc + 1],
                         sem=(c_m if half == 1 else None))
                c.mark(c_m, f"xnbf:{t}:{tc}")
                for half, dst in ((0, y0), (1, y1)):
                    c.op("activation", dst,
                         out_b[:, D * tc + 512 * half:D * tc + 512 * half + 512],
                         AF.Identity, bias=nm[:, NTC + tc:NTC + tc + 1],
                         scale=rr[:, NTC + tc:NTC + tc + 1],
                         sem=(c_m if half == 1 else None))
                c.mark(c_m, f"xn2bf:{t}:{tc}")
            ctx.section("scalar", ap_c1)

        def go(name):
            if STOP_AFTER is None:
                return True
            sa = "att" if STOP_AFTER in _SUBATT else STOP_AFTER
            return _PHASES.index(name) <= _PHASES.index(sa)

        for tc0 in range(NTC):
            early_ln(0, tc0)

        for t in range(n_tiles):
            tis = t % tps
            tile_done = [None]
            out_sb = outs2[t % 2]

            for tc in range(NTC) if go("ln") else []:
                if not (t > 0 and tc <= 1 and STOP_AFTER is None):
                    emit_ap_c1(t, tc)

                def ap_tr(tn, t=t, tc=tc):
                    tn.wait(s_cres, "all")
                    (x0, x1), (y0, y1) = ln_stage(tc)
                    for half, (bufname, srcs, mk) in enumerate((
                            ("x", (x0, x1), f"xnbf:{t}:{tc}"),
                            ("x2", (y0, y1), f"xn2bf:{t}:{tc}"))):
                        tn.wait(c_m, mk)
                        ps_wait(tn, 6 + half)
                        for kc in range(8):
                            src = srcs[kc // 4]
                            tn.op("matmul", pt[half][:, 128 * kc:128 * (kc + 1)],
                                  src[:, 128 * (kc % 4):128 * (kc % 4) + 128], ident,
                                  start=True, stop=True, is_transpose=True,
                                  sem=(t_m if kc == 7 else None))
                        tn.mark(t_m, f"{bufname}tr:{t}:{tc}")
                        ps_set_free(6 + half, v_m, f"{bufname}trev:{t}:{tc}")
                        tn.mark(t_m, f"{bufname}trd:{t}:{tc}")
                ctx.section("tensor", ap_tr)

                def ap_trev(v, t=t, tc=tc):
                    for half, bufname in enumerate(("x", "x2")):
                        dstt = xn_f if bufname == "x" else xn2_f
                        v.wait(t_m, f"{bufname}tr:{t}:{tc}")
                        dv = dstt.rearrange("p (kc m) -> p kc m", kc=8) \
                            [:, 0:8, 128 * tc:128 * tc + 128]
                        v.op("tensor_copy", dv,
                             pt[half].rearrange("p (j m) -> p j m", j=8),
                             outs=[dstt], ins=[], sem=v_m)
                        v.mark(v_m, f"{bufname}trev:{t}:{tc}")
                ctx.section("vector", ap_trev)

            def gate_mm(tn, t=t):
                tn.wait(s_res, "all")
                ps_wait(tn, 4)
                for tc in range(NTC):
                    tn.wait(v_m, f"xtrev:{t}:{tc}")
                    for kc in range(8):
                        tn.op("matmul", pf[4][:, 8 * tc:8 * tc + 8],
                              xn_f[:, 512 * kc + 128 * tc:512 * kc + 128 * tc + 128],
                              wg_sb[:, 8 * kc:8 * kc + 8],
                              start=(kc == 0), stop=(kc == 7),
                              sem=(t_m if kc == 7 else None))
                    tn.mark(t_m, f"gmm:{t}:{tc}")
                ps_set_free(4, v_m, f"glev:{t}")
            if go("gate"):
                ctx.section("tensor", gate_mm)

            def gate_v1(v, t=t):
                v.wait(t_m, f"gmm:{t}:{NTC - 1}")
                v.op("tensor_copy", gl_sb, pf[4][:, 0:NTC * E],
                     outs=[gl_sb], ins=[], sem=v_m)
                v.mark(v_m, f"glev:{t}")
                v.wait(v_m, f"glev:{t}")
                for tc in range(NTC):
                    v.op("reduce_max", gsc[:, tc:tc + 1], gl_sb[:, 8 * tc:8 * tc + 8],
                         axis=AX.X, outs=[gsc[:, tc:tc + 1]], ins=[],
                         sem=(v_m if tc == NTC - 1 else None), allow_hazard=True)
                v.mark(v_m, f"grm:{t}")
                v.wait(v_m, f"grm:{t}")
                for tc in range(NTC):
                    v.op("tensor_scalar", gt[:, 8 * tc:8 * tc + 8],
                         gl_sb[:, 8 * tc:8 * tc + 8], gsc[:, tc:tc + 1], None,
                         ALU.subtract, outs=[gt[:, 8 * tc:8 * tc + 8]],
                         ins=[gsc[:, tc:tc + 1]], sem=v_m, allow_hazard=True)
                    v.mark(v_m, f"gsh:{t}:{tc}")
            if go("gate"):
                ctx.section("vector", gate_v1)

            def gate_c1(c, t=t):
                for tc in range(NTC):
                    c.wait(v_m, f"gsh:{t}:{tc}")
                    c.op("activation", gtp[:, 8 * tc:8 * tc + 8], gt[:, 8 * tc:8 * tc + 8],
                         AF.Exp, accum_out=gsc[:, 4 + tc:5 + tc], sem=c_m)
                    c.mark(c_m, f"gexp:{t}:{tc}")
            if go("gate"):
                ctx.section("scalar", gate_c1)

            def gate_v2(v, t=t):
                for tc in range(NTC):
                    v.wait(c_m, f"gexp:{t}:{tc}")
                    v.op("reciprocal", gsc[:, 16 + tc:17 + tc], gsc[:, 4 + tc:5 + tc],
                         outs=[gsc[:, 16 + tc:17 + tc]], ins=[gsc[:, 4 + tc:5 + tc]],
                         sem=v_m, allow_hazard=(tc > 0))
                    v.mark(v_m, f"grp:{t}:{tc}")
            if go("gate"):
                ctx.section("vector", gate_v2)

            def gate_c2(c, t=t):
                for tc in range(NTC):
                    c.wait(v_m, f"grp:{t}:{tc}")
                    c.wait(c_m, f"gexp:{t}:{tc}")
                    c.op("activation", w8[:, 8 * tc:8 * tc + 8], gtp[:, 8 * tc:8 * tc + 8],
                         AF.Identity, scale=gsc[:, 16 + tc:17 + tc], sem=c_m)
                    c.mark(c_m, f"gpr:{t}:{tc}")
            if go("gate"):
                ctx.section("scalar", gate_c2)

            def gate_v3(v, t=t):
                def loop(tag, fn, n=NTC):
                    for tc in range(n):
                        fn(tc, tc == n - 1)
                    v.mark(v_m, f"{tag}:{t}")
                    v.wait(v_m, f"{tag}:{t}")
                for tc in range(NTC):
                    v.wait(c_m, f"gpr:{t}:{tc}")
                loop("g_m1", lambda tc, last: v.op(
                    "reduce_max", gsc[:, 8 + tc:9 + tc], w8[:, 8 * tc:8 * tc + 8],
                    axis=AX.X, outs=[gsc[:, 8 + tc:9 + tc]], ins=[],
                    sem=(v_m if last else None)))
                loop("g_eq", lambda tc, last: v.op(
                    "tensor_scalar", gt[:, 8 * tc:8 * tc + 8],
                    w8[:, 8 * tc:8 * tc + 8], gsc[:, 8 + tc:9 + tc], None,
                    ALU.is_ge, outs=[gt[:, 8 * tc:8 * tc + 8]],
                    ins=[gsc[:, 8 + tc:9 + tc]],
                    sem=(v_m if last else None), allow_hazard=True))
                loop("g_stt", lambda tc, last: v.op(
                    "scalar_tensor_tensor", gt[:, 8 * tc:8 * tc + 8],
                    gt[:, 8 * tc:8 * tc + 8], NEG_BIG, w8[:, 8 * tc:8 * tc + 8],
                    ALU.mult, ALU.add, outs=[gt[:, 8 * tc:8 * tc + 8]], ins=[],
                    sem=(v_m if last else None), allow_hazard=True))
                loop("g_m2", lambda tc, last: v.op(
                    "reduce_max", gsc[:, 12 + tc:13 + tc], gt[:, 8 * tc:8 * tc + 8],
                    axis=AX.X, outs=[gsc[:, 12 + tc:13 + tc]], ins=[],
                    sem=(v_m if last else None), allow_hazard=True))
                loop("g_msk", lambda tc, last: v.op(
                    "tensor_scalar", gt[:, 8 * tc:8 * tc + 8],
                    w8[:, 8 * tc:8 * tc + 8], gsc[:, 12 + tc:13 + tc], None,
                    ALU.is_ge, outs=[gt[:, 8 * tc:8 * tc + 8]],
                    ins=[gsc[:, 12 + tc:13 + tc]],
                    sem=(v_m if last else None), allow_hazard=True))
                loop("g_w8", lambda tc, last: v.op(
                    "tensor_mul", w8[:, 8 * tc:8 * tc + 8],
                    gt[:, 8 * tc:8 * tc + 8], w8[:, 8 * tc:8 * tc + 8],
                    outs=[w8[:, 8 * tc:8 * tc + 8]],
                    ins=[gt[:, 8 * tc:8 * tc + 8]],
                    sem=(v_m if last else None), allow_hazard=True))
                for tc in range(NTC):
                    v.op("tensor_copy", w8b[:, 8 * tc:8 * tc + 8],
                         w8[:, 8 * tc:8 * tc + 8],
                         outs=[w8b[:, 8 * tc:8 * tc + 8]],
                         ins=[w8[:, 8 * tc:8 * tc + 8]], sem=v_m,
                         allow_hazard=True)
                    v.mark(v_m, f"gw8b:{t}:{tc}")
            if go("gate"):
                ctx.section("vector", gate_v3)


            def qkv_v1(v, fcg, t=t):
                bank = fcg % 4
                v.wait(t_m, f"qkmm:{t}:{fcg}")
                if fcg >= 2:
                    v.wait(v_m, f"eluv2:{t}:{fcg - 2}")
                v.op("tensor_scalar_min",
                     elu1[:, 512 * (fcg % 2):512 * (fcg % 2) + 512],
                     pf[bank], 0.0,
                     outs=[elu1[:, 512 * (fcg % 2):512 * (fcg % 2) + 512]],
                     ins=[], sem=v_m)
                v.mark(v_m, f"eluv1:{t}:{fcg}")

            def qkv_v2(v, fcg, t=t):
                bank = fcg % 4
                v.wait(c_m, f"eluexp:{t}:{fcg}")
                dst = q_f if fcg < 8 else k_f
                fc = fcg % 8
                v.op("scalar_tensor_tensor", dst[:, 512 * fc:512 * (fc + 1)],
                     pf[bank], 0.0,
                     elu1[:, 512 * (fcg % 2):512 * (fcg % 2) + 512],
                     ALU.max, ALU.add,
                     outs=[dst[:, 512 * fc:512 * (fc + 1)]],
                     ins=[elu1[:, 512 * (fcg % 2):512 * (fcg % 2) + 512]],
                     sem=v_m)
                v.mark(v_m, f"eluv2:{t}:{fcg}")

            for g in range(8) if go("qkv") else []:
                def qkv_mm(tn, t=t, g=g):
                    if g == 0:
                        tn.wait(v_m, f"x2trev:{t}:{NTC - 1}")
                    for fcg in (2 * g, 2 * g + 1):
                        key = [f"qhA{t}", f"qhB{t}", f"khA{t}", f"khB{t}"][fcg // 4]
                        rwait(tn, key)
                        wsrc = rview(key, 8, 512)
                        fc = fcg % 4
                        bank = fcg % 4
                        ps_wait(tn, bank)
                        for kc in range(8):
                            tn.op("matmul", pf[bank],
                                  wsrc[:, 512 * kc + 128 * fc:512 * kc + 128 * fc + 128],
                                  xn2_f[:, 512 * kc:512 * (kc + 1)],
                                  start=(kc == 0), stop=(kc == 7),
                                  sem=(t_m if kc == 7 else None))
                        tn.mark(t_m, f"qkmm:{t}:{fcg}")
                        ps_set_free(bank, v_m, f"eluv2:{t}:{fcg}")
                        if fcg % 4 == 3:
                            tn.mark(t_m, f"use:{key}")
                ctx.section("tensor", qkv_mm)

                def qkv_vv(v, t=t, g=g):
                    if g > 0:
                        qkv_v2(v, 2 * g - 2)
                        qkv_v2(v, 2 * g - 1)
                    qkv_v1(v, 2 * g)
                    qkv_v1(v, 2 * g + 1)
                ctx.section("vector", qkv_vv)

                def qkv_cc(c, t=t, g=g):
                    for fcg in (2 * g, 2 * g + 1):
                        c.wait(v_m, f"eluv1:{t}:{fcg}")
                        if fcg >= 2:
                            c.wait(v_m, f"eluv2:{t}:{fcg - 2}")
                        c.op("activation",
                             elu1[:, 512 * (fcg % 2):512 * (fcg % 2) + 512],
                             elu1[:, 512 * (fcg % 2):512 * (fcg % 2) + 512],
                             AF.Exp, sem=c_m)
                        c.mark(c_m, f"eluexp:{t}:{fcg}")
                ctx.section("scalar", qkv_cc)

            if go("qkv"):
                def qkv_fin(v, t=t):
                    qkv_v2(v, 14)
                    qkv_v2(v, 15)
                ctx.section("vector", qkv_fin)
                tile_done[0] = (v_m, f"eluv2:{t}:15")

            ring_flush()
            for tc in range(NTC) if go("qkv") else []:
                def vprod_mm(tn, t=t, tc=tc):
                    tn.wait(v_m, "vones")
                    for half in range(2):
                        key = f"wvA{t}" if half == 0 else f"wvB{t}"
                        rwait(tn, key)
                        wvv = rview(key, 8, 512)
                        bank = (16 + 2 * tc + half) % 4
                        ps_wait(tn, bank)
                        for kc in range(8):
                            tn.op("matmul", pf[bank],
                                  xn2_f[:, 512 * kc + 128 * tc:512 * kc + 128 * tc + 128],
                                  wvv[:, 512 * kc:512 * kc + 512],
                                  start=(kc == 0), stop=(kc == 7),
                                  sem=(t_m if kc == 7 else None))
                        tn.mark(t_m, f"vmm:{t}:{tc}:{half}")
                        ps_set_free(bank, v_m, f"vev:{t}:{tc}:{half}")
                        if tc == NTC - 1:
                            tn.mark(t_m, f"use:{key}")
                ctx.section("tensor", vprod_mm)

                def vprod_ev(v, t=t, tc=tc):
                    vb = v_sb[tc]
                    if t > 0:
                        v.wait(t_m, f"szmm:{t - 1}:{tc}")
                    for half in range(2):
                        v.wait(t_m, f"vmm:{t}:{tc}:{half}")
                        dv = vb[:, 65 * 8 * half:65 * 8 * half + 520] \
                            .rearrange("p (h c) -> p h c", c=65)[:, :, 0:64]
                        v.op("tensor_copy", dv,
                             pf[(16 + 2 * tc + half) % 4].rearrange("p (h c) -> p h c", c=64),
                             outs=[vb], ins=[], sem=v_m)
                        v.mark(v_m, f"vev:{t}:{tc}:{half}")
                ctx.section("vector", vprod_ev)

            ring_flush()
            if STOP_AFTER is not None and STOP_AFTER in _SUBATT:
                _attn = 1
            elif STOP_AFTER is not None and STOP_AFTER.startswith("att") and len(STOP_AFTER) == 5 and STOP_AFTER[3].isdigit():
                _attn = int(STOP_AFTER[3])
            elif go("att"):
                _attn = NTC
            else:
                _attn = 0

            def ktr_mms(tn, t, tc):
                ps_wait(tn, 6)
                for hc in range(8):
                    tn.op("matmul", pt[0][:, 128 * hc:128 * (hc + 1)],
                          k_f[:, 512 * hc + 128 * tc:512 * hc + 128 * tc + 128],
                          ident, start=True, stop=True, is_transpose=True,
                          sem=(t_m if hc == 7 else None))
                tn.mark(t_m, f"ktr:{t}:{tc}")
                ps_set_free(6, v_m, f"ktrev:{t}:{tc}")

            def ktrev(v, t, tc):
                v.wait(t_m, f"ktr:{t}:{tc}")
                v.op("tensor_copy", k_tm, pt[0],
                     outs=[k_tm], ins=[], sem=v_m)
                v.mark(v_m, f"ktrev:{t}:{tc}")

            def a_round(tn, rnd, t, tc):
                ps_wait(tn, 0)
                ps_wait(tn, 1)
                for i in range(8):
                    h = 8 * rnd + i
                    fc = h // 2
                    off = 64 * (h % 2)
                    bank = i % 2
                    blk = i // 2
                    tn.op("matmul", pf[bank][:, 128 * blk:128 * (blk + 1)],
                          k_f[off:off + 64,
                              512 * fc + 128 * tc:512 * fc + 128 * tc + 128],
                          q_f[off:off + 64,
                              512 * fc + 128 * tc:512 * fc + 128 * tc + 128],
                          start=True, stop=True,
                          sem=(t_m if i >= 6 else None))
                tn.mark(t_m, f"ammr:{t}:{tc}:{rnd}")
                ps_set_free(0, v_m, f"amevr:{t}:{tc}:{rnd}")
                ps_set_free(1, v_m, f"amevr:{t}:{tc}:{rnd}")

            def am_round(v, rnd, t, tc):
                v.wait(t_m, f"ammr:{t}:{tc}:{rnd}")
                for par in range(2):
                    v.op("tensor_tensor",
                         a_m[2 * rnd + par].rearrange("p (r m) -> p r m", r=4),
                         pf[par].rearrange("p (r m) -> p r m", r=4),
                         mask4.unsqueeze(1).broadcast_to([128, 4, 128]), ALU.mult,
                         outs=[a_m[2 * rnd + par]], ins=[],
                         sem=(v_m if par == 1 else None))
                v.mark(v_m, f"amevr:{t}:{tc}:{rnd}")

            def o_pair(tn, pair, t, tc, cq):
                vb = v_sb[tc]
                rnd = pair
                tn.wait(v_m, f"amevr:{t}:{tc}:{rnd}")
                tn.wait(v_m, f"vev:{t}:{tc}:0")
                tn.wait(v_m, f"vev:{t}:{tc}:1")
                if cq > 0:
                    pk = (t, tc - 1) if tc > 0 else (t - 1, NTC - 1)
                    tn.wait(v_m, f"szbf:{pk[0]}:{pk[1]}")
                for qq in range(2):
                    q4 = 2 * pair + qq
                    bO = 2 + 2 * pair + qq
                    ps_wait(tn, bO)
                    for j in range(4):
                        h = 4 * q4 + j
                        fc = h // 2
                        off = 64 * (h % 2)
                        am = a_m[2 * rnd + h % 2]
                        blk = (h - 8 * rnd) // 2
                        tn.op("matmul", pf[bO][:, 65 * j:65 * j + 65],
                              am[:, 128 * blk:128 * (blk + 1)],
                              vb[:, 65 * h:65 * h + 65],
                              start=True, stop=(cq == 0),
                              sem=(t_m if (cq == 0 and j == 3) else None))
                        if cq > 0:
                            tn.op("matmul", pf[bO][:, 65 * j:65 * j + 65],
                                  q_f[off:off + 64,
                                      512 * fc + 128 * tc:512 * fc + 128 * tc + 128],
                                  sz_bf[off:off + 64,
                                        65 * (h // 2):65 * (h // 2) + 65],
                                  start=False, stop=True,
                                  sem=(t_m if j == 3 else None))
                    tn.mark(t_m, f"omm:{t}:{tc}:{q4}")
                    ps_set_free(bO, c_m, f"yscale:{t}:{tc}:{q4}")

            def gath(v, q4, t, tc):
                v.wait(t_m, f"omm:{t}:{tc}:{q4}")
                if q4 >= 2:
                    v.wait(v_m, f"rde:{t}:{tc}:{q4 - 2}")
                elif t > 0 or tc > 0:
                    pk = (t, tc - 1) if tc > 0 else (t - 1, NTC - 1)
                    v.wait(v_m, f"rde:{pk[0]}:{pk[1]}:{q4 + 2}")
                v.op("tensor_copy", d4[:, 4 * (q4 % 2):4 * (q4 % 2) + 4],
                     pf[2 + q4][:, 64:64 + 4 * 65:65],
                     outs=[d4[:, 4 * (q4 % 2):4 * (q4 % 2) + 4]], ins=[],
                     sem=v_m)
                v.mark(v_m, f"gath:{t}:{tc}:{q4}")

            def recip(v, q4, t, tc):
                if q4 >= 2:
                    v.wait(c_m, f"yscale:{t}:{tc}:{q4 - 2}")
                elif t > 0 or tc > 0:
                    pk = (t, tc - 1) if tc > 0 else (t - 1, NTC - 1)
                    v.wait(c_m, f"yscale:{pk[0]}:{pk[1]}:{q4 + 2}")
                v.wait(v_m, f"gath:{t}:{tc}:{q4}")
                v.op("reciprocal", r4[:, 4 * (q4 % 2):4 * (q4 % 2) + 4],
                     d4[:, 4 * (q4 % 2):4 * (q4 % 2) + 4],
                     outs=[r4[:, 4 * (q4 % 2):4 * (q4 % 2) + 4]],
                     ins=[d4[:, 4 * (q4 % 2):4 * (q4 % 2) + 4]],
                     sem=v_m, allow_hazard=True)
                v.mark(v_m, f"rde:{t}:{tc}:{q4}")

            def yscale_pair(c, pair, t, tc):
                for qq in range(2):
                    q4 = 2 * pair + qq
                    c.wait(v_m, f"rde:{t}:{tc}:{q4}")
                    for j in range(4):
                        h = 4 * q4 + j
                        c.op("activation", y_tm[:, 64 * h:64 * h + 64],
                             pf[2 + q4][:, 65 * j:65 * j + 64],
                             AF.Copy,
                             scale=r4[:, 4 * (q4 % 2) + j:4 * (q4 % 2) + j + 1],
                             sem=(c_m if j == 3 else None))
                    c.mark(c_m, f"yscale:{t}:{tc}:{q4}")

            if _attn > 0:
                def att_pre(tn, t=t):
                    tn.wait(v_m, f"eluv2:{t}:15")
                    tn.wait(v_m, f"vev:{t}:0:1")
                    ktr_mms(tn, t, 0)
                    a_round(tn, 0, t, 0)
                ctx.section("tensor", att_pre)

                def att_prev(v, t=t):
                    ktrev(v, t, 0)
                    am_round(v, 0, t, 0)
                ctx.section("vector", att_prev)

            for tc in range(_attn):
                cq = NTC * tis + tc

                def att_t1(tn, t=t, tc=tc, cq=cq):
                    a_round(tn, 1, t, tc)
                ctx.section("tensor", att_t1)

                def att_v1(v, t=t, tc=tc):
                    am_round(v, 1, t, tc)
                ctx.section("vector", att_v1)

                def att_t2(tn, t=t, tc=tc, cq=cq):
                    o_pair(tn, 0, t, tc, cq)
                ctx.section("tensor", att_t2)

                def att_v2(v, t=t, tc=tc):
                    gath(v, 0, t, tc)
                    gath(v, 1, t, tc)
                    recip(v, 0, t, tc)
                    recip(v, 1, t, tc)
                ctx.section("vector", att_v2)

                def att_s1(c, t=t, tc=tc):
                    yscale_pair(c, 0, t, tc)
                ctx.section("scalar", att_s1)

                def att_t3(tn, t=t, tc=tc, cq=cq):
                    o_pair(tn, 1, t, tc, cq)
                ctx.section("tensor", att_t3)

                def att_v3(v, t=t, tc=tc):
                    gath(v, 2, t, tc)
                    gath(v, 3, t, tc)
                    recip(v, 2, t, tc)
                    recip(v, 3, t, tc)
                ctx.section("vector", att_v3)

                def att_s2(c, t=t, tc=tc):
                    yscale_pair(c, 1, t, tc)
                ctx.section("scalar", att_s2)

                def att_t4(tn, t=t, tc=tc, cq=cq):
                    vb = v_sb[tc]
                    tn.wait(v_m, f"ktrev:{t}:{tc}")
                    if cq > 0:
                        pk = (t, tc - 1) if tc > 0 else (t - 1, NTC - 1)
                        tn.wait(v_m, f"szbf:{pk[0]}:{pk[1]}")
                    for half in range(2):
                        bank = 2 + half
                        ps_wait(tn, bank)
                        for pp in range(4):
                            p = 4 * half + pp
                            for par in range(2):
                                h = 2 * p + par
                                tn.op("matmul",
                                      pf[bank][64 * par:64 * par + 64,
                                               65 * pp:65 * pp + 65],
                                      k_tm[:, 128 * p + 64 * par:128 * p + 64 * par + 64],
                                      vb[:, 65 * h:65 * h + 65],
                                      start=True, stop=True,
                                      tile_position=(0, 64 * par),
                                      sem=(t_m if (pp == 3 and par == 1) else None))
                        tn.mark(t_m, f"szd:{t}:{tc}:{half}")
                        ps_set_free(bank, v_m, f"szup:{t}:{tc}:{half}")
                    tn.mark(t_m, f"szmm:{t}:{tc}")
                    if tc + 1 < NTC:
                        ktr_mms(tn, t, tc + 1)
                        a_round(tn, 0, t, tc + 1)
                    tn.wait(c_m, f"yscale:{t}:{tc}:3")
                    ps_wait(tn, 7)
                    for yc in range(8):
                        tn.op("matmul", pt[1][:, 128 * yc:128 * (yc + 1)],
                              y_tm[:, 128 * yc:128 * (yc + 1)], ident,
                              start=True, stop=True, is_transpose=True,
                              sem=(t_m if yc == 7 else None))
                    tn.mark(t_m, f"ytr:{t}:{tc}")
                    ps_set_free(7, v_m, f"ytrev:{t}:{tc}")
                ctx.section("tensor", att_t4)

                def att_v4(v, t=t, tc=tc, cq=cq):
                    if tc + 1 < NTC:
                        ktrev(v, t, tc + 1)
                        am_round(v, 0, t, tc + 1)
                    v.wait(t_m, f"ytr:{t}:{tc}")
                    dv = y_fT.rearrange("p (kc m) -> p kc m", kc=8) \
                        [:, 0:8, 128 * tc:128 * tc + 128]
                    v.op("tensor_copy", dv,
                         pt[1].rearrange("p (j m) -> p j m", j=8),
                         outs=[y_fT], ins=[], sem=v_m)
                    v.mark(v_m, f"ytrev:{t}:{tc}")
                    for half in range(2):
                        v.wait(t_m, f"szd:{t}:{tc}:{half}")
                        dst = sz_m[:, 260 * half:260 * half + 260]
                        if cq == 0:
                            v.op("tensor_copy", dst, pf[2 + half][:, 0:260],
                                 outs=[dst], ins=[], sem=v_m)
                        else:
                            v.op("tensor_tensor", dst, dst, pf[2 + half][:, 0:260],
                                 ALU.add, outs=[dst], ins=[], sem=v_m)
                        v.mark(v_m, f"szup:{t}:{tc}:{half}")
                    v.wait(v_m, f"szup:{t}:{tc}:0")
                    v.wait(v_m, f"szup:{t}:{tc}:1")
                    for half in range(2):
                        v.op("tensor_copy", sz_bf[:, 260 * half:260 * half + 260],
                             sz_m[:, 260 * half:260 * half + 260],
                             outs=[sz_bf[:, 260 * half:260 * half + 260]],
                             ins=[sz_m[:, 260 * half:260 * half + 260]],
                             sem=(v_m if half == 1 else None),
                             allow_hazard=True)
                    v.mark(v_m, f"szbf:{t}:{tc}")
                ctx.section("vector", att_v4)

            ring_flush()
            if _attn > 0:
                if STOP_AFTER in _SUBATT and STOP_AFTER != "attt4":
                    tile_done[0] = {"attt1": (v_m, f"amevr:{t}:0:0"),
                                    "attt2": (v_m, f"rde:{t}:0:1"),
                                    "atts1": (c_m, f"yscale:{t}:0:1"),
                                    "attt3": (v_m, f"rde:{t}:0:3"),
                                    "atts2": (c_m, f"yscale:{t}:0:3")}[STOP_AFTER]
                else:
                    tile_done[0] = (v_m, f"szbf:{t}:{_attn - 1}")

            def gate_tr(tn, t=t):
                ps_wait(tn, 6)
                for tc in range(NTC):
                    tn.wait(v_m, f"gw8b:{t}:{tc}")
                    tn.op("matmul", pt[0][0:8, 128 * tc:128 * (tc + 1)],
                          w8b[:, 8 * tc:8 * tc + 8], ident,
                          start=True, stop=True, is_transpose=True,
                          sem=(t_m if tc == NTC - 1 else None))
                tn.mark(t_m, f"gw8tr:{t}")
                ps_set_free(6, v_m, f"gw8tev:{t}")
            if go("gate"):
                ctx.section("tensor", gate_tr)

            def gate_trev(v, t=t):
                v.wait(t_m, f"gw8tr:{t}")
                v.op("tensor_copy", w8t, pt[0][0:8, 0:512],
                     outs=[w8t], ins=[], sem=v_m)
                v.mark(v_m, f"gw8tev:{t}")
            if go("gate"):
                ctx.section("vector", gate_trev)

            if go("gate"):
                tile_done[0] = (v_m, f"gw8tev:{t}")


            for pj in range(6) if go("ush") else []:
                def ush_mm(tn, t=t, pj=pj):
                    if pj == 0:
                        tn.wait(v_m, f"ytrev:{t}:{NTC - 1}")
                    for j in (2 * pj, 2 * pj + 1):
                        qy, qg = j // 6, 2 + j // 6
                        ky, kg = f"s1_{t}_{qy}", f"s1_{t}_{qg}"
                        rwait(tn, ky)
                        rwait(tn, kg)
                        wy = rview(ky, 8, 768)
                        wgr = rview(kg, 8, 768)
                        hy = j % 6
                        bY, bG = j % 3, 3 + j % 3
                        ps_wait(tn, bY)
                        for kc in range(8):
                            tn.op("matmul", pf[bY],
                                  wy[:, 768 * kc + 128 * hy:768 * kc + 128 * hy + 128],
                                  xn_f[:, 512 * kc:512 * (kc + 1)],
                                  start=(kc == 0), stop=(kc == 7),
                                  sem=(t_m if kc == 7 else None))
                        tn.mark(t_m, f"usy:{t}:{j}")
                        ps_set_free(bY, v_m, f"usu:{t}:{j}")
                        ps_wait(tn, bG)
                        for kc in range(8):
                            tn.op("matmul", pf[bG],
                                  wgr[:, 768 * kc + 128 * hy:768 * kc + 128 * hy + 128],
                                  xn_f[:, 512 * kc:512 * (kc + 1)],
                                  start=(kc == 0), stop=(kc == 7),
                                  sem=(t_m if kc == 7 else None))
                        tn.mark(t_m, f"usg:{t}:{j}")
                        ps_set_free(bG, v_m, f"ustg:{t}:{j // 2}:{j % 2}")
                        if j == 5:
                            tn.mark(t_m, f"use:s1_{t}_0")
                            tn.mark(t_m, f"use:s1_{t}_2")
                        if j == 11:
                            tn.mark(t_m, f"use:s1_{t}_1")
                            tn.mark(t_m, f"use:s1_{t}_3")
                ctx.section("tensor", ush_mm)

                def ush_c(c, t=t, pj=pj):
                    for j in (2 * pj, 2 * pj + 1):
                        c.wait(t_m, f"usg:{t}:{j}")
                        rd = sg_reader.get(j % 2)
                        if rd is not None:
                            c.wait(rd[0], rd[1])
                        c.op("activation", sg_sb[j % 2], pf[3 + j % 3],
                             AF.Silu, sem=c_m)
                        c.mark(c_m, f"ussg:{t}:{j}")
                ctx.section("scalar", ush_c)

                def ush_v(v, t=t, pj=pj):
                    for j in (2 * pj, 2 * pj + 1):
                        v.wait(c_m, f"ussg:{t}:{j}")
                        v.wait(t_m, f"usy:{t}:{j}")
                        v.op("tensor_tensor", us_sb[:, 512 * j:512 * (j + 1)],
                             pf[j % 3], sg_sb[j % 2], ALU.mult,
                             outs=[us_sb[:, 512 * j:512 * (j + 1)]],
                             ins=[sg_sb[j % 2]], sem=v_m)
                        v.mark(v_m, f"usu:{t}:{j}")
                        v.mark(v_m, f"ustg:{t}:{j // 2}:{j % 2}")
                        sg_reader[j % 2] = (v_m, f"usu:{t}:{j}")
                ctx.section("vector", ush_v)

            ring_flush()
            if go("ush"):
                tile_done[0] = (v_m, f"usu:{t}:11")

            for ph in range(2) if go("z0") else []:
                def z0_mm(tn, t=t, ph=ph):
                    if ph == 0:
                        tn.wait(s_res, "all")
                    for tc in (2 * ph, 2 * ph + 1):
                        banks = (2, 3) if tc % 2 == 0 else (4, 5)
                        if ph == 0 and tc == 0:
                            tn.wait(v_m, f"ytrev:{t}:{NTC - 1}")
                        for half in range(2):
                            b = banks[half]
                            ps_wait(tn, b)
                            for yc in range(8):
                                tn.op("matmul", pf[b],
                                      y_fT[:, 512 * yc + 128 * tc:512 * yc + 128 * tc + 128],
                                      wo_sb[:, 1024 * yc + 512 * half:1024 * yc + 512 * half + 512],
                                      start=(yc == 0), stop=False)
                            for hc in range(12):
                                if ph == 0 and tc == 0 and half == 0:
                                    tn.wait(v_m, f"usu:{t}:{hc}")
                                tn.op("matmul", pf[b],
                                      us_sb[:, 512 * hc + 128 * tc:512 * hc + 128 * tc + 128],
                                      ws2_sb[:, 1024 * hc + 512 * half:1024 * hc + 512 * half + 512],
                                      start=False, stop=(hc == 11),
                                      sem=(t_m if hc == 11 else None))
                            tn.mark(t_m, f"z:{t}:0:{tc}:{half}")
                            ps_set_free(b, v_m, f"zev:{t}:0:{tc}:{half}")
                ctx.section("tensor", z0_mm)

                def z0_ev(v, t=t, ph=ph):
                    for tc in (2 * ph, 2 * ph + 1):
                        v.wait(s_x, f"x{t}:{tc}")
                        banks = (2, 3) if tc % 2 == 0 else (4, 5)
                        for half in range(2):
                            v.wait(t_m, f"z:{t}:0:{tc}:{half}")
                            dst = out_sb[:, D * tc + 512 * half:D * tc + 512 * half + 512]
                            v.op("tensor_tensor", dst, dst, pf[banks[half]],
                                 ALU.add, outs=[dst], ins=[], sem=v_m,
                                 allow_hazard=True)
                            v.mark(v_m, f"zev:{t}:0:{tc}:{half}")
                ctx.section("vector", z0_ev)

            if go("z0"):
                tile_done[0] = (v_m, f"zev:{t}:0:{NTC - 1}:1")
            XMAP = {0: (0, 1), 1: (2,), 2: (3,), 3: ()}
            for w in [w_ for w_ in range(4) if go(f"uz{w_}")]:
                if STOP_AFTER is None and t + 1 < n_tiles:
                    for xw in XMAP[w]:
                        def xload(sy, t=t, xw=xw):
                            if t > 0:
                                sy.wait(s_out, f"o{t - 1}:{xw}")
                            pk = (t + 1, xw - 1) if xw > 0 else (t, NTC - 1)
                            sy.wait(s_x, f"x{pk[0]}:{pk[1]}")
                            sy.op("dma_start",
                                  out=outs2[(t + 1) % 2][:, D * xw:D * (xw + 1)],
                                  in_=x_d[TILE * (t + 1) + TC * xw:
                                          TILE * (t + 1) + TC * (xw + 1), :],
                                  sem=s_x, semk=16)
                            sy.mark(s_x, f"x{t + 1}:{xw}")
                        ctx.section("sync", xload)
                    early_ln(t + 1, w)
                for ee in range(2):
                    e = 2 * w + ee

                    def ue_pre(tn, t=t, e=e):
                        tn.wait(v_m, f"gw8tev:{t}")
                        ps_wait(tn, 4)
                        tn.op("matmul", pf[4], sel[:, 128 * e:128 * (e + 1)],
                              w8t, start=True, stop=True, sem=t_m)
                        tn.mark(t_m, f"wbmm:{t}:{e}")
                        ps_set_free(4, v_m, f"wbev:{t}:{e}")
                    ctx.section("tensor", ue_pre)

                    def ue_vwb(v, t=t, e=e):
                        v.wait(t_m, f"wbmm:{t}:{e}")
                        if e >= 2:
                            v.wait(v_m, f"ueu:{t}:{e - 2}:5")
                        elif t > 0:
                            v.wait(v_m, f"ueu:{t - 1}:{e + 6}:5")
                        v.op("tensor_copy", wb_sb2[e % 2], pf[4],
                             outs=[wb_sb2[e % 2]], ins=[], sem=v_m)
                        v.mark(v_m, f"wbev:{t}:{e}")
                    ctx.section("vector", ue_vwb)

                    for pj in range(3):
                        def ue_mm(tn, t=t, e=e, pj=pj):
                            ka, kb = f"e1a_{t}_{e}", f"e1b_{t}_{e}"
                            rwait(tn, ka)
                            rwait(tn, kb)
                            wa = rview(ka, 8, 768)
                            wbv = rview(kb, 8, 768)
                            for j in (2 * pj, 2 * pj + 1):
                                bY, bG = j % 3, 3 + j % 3
                                ps_wait(tn, bY)
                                for kc in range(8):
                                    tn.op("matmul", pf[bY],
                                          wa[:, 768 * kc + 128 * j:768 * kc + 128 * j + 128],
                                          xn_f[:, 512 * kc:512 * (kc + 1)],
                                          start=(kc == 0), stop=(kc == 7),
                                          sem=(t_m if kc == 7 else None))
                                tn.mark(t_m, f"uey:{t}:{e}:{j}")
                                ps_set_free(bY, v_m, f"ueu:{t}:{e}:{j}")
                                ps_wait(tn, bG)
                                for kc in range(8):
                                    tn.op("matmul", pf[bG],
                                          wbv[:, 768 * kc + 128 * j:768 * kc + 128 * j + 128],
                                          xn_f[:, 512 * kc:512 * (kc + 1)],
                                          start=(kc == 0), stop=(kc == 7),
                                          sem=(t_m if kc == 7 else None))
                                tn.mark(t_m, f"ueg:{t}:{e}:{j}")
                                ps_set_free(bG, c_m, f"uesg:{t}:{e}:{j}")
                            if pj == 2:
                                tn.mark(t_m, f"use:{ka}")
                                tn.mark(t_m, f"use:{kb}")
                        ctx.section("tensor", ue_mm)

                        def ue_c(c, t=t, e=e, pj=pj):
                            for j in (2 * pj, 2 * pj + 1):
                                c.wait(t_m, f"ueg:{t}:{e}:{j}")
                                rd = sg_reader.get(j % 2)
                                if rd is not None:
                                    c.wait(rd[0], rd[1])
                                c.op("activation", sg_sb[j % 2], pf[3 + j % 3],
                                     AF.Silu, sem=c_m)
                                c.mark(c_m, f"uesg:{t}:{e}:{j}")
                        ctx.section("scalar", ue_c)

                        def ue_v(v, t=t, e=e, pj=pj):
                            slot = u_slot[e % 3]
                            if pj == 0:
                                v.wait(v_m, f"wbev:{t}:{e}")
                                key = uslot_consumer.get(e % 3)
                                if key is not None:
                                    v.wait(t_m, key)
                            for j in (2 * pj, 2 * pj + 1):
                                v.wait(c_m, f"uesg:{t}:{e}:{j}")
                                v.op("tensor_tensor", sg_sb[j % 2], sg_sb[j % 2],
                                     wb_sb2[e % 2], ALU.mult,
                                     outs=[sg_sb[j % 2]], ins=[wb_sb2[e % 2]],
                                     sem=(v_m if j % 2 == 1 else None))
                            v.mark(v_m, f"uetsg:{t}:{e}:{pj}")
                            v.wait(v_m, f"uetsg:{t}:{e}:{pj}")
                            for j in (2 * pj, 2 * pj + 1):
                                v.wait(t_m, f"uey:{t}:{e}:{j}")
                                v.op("tensor_tensor", slot[:, 512 * j:512 * (j + 1)],
                                     pf[j % 3], sg_sb[j % 2], ALU.mult,
                                     outs=[slot[:, 512 * j:512 * (j + 1)]],
                                     ins=[sg_sb[j % 2]], sem=v_m,
                                     allow_hazard=True)
                                v.mark(v_m, f"ueu:{t}:{e}:{j}")
                                sg_reader[j % 2] = (v_m, f"ueu:{t}:{e}:{j}")
                        ctx.section("vector", ue_v)

                    ring_flush()

                for ph in range(2):
                    def zw_mm(tn, t=t, w=w, ph=ph):
                        e0, e1 = 2 * w, 2 * w + 1
                        k20, k21 = f"e2_{t}_{e0}", f"e2_{t}_{e1}"
                        rwait(tn, k20)
                        rwait(tn, k21)
                        w20 = rview(k20, 6, 1024)
                        w21 = rview(k21, 6, 1024)
                        for tc in (2 * ph, 2 * ph + 1):
                            banks = (2, 3) if tc % 2 == 0 else (4, 5)
                            for half in range(2):
                                b = banks[half]
                                ps_wait(tn, b)
                                for ei, (e, w2) in enumerate(((e0, w20), (e1, w21))):
                                    slot = u_slot[e % 3]
                                    for hc in range(6):
                                        if ph == 0 and tc == 2 * ph and half == 0:
                                            tn.wait(v_m, f"ueu:{t}:{e}:{hc}")
                                        tn.op("matmul", pf[b],
                                              slot[:, 512 * hc + 128 * tc:512 * hc + 128 * tc + 128],
                                              w2[:, 1024 * hc + 512 * half:1024 * hc + 512 * half + 512],
                                              start=(ei == 0 and hc == 0),
                                              stop=(ei == 1 and hc == 5),
                                              sem=(t_m if (ei == 1 and hc == 5) else None))
                                tn.mark(t_m, f"z:{t}:{w + 1}:{tc}:{half}")
                                ps_set_free(b, v_m, f"zev:{t}:{w + 1}:{tc}:{half}")
                        if ph == 1:
                            tn.mark(t_m, f"use:{k20}")
                            tn.mark(t_m, f"use:{k21}")
                            uslot_consumer[e0 % 3] = f"z:{t}:{w + 1}:{NTC - 1}:1"
                            uslot_consumer[e1 % 3] = f"z:{t}:{w + 1}:{NTC - 1}:1"
                    ctx.section("tensor", zw_mm)

                    def zw_ev(v, t=t, w=w, ph=ph):
                        for tc in (2 * ph, 2 * ph + 1):
                            banks = (2, 3) if tc % 2 == 0 else (4, 5)
                            for half in range(2):
                                v.wait(t_m, f"z:{t}:{w + 1}:{tc}:{half}")
                                v.wait(v_m, f"zev:{t}:{w}:{tc}:{half}")
                                dst = out_sb[:, D * tc + 512 * half:D * tc + 512 * half + 512]
                                v.op("tensor_tensor", dst, dst, pf[banks[half]],
                                     ALU.add, outs=[dst], ins=[], sem=v_m,
                                     allow_hazard=True)
                                v.mark(v_m, f"zev:{t}:{w + 1}:{tc}:{half}")
                        if ph == 1:
                            v.mark(v_m, f"zwin:{t}:{w + 1}")
                    ctx.section("vector", zw_ev)

                ring_flush()
                tile_done[0] = (v_m, f"zwin:{t}:{w + 1}")
                if STOP_AFTER is None and t + 1 < n_tiles:
                    if w == 3:
                        emit_ap_c1(t + 1, 0)
                        emit_ap_c1(t + 1, 1)

            if STOP_AFTER is not None and t + 1 < n_tiles:
                for tc0 in range(NTC):
                    early_ln(t + 1, tc0)
            def store_sync(s, t=t, tile_done=tile_done, out_sb=out_sb):
                full = STOP_AFTER is None
                for tc in range(NTC):
                    if full:
                        s.wait(v_m, f"zev:{t}:4:{tc}:1")
                    else:
                        s.wait(tile_done[0][0], tile_done[0][1])
                    if tc > 0 or t > 0:
                        pk = (t, tc - 1) if tc > 0 else (t - 1, NTC - 1)
                        s.wait(s_out, f"o{pk[0]}:{pk[1]}")
                    s.op("dma_start",
                         out=out_d[TILE * t + TC * tc:TILE * t + TC * (tc + 1), :],
                         in_=out_sb[:, D * tc:D * (tc + 1)],
                         sem=s_out, semk=16)
                    s.mark(s_out, f"o{t}:{tc}")
                if not full and t + 1 < n_tiles:
                    for tc in range(NTC):
                        s.wait(s_out, f"o{t}:{tc}")
                        if tc > 0:
                            s.wait(s_x, f"x{t + 1}:{tc - 1}")
                        s.op("dma_start",
                             out=outs2[(t + 1) % 2][:, D * tc:D * (tc + 1)],
                             in_=x_d[TILE * (t + 1) + TC * tc:TILE * (t + 1) + TC * (tc + 1), :],
                             sem=s_x, semk=16)
                        s.mark(s_x, f"x{t + 1}:{tc}")
            ctx.section("sync", store_sync)

            ring_flush()

        def fin(s):
            s.wait(s_out, f"o{n_tiles - 1}:{NTC - 1}")
        ctx.section("sync", fin)

        if STOP_AFTER is None:
            assert not ring_q, f"{len(ring_q)} ring items never loaded"

    for s in SEMS:
        cm = nc.semaphore(s.name)
        s.h = cm.__enter__()
        ctxs.append(cm)
    ctx1 = Ctx(nc, None, dry=True)
    emit(ctx1)
    blk = nc.Block()
    block = blk.__enter__()
    for engine, tape in ctx1.tape.items():
        if not tape:
            continue

        def replay(e, tape=tape):
            for ent in tape:
                if ent[0] == "op":
                    _, method, args, kw, sem, semk = ent
                    inst = getattr(e, method)(*args, **kw)
                    if sem is not None:
                        inst.then_inc(sem.h, semk)
                else:
                    _, sem, val = ent
                    e.wait_ge(sem.h, val)
        getattr(block, engine)(replay)
    blk.__exit__(None, None, None)
    nc._keepalive = ctxs
    return nc


_CACHE = {}


def _consts():
    ident = np.eye(128, dtype=ml_dtypes.bfloat16)
    mask4 = np.ascontiguousarray(np.triu(np.ones((128, 128), np.float32)).astype(ml_dtypes.bfloat16))
    selm = np.zeros((8, 1024), np.float32)
    for e in range(8):
        selm[e, 128 * e:128 * (e + 1)] = 1.0
    return ident, mask4, selm.astype(ml_dtypes.bfloat16)


def kernel(L0_attn_x, L0_attn_W1, L0_attn_W2, L0_ffn_gate_w, L0_ffn_W1,
           L0_ffn_W2, L0_ffn_Ws1, L0_ffn_Ws2):
    x = np.ascontiguousarray(np.asarray(L0_attn_x, dtype=np.float32))
    B, T, _ = x.shape
    n_seq = B // N_CORES
    key = (n_seq, T)
    if key not in _CACHE:
        _CACHE[key] = build_program(n_seq, T)
    nc = _CACHE[key]

    W1 = np.asarray(L0_attn_W1, np.float32)
    wqkT = np.ascontiguousarray(W1[0:2048, :].T)
    wvT = np.ascontiguousarray(W1[2048:3072, :].T)
    woT = np.ascontiguousarray(np.asarray(L0_attn_W2, np.float32).T)
    wgT = np.ascontiguousarray(np.asarray(L0_ffn_gate_w, np.float32).T)
    we1T = np.ascontiguousarray(np.asarray(L0_ffn_W1, np.float32).transpose(0, 2, 1))
    we2T = np.ascontiguousarray(np.asarray(L0_ffn_W2, np.float32).transpose(0, 2, 1))
    ws1T = np.ascontiguousarray(np.asarray(L0_ffn_Ws1, np.float32).T)
    ws2T = np.ascontiguousarray(np.asarray(L0_ffn_Ws2, np.float32).T)
    ident, mask4, selm = _consts()

    in_maps = []
    for c in range(N_CORES):
        xc = np.ascontiguousarray(x[c * n_seq:(c + 1) * n_seq].reshape(n_seq * T, D))
        in_maps.append(dict(
            x=xc, w_qkT=wqkT, w_vT=wvT, w_oT=woT, w_gT=wgT,
            w_e1T=we1T, w_e2T=we2T, w_s1T=ws1T, w_s2T=ws2T,
            c_ident=ident, c_mask4=mask4, c_sel=selm,
        ))
    res = run_bass_kernel_spmd(nc, in_maps, core_ids=list(range(N_CORES)))
    outs = [np.asarray(res.results[c]["out"]).reshape(n_seq, T, D)
            for c in range(N_CORES)]
    out = np.ascontiguousarray(np.concatenate(outs, axis=0))
    if out.dtype != np.float64:
        out = out.astype(np.float32, copy=False)
    return out
